# version 02
# baseline (speedup 1.0000x reference)
import contextlib
import numpy as np
import ml_dtypes
import concourse.bass as bass
import concourse.mybir as mybir
from concourse.bass_utils import run_bass_kernel_spmd

F32 = mybir.dt.float32
BF16 = mybir.dt.bfloat16
I32 = mybir.dt.int32
AF = mybir.ActivationFunctionType
ALU = mybir.AluOpType
AX = mybir.AxisListType
BFNP = ml_dtypes.bfloat16

B, S, D = 2, 8192, 1024
T = B * S
NCORES = 8
TOKC = T // NCORES
LN_EPS = 1e-5

_launch_count = [0]


def _run(nc, in_maps):
    import time
    _launch_count[0] += 1
    try:
        return run_bass_kernel_spmd(nc, in_maps, core_ids=list(range(NCORES)))
    except Exception as exc:
        if "UNRECOVERABLE" not in str(exc) and "PassThrough" not in str(exc):
            raise
        time.sleep(5)
        _launch_count[0] += 1
        return run_bass_kernel_spmd(nc, in_maps, core_ids=list(range(NCORES)))


class Emit:

    def __init__(self, name):
        self.nc = bass.Bass("TRN2", target_bir_lowering=False)
        self.name = name
        self.stack = contextlib.ExitStack()
        self.scopes = [self.stack]
        self.n = 0
        self.dsem = None
        self.dcount = 0
        self.sems = {}
        self.ccsem = None
        self.cccount = 0
        self.counters = {}

    def _nm(self, pfx):
        self.n += 1
        return f"{pfx}_{self.n}"

    def start(self):
        self.dsem = self.stack.enter_context(self.nc.semaphore("dsem"))
        return self

    def rotate_dsem(self):
        if not hasattr(self, "_dsem_pool"):
            self._dsem_pool = [self.dsem] + [
                self.stack.enter_context(self.nc.semaphore(f"dsemr{i}"))
                for i in range(1, 8)]
            self._dsem_vals = [0] * 8
            self._dsem_i = 0
        self._dsem_vals[self._dsem_i] = self.dcount
        self._dsem_i = (self._dsem_i + 1) % 8
        self.dsem = self._dsem_pool[self._dsem_i]
        self.dcount = self._dsem_vals[self._dsem_i]

    def push_scope(self):
        st = contextlib.ExitStack()
        self.scopes.append(st)

    def pop_scope(self):
        st = self.scopes.pop()
        st.close()

    @contextlib.contextmanager
    def scope(self):
        st = contextlib.ExitStack()
        self.scopes.append(st)
        try:
            yield
        finally:
            self.scopes.pop()
            st.close()

    def allgather(self, src_ap, dst_ap):
        if self.ccsem is None:
            self.ccsem = self.stack.enter_context(self.nc.semaphore("ccsem"))
        self.bar()
        self.nc.gpsimd.collective_compute(
            "AllGather", mybir.AluOpType.bypass,
            replica_groups=[list(range(NCORES))],
            ins=[src_ap], outs=[dst_ap],
        ).then_inc(self.ccsem, 1)
        self.cccount += 1
        self.nc.gpsimd.wait_ge(self.ccsem, self.cccount)
        self.bar()

    def sem(self, key):
        if key not in self.sems:
            self.sems[key] = self.stack.enter_context(self.nc.semaphore(key))
        return self.sems[key]

    def close(self):
        self.stack.close()

    def sb(self, shape, dtype=F32, pfx="t"):
        return self.scopes[-1].enter_context(
            self.nc.sbuf_tensor(self._nm(pfx), shape, dtype))

    def ps(self, shape, dtype=F32, pfx="p"):
        import math
        esz = {F32: 4, BF16: 2, I32: 4}[dtype]
        per_part = 1
        for d in shape[1:]:
            per_part *= d
        per_part *= esz
        pad_bytes = math.ceil(per_part / 2048) * 2048
        if pad_bytes != per_part:
            assert len(shape) == 2, "pad only for 2-d psum shapes"
            shape = [shape[0], pad_bytes // esz]
        return self.scopes[-1].enter_context(
            self.nc.psum_tensor(self._nm(pfx), shape, dtype))

    def dram(self, name, shape, dtype=F32, shared=False):
        kw = {"addr_space": "Shared"} if shared else {}
        return self.nc.dram_tensor(name, shape, dtype, kind="Internal", **kw).ap()

    def din(self, name, shape, dtype=F32):
        return self.nc.dram_tensor(name, shape, dtype, kind="ExternalInput").ap()

    def dout(self, name, shape, dtype=F32):
        return self.nc.dram_tensor(name, shape, dtype, kind="ExternalOutput").ap()

    def dma(self, out, in_, eng=None):
        eng = eng or self.nc.sync
        eng.dma_start(out=out, in_=in_).then_inc(self.dsem, 16)
        self.dcount += 16

    def dma_wait(self, eng=None):
        (eng or self.nc.sync).wait_ge(self.dsem, self.dcount)

    def flush(self, eng=None):
        self.dma_wait(eng)
        self.nc.all_engine_barrier()

    def bar(self):
        self.nc.all_engine_barrier()

    def const_col(self, value):
        c = self.sb([128, 1], F32, "c")
        self.nc.vector.memset(c[:], value)
        return c

    def ones_row(self, n=128, dtype=F32):
        r = self.sb([1, n], dtype, "ones")
        self.nc.vector.memset(r[:], 1.0)
        return r

    def ln_cols(self, x_ap, mu_col, rstd_col, stats, eps_col, width=D):
        nc = self.nc
        nchunk = (width + 511) // 512
        for i in range(nchunk):
            lo, hi = i * 512, min((i + 1) * 512, width)
            nc.vector.bn_stats(stats[:, 6 * i:6 * i + 6], x_ap[:, lo:hi])
        self.bar()
        nc.vector.bn_aggr(stats[:, 6 * nchunk:6 * nchunk + 2], stats[:, 0:6 * nchunk])
        self.bar()
        nc.vector.tensor_copy(mu_col[:], stats[:, 6 * nchunk:6 * nchunk + 1])
        nc.scalar.activation(rstd_col[:], stats[:, 6 * nchunk + 1:6 * nchunk + 2],
                            AF.Sqrt, bias=eps_col[:, 0:1])
        self.bar()
        nc.vector.reciprocal(rstd_col[:], rstd_col[:])



def build_L3F(em=None, io=None):
    standalone = em is None
    if standalone:
        em = Emit("l3f").start()
        x_in = em.din("x", [TOKC, D], F32)
        x_out = em.dout("xo", [TOKC, D], F32)
        io = {"x_tile": lambda r: x_in[r:r + 128, :],
              "xo_tile": lambda r: x_out[r:r + 128, :],
              "wg": em.din("wg", [D, 3072], I32),
              "wu": em.din("wu", [D, 3072], I32),
              "wd": em.din("wd", [3072, D], I32),
              "ident": em.din("ident", [128, 128], BF16)}
    nc = em.nc
    HID = 3072
    NH = HID // 128
    wg_in, wu_in, wd_in, ident_in = io["wg"], io["wu"], io["wd"], io["ident"]
    em.push_scope()

    GRP = 512
    NGRP = TOKC // GRP

    wg = em.sb([128, 8 * HID], BF16, "wg")
    wu = em.sb([128, 8 * HID], BF16, "wu")
    wd = em.sb([128, NH * D], BF16, "wd")
    ident = em.sb([128, 128], BF16, "id")
    xg = em.sb([128, D], F32, "xg")
    xn = em.sb([128, D], F32, "xn")
    xnb = em.sb([128, D], BF16, "xnb")
    xnT = em.sb([128, 8 * GRP], BF16, "xnT")
    hbuf = em.sb([128, NH * GRP], BF16, "h")
    tmp = em.sb([128, GRP], F32, "tmp")
    stats = em.sb([128, 16], F32, "st")
    mu = em.sb([128, 1], F32, "mu")
    rstd = em.sb([128, 1], F32, "rs")
    eps = em.const_col(LN_EPS)
    xo = em.sb([128, D], F32, "xo")

    pst = em.ps([128, 256], BF16, "pt")
    psA = em.ps([128, 512], F32, "pa")
    psB = em.ps([128, 512], F32, "pb")
    psO = em.ps([128, D], F32, "po")

    for k in range(8):
        em.dma(wg[:, k * HID:(k + 1) * HID], wg_in[k * 128:(k + 1) * 128, :], nc.gpsimd)
        em.dma(wu[:, k * HID:(k + 1) * HID], wu_in[k * 128:(k + 1) * 128, :], nc.gpsimd)
    for k in range(NH):
        em.dma(wd[:, k * D:(k + 1) * D], wd_in[k * 128:(k + 1) * 128, :], nc.gpsimd)
    em.dma(ident[:], ident_in)
    em.flush(nc.gpsimd)
    em.flush()

    s = 1.0 / 127.0
    for g in range(NGRP):
        for t in range(4):
            em.dma(xg[:], io["x_tile"]((g * 4 + t) * 128))
            em.flush()
            xt = xg[:]
            em.ln_cols(xt, mu, rstd, stats, eps)
            em.bar()
            nc.vector.tensor_scalar(xn[:], xt, mu[:, 0:1], rstd[:, 0:1],
                                    ALU.subtract, ALU.mult)
            em.bar()
            _transpose_to(em, xnT, xn[:], xnb, ident, pst, t * 128, GRP)
        for hcH in range(NH):
            for kd in range(8):
                nc.tensor.matmul(psA[:, 0:GRP], wg[:, kd * HID + hcH * 128:
                                                    kd * HID + (hcH + 1) * 128],
                                 xnT[:, kd * GRP:(kd + 1) * GRP],
                                 start=(kd == 0), stop=(kd == 7))
            for kd in range(8):
                nc.tensor.matmul(psB[:, 0:GRP], wu[:, kd * HID + hcH * 128:
                                                    kd * HID + (hcH + 1) * 128],
                                 xnT[:, kd * GRP:(kd + 1) * GRP],
                                 start=(kd == 0), stop=(kd == 7))
            em.bar()
            nc.scalar.activation(tmp[:], psA[:, 0:GRP], AF.Silu, scale=s)
            em.bar()
            nc.vector.scalar_tensor_tensor(hbuf[:, hcH * GRP:(hcH + 1) * GRP],
                                           psB[:, 0:GRP], s, tmp[:],
                                           ALU.mult, ALU.mult)
            em.bar()
        for t in range(4):
            em.dma(xg[:], io["x_tile"]((g * 4 + t) * 128))
            for kf in range(NH):
                for nn in range(2):
                    nc.tensor.matmul(
                        psO[:, nn * 512:(nn + 1) * 512],
                        hbuf[:, kf * GRP + t * 128:kf * GRP + (t + 1) * 128],
                        wd[:, kf * D + nn * 512:kf * D + (nn + 1) * 512],
                        start=(kf == 0), stop=(kf == NH - 1))
            em.flush()
            nc.vector.scalar_tensor_tensor(xo[:], psO[:, 0:D],
                                           0.5 * s, xg[:],
                                           ALU.mult, ALU.add)
            em.bar()
            em.dma(io["xo_tile"]((g * 4 + t) * 128), xo[:])
            em.flush()
    em.pop_scope()
    if standalone:
        em.close()
    return em


def run_L3F(x, wg, wu, wd):
    em = _cached("build_L3F", build_L3F)
    xf = x.reshape(T, D)
    wgT = np.ascontiguousarray(wg.T)
    wuT = np.ascontiguousarray(wu.T)
    wdT = np.ascontiguousarray(wd.T)
    ident = np.eye(128, dtype=np.float32).astype(BFNP)
    in_maps = [{"x": xf[c * TOKC:(c + 1) * TOKC], "wg": wgT, "wu": wuT,
                "wd": wdT, "ident": ident} for c in range(NCORES)]
    res = _run(em.nc, in_maps)
    out = np.concatenate([res.results[c]["xo"] for c in range(NCORES)], 0)
    return out.reshape(B, S, D)





def _ln_affine(em, out_ap, x_ap, gb, bb, mu, rstd, stats, eps, tmpf):
    nc = em.nc
    em.ln_cols(x_ap, mu, rstd, stats, eps)
    em.bar()
    nc.vector.tensor_scalar(tmpf[:], x_ap, mu[:, 0:1], rstd[:, 0:1],
                            ALU.subtract, ALU.mult)
    em.bar()
    nc.vector.tensor_tensor(tmpf[:], tmpf[:], gb[:], ALU.mult)
    em.bar()
    nc.vector.tensor_tensor(out_ap, tmpf[:], bb[:], ALU.add)
    em.bar()


def _bcast_row(em, dst_sb, row_sb, ones_row_f32, psum, width=D):
    nc = em.nc
    for nn in range(0, width, 512):
        w = min(512, width - nn)
        nc.tensor.matmul(psum[:, 0:w], ones_row_f32[:, 0:128], row_sb[0:1, nn:nn + w],
                         start=True, stop=True)
        em.bar()
        nc.vector.tensor_copy(dst_sb[:, nn:nn + w], psum[:, 0:w])
        em.bar()


def _transpose_to(em, dstT, src_f32, tmp_bf, ident, pst, tok_off, grp_w):
    nc = em.nc
    tr = em.sem("ttr")
    cp = em.sem("tcp")
    em.counters.setdefault("ttr", 0)
    em.counters.setdefault("tcp", 0)
    nc.vector.tensor_copy(tmp_bf[:], src_f32)
    em.bar()
    t0, c0 = em.counters["ttr"], em.counters["tcp"]
    for k in range(8):
        if k >= 1:
            nc.tensor.wait_ge(cp, c0 + k)
        nc.tensor.transpose(pst[:, (k % 2) * 128:(k % 2 + 1) * 128],
                            tmp_bf[:, k * 128:(k + 1) * 128],
                            ident[:]).then_inc(tr, 1)
        em.counters["ttr"] += 1
        nc.vector.wait_ge(tr, t0 + k + 1)
        nc.vector.tensor_copy(dstT[:, k * grp_w + tok_off:k * grp_w + tok_off + 128],
                              pst[:, (k % 2) * 128:(k % 2 + 1) * 128]).then_inc(cp, 1)
        em.counters["tcp"] += 1
    em.bar()


def build_L0F(em=None, io=None):
    standalone = em is None
    if standalone:
        em = Emit("l0f").start()
        x_in = em.din("x", [TOKC, D], F32)
        x_out = em.dout("xo", [TOKC, D], F32)
        io = {"x_tile": lambda r: x_in[r:r + 128, :],
              "xo_tile": lambda r: x_out[r:r + 128, :],
              "mixw": em.din("mixw", [D, D], F32),
              "router": em.din("router", [D, 8], F32),
              "w1": em.din("w1", [8, D, 4096], F32),
              "w2": em.din("w2", [8, 4096, D], F32),
              "rows": em.din("rows", [1, 5 * D], F32),
              "b1c": em.din("b1c", [128, 8 * 32], F32),
              "b2r": em.din("b2r", [1, 8 * D], F32),
              "ident": em.din("ident", [128, 128], BF16)}
    nc = em.nc
    NE, HID = 8, 4096
    mixw_in, router_in = io["mixw"], io["router"]
    w1_in, w2_in = io["w1"], io["w2"]
    rows_in, b1_in, b2_in, ident_in = io["rows"], io["b1c"], io["b2r"], io["ident"]
    em.push_scope()

    GRP = 512
    NGRP = TOKC // GRP
    NT = GRP // 128

    mixw = em.sb([128, 8 * D], BF16, "mixw")
    router = em.sb([128, 8 * NE], BF16, "rt")
    stage = em.sb([1, D], F32, "stage")
    mixb_row = em.sb([1, D], BF16, "mixbr")
    combT = em.sb([8, 4 * 128], BF16, "combT")
    b1c = em.sb([128, NE * 32], F32, "b1c")
    b2sb = em.sb([8, D], BF16, "b2sb")
    ident = em.sb([128, 128], BF16, "id")
    onesf = em.ones_row(128, F32)
    onesb = em.sb([1, 128], BF16, "onesb")
    g1b = em.sb([128, D], F32, "g1b")
    be1b = em.sb([128, D], F32, "be1b")
    g2b = em.sb([128, D], F32, "g2b")
    be2b = em.sb([128, D], F32, "be2b")
    xg = em.sb([128, D], F32, "xg")
    n0 = em.sb([128, D], F32, "n0")
    ln1 = em.sb([128, D], F32, "ln1")
    ln2 = em.sb([128, D], F32, "ln2")
    tmpf = em.sb([128, D], F32, "tmpf")
    tmpb = em.sb([128, D], BF16, "tmpb")
    lnT = em.sb([128, 8 * 128], BF16, "lnT")
    ln2T = em.sb([128, 8 * GRP], BF16, "ln2T")
    hbuf = em.sb([128, 32 * GRP], BF16, "h")
    outacc = em.sb([128, NT * D], F32, "oacc")
    comb = em.sb([128, NT * NE], F32, "comb")
    w1b = [em.sb([128, 8 * 512], BF16, "w1b") for _ in range(2)]
    w2full = em.sb([128, 32 * D], BF16, "w2f")
    stats = em.sb([128, 16], F32, "st")
    mu = em.sb([128, 1], F32, "mu")
    rstd = em.sb([128, 1], F32, "rs")
    eps = em.const_col(LN_EPS)
    colA = em.sb([128, 8], F32, "cA")
    colB = em.sb([128, 8], F32, "cB")
    colC = em.sb([128, 8], F32, "cC")
    ones8 = em.sb([128, 8], F32, "o8")
    ncol = em.sb([128, 1], F32, "ncol")
    scol = em.sb([128, 1], F32, "scol")

    pst = em.ps([128, 256], BF16, "pt")
    psR = em.ps([128, 512], F32, "pr")
    psA = [em.ps([128, 512], F32, "pa") for _ in range(2)]
    psO = [em.ps([128, D], F32, "po") for _ in range(2)]
    hmm_s = em.sem("moe_hmm")
    hev_s = em.sem("moe_hev")
    ymm_s = em.sem("moe_ymm")
    yev_s = em.sem("moe_yev")
    wld_s = em.sem("moe_wld")
    for key in ["moe_hmm", "moe_hev", "moe_ymm", "moe_yev", "moe_wld"]:
        em.counters.setdefault(key, 0)
    C = em.counters

    nc.vector.memset(ones8[:], 1.0)
    nc.vector.memset(onesb[:], 1.0)
    for k in range(8):
        em.dma(mixw[:, k * D:(k + 1) * D], mixw_in[k * 128:(k + 1) * 128, :], nc.gpsimd)
        em.dma(router[:, k * NE:(k + 1) * NE],
               router_in[k * 128:(k + 1) * 128, :], nc.gpsimd)
    em.dma(b1c[:], b1_in)
    em.dma(b2sb[:], b2_in.rearrange("o (e d) -> (o e) d", e=NE), nc.gpsimd)
    em.dma(mixb_row[:], rows_in[0:1, 4 * D:5 * D], nc.gpsimd)
    em.dma(ident[:], ident_in)
    em.flush(nc.gpsimd)
    em.flush()
    for i, dst in enumerate([g1b, be1b, g2b, be2b]):
        em.dma(stage[:], rows_in[0:1, i * D:(i + 1) * D])
        em.flush()
        _bcast_row(em, dst, stage[0:1, :], onesf, psR)

    def w1_chunk_ap(e, fc):
        return (w1_in[e].rearrange("(dc p) f -> p dc f", p=128)
                [:, :, fc * 512:(fc + 1) * 512])

    def w1load(buf, e, fc):
        em.dma(buf[:].rearrange("p (dc f) -> p dc f", dc=8), w1_chunk_ap(e, fc),
               nc.gpsimd)

    def w2load(e):
        for kc in range(8):
            em.dma(w2full[:, kc * 4 * D:(kc + 1) * 4 * D]
                   .rearrange("p (fc d) -> p fc d", fc=4),
                   w2_in[e][kc * 512:(kc + 1) * 512, :]
                   .rearrange("(fc p) d -> p fc d", p=128),
                   nc.gpsimd)

    for g in range(NGRP):
        for t in range(NT):
            row = (g * NT + t) * 128
            em.dma(xg[:], io["x_tile"](row))
            em.flush()
            em.ln_cols(xg[:], mu, rstd, stats, eps)
            em.bar()
            nc.vector.tensor_scalar(n0[:], xg[:], mu[:, 0:1], rstd[:, 0:1],
                                    ALU.subtract, ALU.mult)
            em.bar()
            _ln_affine(em, ln1[:], n0[:], g1b, be1b, mu, rstd, stats, eps, tmpf)
            _transpose_to(em, lnT, ln1[:], tmpb, ident, pst, 0, 128)
            for kd in range(8):
                for nn in range(2):
                    nc.tensor.matmul(psO[0][:, nn * 512:(nn + 1) * 512],
                                     lnT[:, kd * 128:(kd + 1) * 128],
                                     mixw[:, kd * D + nn * 512:kd * D + (nn + 1) * 512],
                                     start=(kd == 0), stop=False)
            for nn in range(2):
                nc.tensor.matmul(psO[0][:, nn * 512:(nn + 1) * 512], onesb[:, 0:128],
                                 mixb_row[0:1, nn * 512:(nn + 1) * 512],
                                 start=False, stop=True)
            em.bar()
            nc.vector.tensor_tensor(tmpf[:], n0[:], ln1[:], ALU.add)
            em.bar()
            nc.vector.tensor_tensor(outacc[:, t * D:(t + 1) * D], tmpf[:],
                                    psO[0][:, 0:D], ALU.add)
            em.bar()
            _ln_affine(em, ln2[:], outacc[:, t * D:(t + 1) * D], g2b, be2b,
                       mu, rstd, stats, eps, tmpf)
            _transpose_to(em, ln2T, ln2[:], tmpb, ident, pst, t * 128, GRP)
            for kd in range(8):
                nc.tensor.matmul(psR[:, 0:NE],
                                 ln2T[:, kd * GRP + t * 128:kd * GRP + (t + 1) * 128],
                                 router[:, kd * NE:(kd + 1) * NE],
                                 start=(kd == 0), stop=(kd == 7))
            em.bar()
            nc.vector.reduce_max(ncol[:], psR[:, 0:NE], axis=AX.X, negate=True)
            em.bar()
            nc.scalar.activation(colA[:], psR[:, 0:NE], AF.Exp, bias=ncol[:, 0:1],
                                 accum_out=scol[:, 0:1])
            em.bar()
            nc.vector.reciprocal(ncol[:], scol[:])
            em.bar()
            nc.vector.tensor_scalar(colA[:], colA[:], ncol[:, 0:1], None, ALU.mult)
            em.bar()
            nc.vector.reduce_max(ncol[:], colA[:], axis=AX.X)
            em.bar()
            nc.vector.tensor_scalar(colB[:], colA[:], ncol[:, 0:1], None, ALU.is_ge)
            em.bar()
            nc.vector.tensor_tensor_scan(colC[:], ones8[:], colB[:], 0.0,
                                         ALU.mult, ALU.add)
            em.bar()
            nc.vector.tensor_scalar(colC[:], colC[:], 1.0, None, ALU.is_le)
            em.bar()
            nc.vector.tensor_tensor(colB[:], colB[:], colC[:], ALU.mult)
            em.bar()
            cslot = comb[:, t * NE:(t + 1) * NE]
            nc.vector.tensor_scalar(cslot, colB[:], ncol[:, 0:1], None, ALU.mult)
            em.bar()
            nc.vector.tensor_tensor(colC[:], colA[:], colB[:], ALU.mult)
            em.bar()
            nc.vector.tensor_tensor(colA[:], colA[:], colC[:], ALU.subtract)
            em.bar()
            nc.vector.reduce_max(ncol[:], colA[:], axis=AX.X)
            em.bar()
            nc.vector.tensor_scalar(colB[:], colA[:], ncol[:, 0:1], None, ALU.is_ge)
            em.bar()
            nc.vector.tensor_tensor_scan(colC[:], ones8[:], colB[:], 0.0,
                                         ALU.mult, ALU.add)
            em.bar()
            nc.vector.tensor_scalar(colC[:], colC[:], 1.0, None, ALU.is_le)
            em.bar()
            nc.vector.tensor_tensor(colB[:], colB[:], colC[:], ALU.mult)
            em.bar()
            nc.vector.scalar_tensor_tensor(cslot, colB[:], ncol[:, 0:1], cslot,
                                           ALU.mult, ALU.add)
            em.bar()
            nc.vector.tensor_copy(tmpb[:, 0:NE], cslot)
            em.bar()
            nc.tensor.transpose(pst[0:NE, 0:128], tmpb[:, 0:NE], ident[:])
            em.bar()
            nc.vector.tensor_copy(combT[:, t * 128:(t + 1) * 128], pst[0:NE, 0:128])
            em.bar()
        em.bar()

        def wmark():
            C["moe_wld"] = em.dcount

        def wwait():
            nc.tensor.wait_ge(em.dsem, C["moe_wld"])

        w1load(w1b[0], 0, 0)
        wmark()
        for e in range(NE):
            if C["moe_ymm"] > 0:
                nc.gpsimd.wait_ge(yev_s, C["moe_ymm"])
            w2load(e)
            w2mark = em.dcount
            for fc in range(8):
                buf = w1b[fc % 2]
                wwait()
                nxt_e, nxt_fc = e, fc + 1
                if nxt_fc == 8:
                    nxt_e, nxt_fc = e + 1, 0
                if C["moe_hmm"] > 0:
                    nc.gpsimd.wait_ge(hmm_s, C["moe_hmm"])
                if nxt_e < NE:
                    w1load(w1b[(fc + 1) % 2], nxt_e, nxt_fc)
                elif g + 1 < NGRP:
                    w1load(w1b[(fc + 1) % 2], 0, 0)
                wmark()
                for fs in range(4):
                    i = C["moe_hmm"]
                    if fc == 0 and fs == 0 and C["moe_ymm"] > 0:
                        nc.scalar.wait_ge(ymm_s, C["moe_ymm"])
                    if i >= 2:
                        nc.tensor.wait_ge(hev_s, i - 1)
                    for kd in range(8):
                        inst = nc.tensor.matmul(
                            psA[i % 2][:, 0:512],
                            buf[:, kd * 512 + fs * 128:kd * 512 + (fs + 1) * 128],
                            ln2T[:, kd * GRP:(kd + 1) * GRP],
                            start=(kd == 0), stop=(kd == 7))
                    inst.then_inc(hmm_s, 1)
                    C["moe_hmm"] += 1
                    fcol = b1c[:, e * 32 + fc * 4 + fs:e * 32 + fc * 4 + fs + 1]
                    nc.scalar.wait_ge(hmm_s, C["moe_hmm"])
                    nc.scalar.activation(
                        hbuf[:, (fc * 4 + fs) * GRP:(fc * 4 + fs + 1) * GRP],
                        psA[i % 2][:, 0:512], AF.Silu,
                        bias=fcol).then_inc(hev_s, 1)
            nc.tensor.wait_ge(hev_s, C["moe_hmm"])
            nc.tensor.wait_ge(em.dsem, w2mark)
            for tc in range(NT):
                i = C["moe_ymm"]
                if i >= 2:
                    nc.tensor.wait_ge(yev_s, i - 1)
                for kf in range(32):
                    for nn in range(2):
                        inst = nc.tensor.matmul(
                            psO[i % 2][:, nn * 512:(nn + 1) * 512],
                            hbuf[:, kf * GRP + tc * 128:kf * GRP + (tc + 1) * 128],
                            w2full[:, kf * D + nn * 512:kf * D + (nn + 1) * 512],
                            start=(kf == 0), stop=(kf == 31))
                inst.then_inc(ymm_s, 1)
                C["moe_ymm"] += 1
                ccol = comb[:, tc * NE + e:tc * NE + e + 1]
                nc.vector.wait_ge(ymm_s, C["moe_ymm"])
                nc.vector.scalar_tensor_tensor(outacc[:, tc * D:(tc + 1) * D],
                                               psO[i % 2][:, 0:D], ccol,
                                               outacc[:, tc * D:(tc + 1) * D],
                                               ALU.mult,
                                               ALU.add).then_inc(yev_s, 1)
        em.bar()
        for tc in range(NT):
            for nn in range(2):
                nc.tensor.matmul(psO[0][:, nn * 512:(nn + 1) * 512],
                                 combT[:, tc * 128:(tc + 1) * 128],
                                 b2sb[:, nn * 512:(nn + 1) * 512],
                                 start=True, stop=True)
            em.bar()
            nc.vector.tensor_tensor(outacc[:, tc * D:(tc + 1) * D],
                                    outacc[:, tc * D:(tc + 1) * D], psO[0][:, 0:D],
                                    ALU.add)
            em.bar()
        for t in range(NT):
            row = (g * NT + t) * 128
            em.dma(xg[:], io["x_tile"](row))
            em.flush()
            nc.vector.scalar_tensor_tensor(tmpf[:], outacc[:, t * D:(t + 1) * D],
                                           0.5, xg[:], ALU.mult, ALU.add)
            em.bar()
            em.dma(io["xo_tile"](row), tmpf[:])
            em.flush()
    em.pop_scope()
    if standalone:
        em.close()
    return em


def run_L0F(x, g1, be1, mix_w, mix_b, g2, be2, router_w, w1, b1, w2, b2):
    em = _cached("build_L0F", build_L0F)
    xf = x.reshape(T, D)
    rows = np.concatenate([g1, be1, g2, be2, mix_b]).astype(np.float32)[None, :]
    b1c = np.ascontiguousarray(b1.reshape(8, 32, 128).transpose(2, 0, 1).reshape(128, 256))
    ident = np.eye(128, dtype=np.float32).astype(BFNP)
    common = {"mixw": mix_w, "router": router_w, "w1": w1, "w2": w2,
              "rows": rows, "b1c": b1c, "b2r": np.ascontiguousarray(b2.reshape(1, -1)), "ident": ident}
    in_maps = [{"x": xf[c * TOKC:(c + 1) * TOKC], **common} for c in range(NCORES)]
    res = _run(em.nc, in_maps)
    out = np.concatenate([res.results[c]["xo"] for c in range(NCORES)], 0)
    return out.reshape(B, S, D)



def build_L2F(em=None, io=None):
    standalone = em is None
    if standalone:
        em = Emit("l2f").start()
        x_in = em.din("x", [TOKC, D], F32)
        x_out = em.dout("xo", [TOKC, D], F32)
        io = {"x_tile": lambda r: x_in[r:r + 128, :],
              "xo_tile": lambda r: x_out[r:r + 128, :],
              "gatew": em.din("gatew", [D, 8], F32),
              "w1": em.din("w1", [9, D, 4096], F32),
              "w2": em.din("w2", [9, 4096, D], F32),
              "ident": em.din("ident", [128, 128], BF16)}
    nc = em.nc
    NE, HID = 9, 4096
    gate_in, w1_in, w2_in, ident_in = io["gatew"], io["w1"], io["w2"], io["ident"]
    em.push_scope()

    GRP = 512
    NGRP = TOKC // GRP
    NT = GRP // 128

    gatew = em.sb([128, 8 * 8], BF16, "gw")
    ident = em.sb([128, 128], BF16, "id")
    xg = em.sb([128, D], F32, "xg")
    n0 = em.sb([128, D], F32, "n0")
    tmpf = em.sb([128, D], F32, "tmpf")
    tmpb = em.sb([128, D], BF16, "tmpb")
    nT = em.sb([128, 8 * GRP], BF16, "nT")
    hbuf = em.sb([128, 32 * GRP], BF16, "h")
    outacc = em.sb([128, NT * D], F32, "oacc")
    comb = em.sb([128, NT * NE], F32, "comb")
    w1b = [em.sb([128, 8 * 512], BF16, "w1b") for _ in range(2)]
    w2full = em.sb([128, 32 * D], BF16, "w2f")
    stats = em.sb([128, 16], F32, "st")
    mu = em.sb([128, 1], F32, "mu")
    rstd = em.sb([128, 1], F32, "rs")
    eps = em.const_col(LN_EPS)
    colA = em.sb([128, 8], F32, "cA")
    colB = em.sb([128, 8], F32, "cB")
    colC = em.sb([128, 8], F32, "cC")
    ones8 = em.sb([128, 8], F32, "o8")
    ncol = em.sb([128, 1], F32, "ncol")
    scol = em.sb([128, 1], F32, "scol")
    g1col = em.sb([128, 1], F32, "g1c")

    pst = em.ps([128, 256], BF16, "pt")
    psR = em.ps([128, 512], F32, "pr")
    psA = [em.ps([128, 512], F32, "pa") for _ in range(2)]
    psO = [em.ps([128, D], F32, "po") for _ in range(2)]
    hmm_s = em.sem("moe_hmm")
    hev_s = em.sem("moe_hev")
    ymm_s = em.sem("moe_ymm")
    yev_s = em.sem("moe_yev")
    for key in ["moe_hmm", "moe_hev", "moe_ymm", "moe_yev"]:
        em.counters.setdefault(key, 0)
    C = em.counters

    nc.vector.memset(ones8[:], 1.0)
    for k in range(8):
        em.dma(gatew[:, k * 8:(k + 1) * 8], gate_in[k * 128:(k + 1) * 128, :],
               nc.gpsimd)
    em.dma(ident[:], ident_in)
    em.flush(nc.gpsimd)
    em.flush()

    def w1load(buf, e, fc):
        em.dma(buf[:].rearrange("p (dc f) -> p dc f", dc=8),
               w1_in[e].rearrange("(dc p) f -> p dc f", p=128)
               [:, :, fc * 512:(fc + 1) * 512], nc.gpsimd)

    def w2load(e):
        for kc in range(8):
            em.dma(w2full[:, kc * 4 * D:(kc + 1) * 4 * D]
                   .rearrange("p (fc d) -> p fc d", fc=4),
                   w2_in[e][kc * 512:(kc + 1) * 512, :]
                   .rearrange("(fc p) d -> p fc d", p=128),
                   nc.gpsimd)

    for g in range(NGRP):
        for t in range(NT):
            row = (g * NT + t) * 128
            em.dma(xg[:], io["x_tile"](row))
            em.flush()
            em.ln_cols(xg[:], mu, rstd, stats, eps)
            em.bar()
            nc.vector.tensor_scalar(n0[:], xg[:], mu[:, 0:1], rstd[:, 0:1],
                                    ALU.subtract, ALU.mult)
            em.bar()
            _transpose_to(em, nT, n0[:], tmpb, ident, pst, t * 128, GRP)
            for kd in range(8):
                nc.tensor.matmul(psR[:, 0:8],
                                 nT[:, kd * GRP + t * 128:kd * GRP + (t + 1) * 128],
                                 gatew[:, kd * 8:(kd + 1) * 8],
                                 start=(kd == 0), stop=(kd == 7))
            em.bar()
            nc.vector.tensor_copy(colA[:], psR[:, 0:8])
            em.bar()
            nc.vector.reduce_max(ncol[:], colA[:], axis=AX.X, negate=True)
            em.bar()
            nc.vector.tensor_scalar(colB[:], colA[:], ncol[:, 0:1], None, ALU.add)
            em.bar()
            nc.vector.tensor_scalar(colB[:], colB[:], 0.0, None, ALU.is_ge)
            em.bar()
            nc.vector.tensor_tensor_scan(colC[:], ones8[:], colB[:], 0.0,
                                         ALU.mult, ALU.add)
            em.bar()
            nc.vector.tensor_scalar(colC[:], colC[:], 1.0, None, ALU.is_le)
            em.bar()
            nc.vector.tensor_tensor(colB[:], colB[:], colC[:], ALU.mult)
            em.bar()
            nc.vector.tensor_scalar(colC[:], colB[:], -1e30, None, ALU.mult)
            em.bar()
            nc.vector.tensor_tensor(colC[:], colA[:], colC[:], ALU.add)
            em.bar()
            nc.vector.reduce_max(scol[:], colC[:], axis=AX.X)
            em.bar()
            nc.scalar.activation(g1col[:], scol[:], AF.Exp, bias=ncol[:, 0:1])
            em.bar()
            nc.vector.tensor_scalar(g1col[:], g1col[:], 1.0, None, ALU.add)
            em.bar()
            nc.vector.reciprocal(g1col[:], g1col[:])
            em.bar()
            nc.vector.tensor_scalar(colC[:], colC[:], scol[:, 0:1], None, ALU.is_ge)
            em.bar()
            nc.vector.tensor_tensor_scan(colA[:], ones8[:], colC[:], 0.0,
                                         ALU.mult, ALU.add)
            em.bar()
            nc.vector.tensor_scalar(colA[:], colA[:], 1.0, None, ALU.is_le)
            em.bar()
            nc.vector.tensor_tensor(colC[:], colC[:], colA[:], ALU.mult)
            em.bar()
            cslot = comb[:, t * NE:t * NE + 8]
            nc.vector.tensor_scalar(cslot, colB[:], g1col[:, 0:1], None, ALU.mult)
            em.bar()
            nc.vector.tensor_scalar(scol[:], g1col[:], -1.0, 1.0, ALU.mult, ALU.add)
            em.bar()
            nc.vector.scalar_tensor_tensor(cslot, colC[:], scol[:, 0:1], cslot,
                                           ALU.mult, ALU.add)
            em.bar()
            nc.vector.memset(comb[:, t * NE + 8:t * NE + 9], 1.0)
            em.bar()
        em.bar()

        def wmark():
            C["moe_wld"] = em.dcount

        def wwait():
            nc.tensor.wait_ge(em.dsem, C["moe_wld"])

        C.setdefault("moe_wld", 0)
        w1load(w1b[0], 0, 0)
        wmark()
        for e in range(NE):
            if C["moe_ymm"] > 0:
                nc.gpsimd.wait_ge(yev_s, C["moe_ymm"])
            w2load(e)
            w2mark = em.dcount
            for fc in range(8):
                buf = w1b[fc % 2]
                wwait()
                nxt_e, nxt_fc = e, fc + 1
                if nxt_fc == 8:
                    nxt_e, nxt_fc = e + 1, 0
                if C["moe_hmm"] > 0:
                    nc.gpsimd.wait_ge(hmm_s, C["moe_hmm"])
                if nxt_e < NE:
                    w1load(w1b[(fc + 1) % 2], nxt_e, nxt_fc)
                elif g + 1 < NGRP:
                    w1load(w1b[(fc + 1) % 2], 0, 0)
                wmark()
                for fs in range(4):
                    i = C["moe_hmm"]
                    if fc == 0 and fs == 0 and C["moe_ymm"] > 0:
                        nc.scalar.wait_ge(ymm_s, C["moe_ymm"])
                    if i >= 2:
                        nc.tensor.wait_ge(hev_s, i - 1)
                    for kd in range(8):
                        inst = nc.tensor.matmul(
                            psA[i % 2][:, 0:512],
                            buf[:, kd * 512 + fs * 128:kd * 512 + (fs + 1) * 128],
                            nT[:, kd * GRP:(kd + 1) * GRP],
                            start=(kd == 0), stop=(kd == 7))
                    inst.then_inc(hmm_s, 1)
                    C["moe_hmm"] += 1
                    nc.scalar.wait_ge(hmm_s, C["moe_hmm"])
                    nc.scalar.activation(
                        hbuf[:, (fc * 4 + fs) * GRP:(fc * 4 + fs + 1) * GRP],
                        psA[i % 2][:, 0:512], AF.Silu).then_inc(hev_s, 1)
            nc.tensor.wait_ge(hev_s, C["moe_hmm"])
            nc.tensor.wait_ge(em.dsem, w2mark)
            for tc in range(NT):
                i = C["moe_ymm"]
                if i >= 2:
                    nc.tensor.wait_ge(yev_s, i - 1)
                for kf in range(32):
                    for nn in range(2):
                        inst = nc.tensor.matmul(
                            psO[i % 2][:, nn * 512:(nn + 1) * 512],
                            hbuf[:, kf * GRP + tc * 128:kf * GRP + (tc + 1) * 128],
                            w2full[:, kf * D + nn * 512:kf * D + (nn + 1) * 512],
                            start=(kf == 0), stop=(kf == 31))
                inst.then_inc(ymm_s, 1)
                C["moe_ymm"] += 1
                ccol = comb[:, tc * NE + e:tc * NE + e + 1]
                nc.vector.wait_ge(ymm_s, C["moe_ymm"])
                if e == 0:
                    nc.vector.tensor_scalar(outacc[:, tc * D:(tc + 1) * D],
                                            psO[i % 2][:, 0:D], ccol, None,
                                            ALU.mult).then_inc(yev_s, 1)
                else:
                    nc.vector.scalar_tensor_tensor(outacc[:, tc * D:(tc + 1) * D],
                                                   psO[i % 2][:, 0:D], ccol,
                                                   outacc[:, tc * D:(tc + 1) * D],
                                                   ALU.mult,
                                                   ALU.add).then_inc(yev_s, 1)
        em.bar()
        for t in range(NT):
            row = (g * NT + t) * 128
            em.dma(xg[:], io["x_tile"](row))
            em.flush()
            nc.vector.scalar_tensor_tensor(tmpf[:], outacc[:, t * D:(t + 1) * D],
                                           0.5, xg[:], ALU.mult, ALU.add)
            em.bar()
            em.dma(io["xo_tile"](row), tmpf[:])
            em.flush()
    em.pop_scope()
    if standalone:
        em.close()
    return em


def run_L2F(x, shared_w1, shared_w2, experts_w1, experts_w2, gate_w):
    em = _cached("build_L2F", build_L2F)
    xf = x.reshape(T, D)
    w1_all = np.concatenate([experts_w1.transpose(0, 2, 1),
                             shared_w1.T[None]], 0)
    w1_all = np.ascontiguousarray(w1_all)
    w2_all = np.concatenate([experts_w2.transpose(0, 2, 1),
                             shared_w2.T[None]], 0)
    w2_all = np.ascontiguousarray(w2_all)
    ident = np.eye(128, dtype=np.float32).astype(BFNP)
    common = {"gatew": np.ascontiguousarray(gate_w.T), "w1": w1_all,
              "w2": w2_all, "ident": ident}
    in_maps = [{"x": xf[c * TOKC:(c + 1) * TOKC], **common} for c in range(NCORES)]
    res = _run(em.nc, in_maps)
    out = np.concatenate([res.results[c]["xo"] for c in range(NCORES)], 0)
    return out.reshape(B, S, D)



def build_L2A(em=None, io=None):
    POSC = TOKC // 2
    F3 = 3072
    standalone = em is None
    if standalone:
        em = Emit("l2a").start()
        x0_in = em.din("x0", [POSC, D], F32)
        x1_in = em.din("x1", [POSC, D], F32)
        xo0 = em.dout("xo0", [POSC, D], F32)
        xo1 = em.dout("xo1", [POSC, D], F32)
        io = {"x_tile": lambda b, r: (x0_in if b == 0 else x1_in)[r:r + 128, :],
              "xo_tile": lambda b, r: (xo0 if b == 0 else xo1)[r:r + 128, :],
              "win": em.din("win", [D, F3], F32),
              "binr": em.din("binr", [1, F3], F32),
              "wout": em.din("wout", [D, D], F32),
              "boutr": em.din("boutr", [1, D], F32),
              "ident": em.din("ident", [128, 128], BF16)}
    nc = em.nc
    win_in, bin_in = io["win"], io["binr"]
    wout_in, bout_in, ident_in = io["wout"], io["boutr"], io["ident"]
    em.push_scope()

    win = em.sb([128, 8 * F3], BF16, "win")
    wout = em.sb([128, 8 * D], BF16, "wout")
    binb = em.sb([128, F3], F32, "binb")
    boutb = em.sb([128, D], F32, "boutb")
    stage = em.sb([1, F3], F32, "stage")
    ident = em.sb([128, 128], BF16, "id")
    onesf = em.ones_row(128, F32)
    xg = [em.sb([128, D], F32, "xg") for _ in range(2)]
    xn = em.sb([128, D], F32, "xn")
    tmpb = em.sb([128, D], BF16, "tmpb")
    xnT = em.sb([128, 8 * 128], BF16, "xnT")
    qkv = [em.sb([128, F3], F32, "qkv") for _ in range(2)]
    tmpf = em.sb([128, D], F32, "tmpf")
    sc = em.sb([128, 4 * 16], F32, "sc")
    pcol = em.sb([128, 4 * 16], F32, "pcol")
    ctx = em.sb([128, D], F32, "ctx")
    stats = em.sb([128, 16], F32, "st")
    mu = em.sb([128, 1], F32, "mu")
    rstd = em.sb([128, 1], F32, "rs")
    eps = em.const_col(LN_EPS)

    pst = em.ps([128, 256], BF16, "pt")
    psQ = em.ps([128, 512], F32, "pq")
    psO = em.ps([128, D], F32, "po")

    for k in range(8):
        em.dma(win[:, k * F3:(k + 1) * F3], win_in[k * 128:(k + 1) * 128, :],
               nc.gpsimd)
        em.dma(wout[:, k * D:(k + 1) * D], wout_in[k * 128:(k + 1) * 128, :],
               nc.gpsimd)
    em.dma(ident[:], ident_in)
    em.dma(stage[:], bin_in)
    em.flush(nc.gpsimd)
    em.flush()
    for nn in range(6):
        nc.tensor.matmul(psQ[:, 0:512], onesf[:, 0:128],
                         stage[0:1, nn * 512:(nn + 1) * 512], start=True, stop=True)
        em.bar()
        nc.vector.tensor_copy(binb[:, nn * 512:(nn + 1) * 512], psQ[:, 0:512])
        em.bar()
    em.dma(stage[0:1, 0:D], bout_in)
    em.flush()
    for nn in range(2):
        nc.tensor.matmul(psQ[:, 0:512], onesf[:, 0:128],
                         stage[0:1, nn * 512:(nn + 1) * 512], start=True, stop=True)
        em.bar()
        nc.vector.tensor_copy(boutb[:, nn * 512:(nn + 1) * 512], psQ[:, 0:512])
        em.bar()

    NTP = POSC // 128
    for t in range(NTP):
        row = t * 128
        for b in range(2):
            em.dma(xg[b][:], io["x_tile"](b, row))
            em.flush()
            em.ln_cols(xg[b][:], mu, rstd, stats, eps)
            em.bar()
            nc.vector.tensor_scalar(xn[:], xg[b][:], mu[:, 0:1], rstd[:, 0:1],
                                    ALU.subtract, ALU.mult)
            em.bar()
            _transpose_to(em, xnT, xn[:], tmpb, ident, pst, 0, 128)
            for nn in range(6):
                for kd in range(8):
                    nc.tensor.matmul(psQ[:, 0:512],
                                     xnT[:, kd * 128:(kd + 1) * 128],
                                     win[:, kd * F3 + nn * 512:kd * F3 + (nn + 1) * 512],
                                     start=(kd == 0), stop=(kd == 7))
                em.bar()
                nc.vector.tensor_tensor(qkv[b][:, nn * 512:(nn + 1) * 512],
                                        psQ[:, 0:512], binb[:, nn * 512:(nn + 1) * 512],
                                        ALU.add)
                em.bar()
        for s in range(2):
            for tt in range(2):
                nc.vector.tensor_tensor(tmpf[:], qkv[s][:, 0:D],
                                        qkv[tt][:, D:2 * D], ALU.mult)
                em.bar()
                nc.vector.reduce_sum(sc[:, (s * 2 + tt) * 16:(s * 2 + tt + 1) * 16],
                                     tmpf[:].rearrange("p (h d) -> p h d", h=16),
                                     axis=AX.X)
                em.bar()
        nc.vector.tensor_scalar(sc[:], sc[:], 0.125, None, ALU.mult)
        em.bar()
        for s in range(2):
            nc.vector.tensor_tensor(pcol[:, (s * 2) * 16:(s * 2 + 1) * 16],
                                    sc[:, (s * 2) * 16:(s * 2 + 1) * 16],
                                    sc[:, (s * 2 + 1) * 16:(s * 2 + 2) * 16],
                                    ALU.subtract)
            em.bar()
            nc.scalar.activation(pcol[:, (s * 2) * 16:(s * 2 + 1) * 16],
                                 pcol[:, (s * 2) * 16:(s * 2 + 1) * 16], AF.Sigmoid)
            em.bar()
            nc.vector.tensor_scalar(pcol[:, (s * 2 + 1) * 16:(s * 2 + 2) * 16],
                                    pcol[:, (s * 2) * 16:(s * 2 + 1) * 16],
                                    -1.0, 1.0, ALU.mult, ALU.add)
            em.bar()
        for s in range(2):
            for h in range(16):
                nc.vector.tensor_scalar(ctx[:, h * 64:(h + 1) * 64],
                                        qkv[0][:, 2 * D + h * 64:2 * D + (h + 1) * 64],
                                        pcol[:, s * 32 + h:s * 32 + h + 1],
                                        None, ALU.mult)
            em.bar()
            for h in range(16):
                nc.vector.scalar_tensor_tensor(
                    ctx[:, h * 64:(h + 1) * 64],
                    qkv[1][:, 2 * D + h * 64:2 * D + (h + 1) * 64],
                    pcol[:, s * 32 + 16 + h:s * 32 + 16 + h + 1],
                    ctx[:, h * 64:(h + 1) * 64], ALU.mult, ALU.add)
            em.bar()
            _transpose_to(em, xnT, ctx[:], tmpb, ident, pst, 0, 128)
            for nn in range(2):
                for kd in range(8):
                    nc.tensor.matmul(psO[:, nn * 512:(nn + 1) * 512],
                                     xnT[:, kd * 128:(kd + 1) * 128],
                                     wout[:, kd * D + nn * 512:kd * D + (nn + 1) * 512],
                                     start=(kd == 0), stop=(kd == 7))
            em.bar()
            nc.vector.tensor_tensor(tmpf[:], psO[:, 0:D], boutb[:], ALU.add)
            em.bar()
            nc.vector.tensor_tensor(tmpf[:], tmpf[:], xg[s][:], ALU.add)
            em.bar()
            em.dma(io["xo_tile"](s, row), tmpf[:])
            em.flush()
    em.pop_scope()
    if standalone:
        em.close()
    return em


def run_L2A(x, in_proj_weight, in_proj_bias, out_proj_weight, out_proj_bias):
    em = _cached("build_L2A", build_L2A)
    H, hd = 16, 64
    perm = np.concatenate([
        np.concatenate([np.arange(h * 192 + c * 64, h * 192 + c * 64 + 64)
                        for h in range(H)])
        for c in range(3)])
    winp = np.ascontiguousarray(in_proj_weight[perm, :].T)
    binp = np.ascontiguousarray(in_proj_bias[perm])[None, :]
    woutp = np.ascontiguousarray(out_proj_weight.T)
    boutp = np.ascontiguousarray(out_proj_bias)[None, :]
    ident = np.eye(128, dtype=np.float32).astype(BFNP)
    POSC = TOKC // 2
    common = {"win": winp, "binr": binp, "wout": woutp, "boutr": boutp,
              "ident": ident}
    in_maps = [{"x0": x[0, c * POSC:(c + 1) * POSC], "x1": x[1, c * POSC:(c + 1) * POSC],
                **common} for c in range(NCORES)]
    res = _run(em.nc, in_maps)
    o0 = np.concatenate([res.results[c]["xo0"] for c in range(NCORES)], 0)
    o1 = np.concatenate([res.results[c]["xo1"] for c in range(NCORES)], 0)
    return np.stack([o0, o1], 0)



def build_L0A_qkv():
    em = Emit("l0aq").start()
    nc = em.nc
    F3 = 3072
    x_in = em.din("x", [TOKC, D], F32)
    w_in = em.din("wqkv", [D, F3], F32)
    gb_in = em.din("gbrow", [1, 2 * D], F32)
    ident_in = em.din("ident", [128, 128], BF16)
    qkv_out = em.dout("qkv", [TOKC, F3], BF16)

    w = em.sb([128, 8 * F3], BF16, "w")
    ident = em.sb([128, 128], BF16, "id")
    onesf = em.ones_row(128, F32)
    stage = em.sb([1, D], F32, "stage")
    gmb = em.sb([128, D], F32, "gmb")
    btb = em.sb([128, D], F32, "btb")
    xg = em.sb([128, D], F32, "xg")
    xn = em.sb([128, D], F32, "xn")
    tmpb = em.sb([128, D], BF16, "tmpb")
    xnT = em.sb([128, 8 * 128], BF16, "xnT")
    ob = em.sb([128, F3], BF16, "ob")
    stats = em.sb([128, 16], F32, "st")
    mu = em.sb([128, 1], F32, "mu")
    rstd = em.sb([128, 1], F32, "rs")
    eps = em.const_col(LN_EPS)
    pst = em.ps([128, 256], BF16, "pt")
    psQ = em.ps([128, 512], F32, "pq")

    for k in range(8):
        em.dma(w[:, k * F3:(k + 1) * F3], w_in[k * 128:(k + 1) * 128, :], nc.gpsimd)
    em.dma(ident[:], ident_in)
    em.flush(nc.gpsimd)
    em.flush()
    for i, dst in enumerate([gmb, btb]):
        em.dma(stage[:], gb_in[0:1, i * D:(i + 1) * D])
        em.flush()
        _bcast_row(em, dst, stage[0:1, :], onesf, psQ)
    for t in range(TOKC // 128):
        em.dma(xg[:], x_in[t * 128:(t + 1) * 128, :])
        em.flush()
        em.ln_cols(xg[:], mu, rstd, stats, eps)
        em.bar()
        nc.vector.tensor_scalar(xn[:], xg[:], mu[:, 0:1], rstd[:, 0:1],
                                ALU.subtract, ALU.mult)
        em.bar()
        _ln_affine(em, xn[:], xn[:], gmb, btb, mu, rstd, stats, eps, xg)
        _transpose_to(em, xnT, xn[:], tmpb, ident, pst, 0, 128)
        for nn in range(6):
            for kd in range(8):
                nc.tensor.matmul(psQ[:, 0:512],
                                 xnT[:, kd * 128:(kd + 1) * 128],
                                 w[:, kd * F3 + nn * 512:kd * F3 + (nn + 1) * 512],
                                 start=(kd == 0), stop=(kd == 7))
            em.bar()
            nc.vector.tensor_copy(ob[:, nn * 512:(nn + 1) * 512], psQ[:, 0:512])
            em.bar()
        em.dma(qkv_out[t * 128:(t + 1) * 128, :], ob[:])
        em.flush()
    em.close()
    return em


def build_L0A_attn():
    em = Emit("l0aa").start()
    nc = em.nc
    NP = 4
    SQ = S
    NQC = SQ // 128
    q_in = em.din("qf", [NP, 64, SQ], BF16)
    k_in = em.din("kf", [NP, 64, SQ], BF16)
    v_in = em.din("vt", [NP, SQ, 64], BF16)
    x_in = em.din("xs", [NP, SQ, 64], F32)
    mask_in = em.din("mask", [128, 128], F32)
    ident_in = em.din("ident", [128, 128], BF16)
    xo = em.dout("xo", [NP, SQ, 64], F32)

    qsb = em.sb([64, SQ], BF16, "q")
    ksb = em.sb([64, SQ], BF16, "k")
    vsb = em.sb([128, 64 * 64], BF16, "v")
    xsb = em.sb([128, 64 * 64], F32, "xs")
    mask = em.sb([128, 128], F32, "mask")
    ident = em.sb([128, 128], BF16, "id")
    S_sb = em.sb([128, SQ], F32, "S")
    P_sb = em.sb([128, SQ], BF16, "P")
    ptb = em.sb([128, 2 * 512], BF16, "ptb")
    osb = em.sb([128, 64], F32, "o")
    negm = em.sb([128, 1], F32, "nm")
    lcol = em.sb([128, 1], F32, "l")
    psS = [em.ps([128, 512], F32, "ps") for _ in range(2)]
    pstg = [em.ps([128, 512], BF16, "pg") for _ in range(2)]
    psO = em.ps([128, 64], F32, "pO")

    em.dma(mask[:], mask_in)
    em.dma(ident[:], ident_in)
    em.flush()

    mm_sem = em.sem("mm")
    cp_sem = em.sem("cp")
    tr_sem = em.sem("tr")
    c2_sem = em.sem("c2")
    pv_sem = em.sem("pv")
    n_mm = [0]; n_cp = [0]; n_tr = [0]; n_c2 = [0]; n_pv = [0]

    for p in range(NP):
        em.dma(qsb[:], q_in[p], nc.sync)
        em.dma(ksb[:], k_in[p], nc.sync)
        em.dma(vsb[:].rearrange("p (kb d) -> p kb d", kb=64),
               v_in[p].rearrange("(kb p) d -> p kb d", p=128), nc.sync)
        em.dma(xsb[:].rearrange("p (kb d) -> p kb d", kb=64),
               x_in[p].rearrange("(kb p) d -> p kb d", p=128), nc.sync)
        em.flush()
        for qc in range(NQC):
            L = (qc + 1) * 128
            nchunks = (L + 511) // 512
            cp0, mm0 = n_cp[0], n_mm[0]
            for i in range(nchunks):
                w = min(512, L - i * 512)
                bank = psS[i % 2]
                if i >= 2:
                    nc.tensor.wait_ge(cp_sem, cp0 + i - 1)
                nc.tensor.matmul(bank[:, 0:w], qsb[:, qc * 128:(qc + 1) * 128],
                                 ksb[:, i * 512:i * 512 + w],
                                 start=True, stop=True).then_inc(mm_sem, 1)
                n_mm[0] += 1
                nc.scalar.wait_ge(mm_sem, mm0 + i + 1)
                nc.scalar.activation(S_sb[:, i * 512:i * 512 + w], bank[:, 0:w],
                                     AF.Copy, scale=0.125).then_inc(cp_sem, 1)
                n_cp[0] += 1
            em.bar()
            nc.vector.tensor_tensor(S_sb[:, L - 128:L], S_sb[:, L - 128:L],
                                    mask[:], ALU.add)
            em.bar()
            nc.vector.reduce_max(negm[:], S_sb[:, 0:L], axis=AX.X, negate=True)
            em.bar()
            nc.scalar.activation(P_sb[:, 0:L], S_sb[:, 0:L], AF.Exp,
                                 bias=negm[:, 0:1], accum_out=lcol[:, 0:1])
            em.bar()
            nc.vector.reciprocal(lcol[:], lcol[:])
            em.bar()
            nblk = L // 128
            ngrp = (nblk + 3) // 4
            tr0, c20, pv0 = n_tr[0], n_c2[0], n_pv[0]
            for g in range(ngrp):
                blo = g * 4
                bhi = min(blo + 4, nblk)
                bank = pstg[g % 2]
                if g >= 2:
                    nc.tensor.wait_ge(c2_sem, c20 + g - 1)
                for b in range(blo, bhi):
                    inst = nc.tensor.transpose(
                        bank[:, (b - blo) * 128:(b - blo + 1) * 128],
                        P_sb[:, b * 128:(b + 1) * 128], ident[:])
                    if b == bhi - 1:
                        inst.then_inc(tr_sem, 1)
                        n_tr[0] += 1
                nc.vector.wait_ge(tr_sem, tr0 + g + 1)
                if g >= 2:
                    nc.vector.wait_ge(pv_sem, pv0 + g - 1)
                nc.vector.tensor_copy(
                    ptb[:, (g % 2) * 512:(g % 2) * 512 + (bhi - blo) * 128],
                    bank[:, 0:(bhi - blo) * 128]).then_inc(c2_sem, 1)
                n_c2[0] += 1
                nc.tensor.wait_ge(c2_sem, c20 + g + 1)
                for b in range(blo, bhi):
                    inst = nc.tensor.matmul(
                        psO[:, 0:64],
                        ptb[:, (g % 2) * 512 + (b - blo) * 128:
                            (g % 2) * 512 + (b - blo + 1) * 128],
                        vsb[:, b * 64:(b + 1) * 64],
                        start=(b == 0), stop=(b == nblk - 1),
                        skip_group_check=True)
                    if b == bhi - 1:
                        inst.then_inc(pv_sem, 1)
                        n_pv[0] += 1
            em.bar()
            nc.vector.tensor_scalar(osb[:], psO[:, 0:64], lcol[:, 0:1], None,
                                    ALU.mult)
            em.bar()
            nc.vector.tensor_tensor(osb[:], osb[:], xsb[:, qc * 64:(qc + 1) * 64],
                                    ALU.add)
            em.bar()
            em.dma(xo[p][qc * 128:(qc + 1) * 128, :], osb[:])
            em.flush()
    em.close()
    return em


def build_L3A_qkv():
    em = Emit("l3aq").start()
    nc = em.nc
    F3 = 3072
    x_in = em.din("x", [TOKC, D], F32)
    w_in = em.din("wqkv", [D, F3], F32)
    ident_in = em.din("ident", [128, 128], BF16)
    qkv_out = em.dout("qkv", [TOKC, F3], BF16)

    w = em.sb([128, 8 * F3], BF16, "w")
    ident = em.sb([128, 128], BF16, "id")
    xg = em.sb([128, D], F32, "xg")
    xn = em.sb([128, D], F32, "xn")
    tmpb = em.sb([128, D], BF16, "tmpb")
    xnT = em.sb([128, 8 * 128], BF16, "xnT")
    ob = em.sb([128, F3], BF16, "ob")
    stats = em.sb([128, 16], F32, "st")
    mu = em.sb([128, 1], F32, "mu")
    rstd = em.sb([128, 1], F32, "rs")
    eps = em.const_col(LN_EPS)
    pst = em.ps([128, 256], BF16, "pt")
    psQ = em.ps([128, 512], F32, "pq")

    for k in range(8):
        em.dma(w[:, k * F3:(k + 1) * F3], w_in[k * 128:(k + 1) * 128, :], nc.gpsimd)
    em.dma(ident[:], ident_in)
    em.flush(nc.gpsimd)
    em.flush()
    for t in range(TOKC // 128):
        em.dma(xg[:], x_in[t * 128:(t + 1) * 128, :])
        em.flush()
        em.ln_cols(xg[:], mu, rstd, stats, eps)
        em.bar()
        nc.vector.tensor_scalar(xn[:], xg[:], mu[:, 0:1], rstd[:, 0:1],
                                ALU.subtract, ALU.mult)
        em.bar()
        _transpose_to(em, xnT, xn[:], tmpb, ident, pst, 0, 128)
        for nn in range(6):
            for kd in range(8):
                nc.tensor.matmul(psQ[:, 0:512],
                                 xnT[:, kd * 128:(kd + 1) * 128],
                                 w[:, kd * F3 + nn * 512:kd * F3 + (nn + 1) * 512],
                                 start=(kd == 0), stop=(kd == 7))
            em.bar()
            nc.vector.tensor_copy(ob[:, nn * 512:(nn + 1) * 512], psQ[:, 0:512])
            em.bar()
        em.dma(qkv_out[t * 128:(t + 1) * 128, :], ob[:])
        em.flush()
    em.close()
    return em


def build_L3A_attn():
    em = Emit("l3aa").start()
    nc = em.nc
    NP = 4
    SQ = S
    NQC = SQ // 128
    q_in = em.din("qf", [NP, 64, SQ], BF16)
    k_in = em.din("kf", [NP, 64, SQ], BF16)
    v_in = em.din("vt", [NP, SQ, 64], BF16)
    mdiag_in = em.din("mdiag", [128, 128], F32)
    mband_in = em.din("mband", [128, 128], F32)
    mc1_in = em.din("mc1", [128, 128], F32)
    ident_in = em.din("ident", [128, 128], BF16)
    ho = em.dout("ho", [NP, SQ, 64], BF16)

    qsb = em.sb([64, SQ], BF16, "q")
    ksb = em.sb([64, SQ], BF16, "k")
    vsb = em.sb([128, 64 * 64], BF16, "v")
    mdiag = em.sb([128, 128], F32, "md")
    mband = em.sb([128, 128], F32, "mb")
    mc1 = em.sb([128, 128], F32, "mc")
    ident = em.sb([128, 128], BF16, "id")
    S_sb = em.sb([128, 320], F32, "S")
    P_sb = em.sb([128, 320], BF16, "P")
    ptb = em.sb([128, 448], BF16, "ptb")
    osb = em.sb([128, 64], BF16, "o")
    negm = em.sb([128, 1], F32, "nm")
    lcol = em.sb([128, 1], F32, "l")
    psS = em.ps([128, 512], F32, "ps")
    pstg = em.ps([128, 512], BF16, "pg")
    psO = em.ps([128, 64], F32, "pO")

    em.dma(mdiag[:], mdiag_in)
    em.dma(mband[:], mband_in)
    em.dma(mc1[:], mc1_in)
    em.dma(ident[:], ident_in)
    em.flush()

    for p in range(NP):
        em.dma(qsb[:], q_in[p], nc.sync)
        em.dma(ksb[:], k_in[p], nc.sync)
        em.dma(vsb[:].rearrange("p (kb d) -> p kb d", kb=64),
               v_in[p].rearrange("(kb p) d -> p kb d", p=128), nc.sync)
        em.flush()
        for qc in range(NQC):
            if qc == 0:
                parts = [(0, 128, 0, mdiag)]
            elif qc == 1:
                parts = [(0, 128, 0, mc1), (128, 128, 1, mdiag)]
            else:
                parts = [(0, 64, -1, None),
                         ((qc - 1) * 128, 128, qc - 1, mband),
                         (qc * 128, 128, qc, mdiag)]
            W = sum(pp[1] for pp in parts)
            off = 0
            for (koff, kw, vb, msk) in parts:
                nc.tensor.matmul(psS[:, off:off + kw],
                                 qsb[:, qc * 128:(qc + 1) * 128],
                                 ksb[:, koff:koff + kw],
                                 start=(off == 0), stop=(off + kw == W))
                off += kw
            em.bar()
            nc.scalar.activation(S_sb[:, 0:W], psS[:, 0:W], AF.Copy, scale=0.125)
            em.bar()
            off = 0
            for (koff, kw, vb, msk) in parts:
                if msk is not None:
                    nc.vector.tensor_tensor(S_sb[:, off:off + kw],
                                            S_sb[:, off:off + kw], msk[:], ALU.add)
                off += kw
            em.bar()
            nc.vector.reduce_max(negm[:], S_sb[:, 0:W], axis=AX.X, negate=True)
            em.bar()
            nc.scalar.activation(P_sb[:, 0:W], S_sb[:, 0:W], AF.Exp,
                                 bias=negm[:, 0:1], accum_out=lcol[:, 0:1])
            em.bar()
            nc.vector.reciprocal(lcol[:], lcol[:])
            em.bar()
            off = 0
            for (koff, kw, vb, msk) in parts:
                nc.tensor.transpose(pstg[:, off:off + kw] if kw == 128
                                    else pstg[0:kw, 0:128],
                                    P_sb[:, off:off + kw], ident[:])
                em.bar()
                if kw == 128:
                    nc.vector.tensor_copy(ptb[:, off:off + 128], pstg[:, off:off + 128])
                else:
                    nc.vector.tensor_copy(ptb[0:kw, 320:448], pstg[0:kw, 0:128])
                em.bar()
                off += kw
            off = 0
            first = True
            for (koff, kw, vb, msk) in parts:
                if kw == 128:
                    nc.tensor.matmul(psO[:, 0:64], ptb[:, off:off + 128],
                                     vsb[:, vb * 64:(vb + 1) * 64],
                                     start=first, stop=(off + kw == W),
                                     skip_group_check=True)
                else:
                    nc.tensor.matmul(psO[:, 0:64], ptb[0:64, 320:448],
                                     vsb[0:64, 0:64],
                                     start=first, stop=(off + kw == W),
                                     skip_group_check=True)
                first = False
                off += kw
            em.bar()
            nc.vector.tensor_scalar(osb[:], psO[:, 0:64], lcol[:, 0:1], None,
                                    ALU.mult)
            em.bar()
            em.dma(ho[p][qc * 128:(qc + 1) * 128, :], osb[:])
            em.flush()
    em.close()
    return em


def build_PROJRES(scale_out=1.0, em=None, io=None):
    standalone = em is None
    if standalone:
        em = Emit("projres").start()
        a_in = em.din("afm", [D, TOKC], BF16)
        x_in = em.din("x", [TOKC, D], F32)
        x_out = em.dout("xo", [TOKC, D], F32)
        io = {"a_src": lambda kd: a_in[kd * 128:(kd + 1) * 128, :],
              "w": em.din("w", [D, D], F32),
              "x_tile": lambda r: x_in[r:r + 128, :],
              "xo_tile": lambda r: x_out[r:r + 128, :]}
    nc = em.nc
    w_in = io["w"]
    em.push_scope()

    w = em.sb([128, 8 * D], BF16, "w")
    afm = em.sb([128, 8 * 2048], BF16, "afm")
    xg = em.sb([128, D], F32, "xg")
    tmpf = em.sb([128, D], F32, "tmpf")
    psO = em.ps([128, D], F32, "po")

    for k in range(8):
        em.dma(w[:, k * D:(k + 1) * D], w_in[k * 128:(k + 1) * 128, :], nc.gpsimd)
        em.dma(afm[:, k * 2048:(k + 1) * 2048], io["a_src"](k), nc.sync)
    em.flush(nc.gpsimd)
    em.flush()
    for t in range(TOKC // 128):
        em.dma(xg[:], io["x_tile"](t * 128))
        for kd in range(8):
            for nn in range(2):
                nc.tensor.matmul(psO[:, nn * 512:(nn + 1) * 512],
                                 afm[:, kd * 2048 + t * 128:kd * 2048 + (t + 1) * 128],
                                 w[:, kd * D + nn * 512:kd * D + (nn + 1) * 512],
                                 start=(kd == 0), stop=(kd == 7))
        em.flush()
        nc.vector.scalar_tensor_tensor(tmpf[:], psO[:, 0:D], scale_out, xg[:],
                                       ALU.mult, ALU.add)
        em.bar()
        em.dma(io["xo_tile"](t * 128), tmpf[:])
        em.flush()
    em.pop_scope()
    if standalone:
        em.close()
    return em


def run_L3A(x, w_qkv, w_out):
    xf = x.reshape(T, D)
    ident = np.eye(128, dtype=np.float32).astype(BFNP)
    em1 = _cached("qkv_plain", lambda: build_QKV(affine=False, name="qkvpl"))
    in_maps = [{"x": xf[c * TOKC:(c + 1) * TOKC], "wqkv": w_qkv, "ident": ident}
               for c in range(NCORES)]
    res = _run(em1.nc, in_maps)
    qkT = np.concatenate([res.results[c]["qkT"] for c in range(NCORES)], 1)
    v = np.concatenate([res.results[c]["v"] for c in range(NCORES)], 0)
    qkT = qkT.reshape(2048, B, S)
    v = v.reshape(B, S, 1024)

    em2 = _cached("attnw", build_ATTN_WINDOW)
    NEG = -1e30
    i = np.arange(128)[:, None]
    j2 = np.arange(128)[None, :]
    mdiag = np.where(j2 <= i, 0.0, NEG).astype(np.float32)
    mband = np.where(j2 >= i, 0.0, NEG).astype(np.float32)
    mc1 = np.where((j2 >= i) | (j2 < 64), 0.0, NEG).astype(np.float32)
    in_maps = []
    for c in range(NCORES):
        bb, h0 = c // 4, (c % 4) * 4
        qf = np.stack([np.ascontiguousarray(qkT[(h0 + k) * 64:(h0 + k + 1) * 64, bb])
                       for k in range(4)])
        kf = np.stack([np.ascontiguousarray(
            qkT[1024 + (h0 + k) * 64:1024 + (h0 + k + 1) * 64, bb])
            for k in range(4)])
        v4 = np.ascontiguousarray(v[bb, :, h0 * 64:(h0 + 4) * 64])
        in_maps.append({"qf": qf, "kf": kf, "v4": v4, "mdiag": mdiag,
                        "mband": mband, "mc1": mc1, "ident": ident})
    res = _run(em2.nc, in_maps)
    headsT = np.zeros((B, 1024, S), dtype=BFNP)
    for c in range(NCORES):
        bb, h0 = c // 4, (c % 4) * 4
        headsT[bb, h0 * 64:(h0 + 4) * 64, :] = res.results[c]["hT"]

    em3 = _cached("projres1", lambda: build_PROJRES(1.0))
    in_maps = []
    for c in range(NCORES):
        bb = (c * TOKC) // S
        scol = slice((c * TOKC) % S, (c * TOKC) % S + TOKC)
        afm = np.ascontiguousarray(headsT[bb][:, scol])
        in_maps.append({"afm": afm, "w": w_out, "x": xf[c * TOKC:(c + 1) * TOKC]})
    res = _run(em3.nc, in_maps)
    out = np.concatenate([res.results[c]["xo"] for c in range(NCORES)], 0)
    return out.reshape(B, S, D)




def build_L1F_pre(em=None, io=None):
    ES = 256
    standalone = em is None
    if standalone:
        em = Emit("l1fp").start()
        x_in = em.din("x", [TOKC, D], F32)
        io = {"x_tile": lambda r: x_in[r:r + 128, :],
              "lnrow": em.din("lnrow", [1, 2 * D], F32),
              "muw": em.din("muw", [D, ES], F32),
              "scl": em.din("scl", [1, 1], F32),
              "ident": em.din("ident", [128, 128], BF16),
              "identf": em.din("identf", [128, 128], F32),
              "xn_out": em.dout("xn", [TOKC, D], F32).__getitem__,
              "lgT_out": em.dout("lgT", [ES, TOKC], F32).__getitem__,
              "cbT_out": em.dout("cbT", [ES, TOKC], F32).__getitem__}
    nc = em.nc
    lnrow_in, mu_in = io["lnrow"], io["muw"]
    scale_in, ident_in = io["scl"], io["ident"]
    em.push_scope()

    muf = em.sb([128, 8 * ES], F32, "muf")
    mub = em.sb([128, 8 * ES], BF16, "mub")
    rb = em.sb([128, ES], F32, "rb")
    stage = em.sb([1, D], F32, "stage")
    srow = em.sb([1, ES], F32, "srow")
    gmb = em.sb([128, D], F32, "gmb")
    btb = em.sb([128, D], F32, "btb")
    ident = em.sb([128, 128], BF16, "id")
    identf = em.sb([128, 128], F32, "idf")
    onesf = em.ones_row(128, F32)
    onescol = em.sb([128, 1], F32, "oc")
    xg = em.sb([128, D], F32, "xg")
    xn = em.sb([128, D], F32, "xn")
    tmpf = em.sb([128, D], F32, "tmpf")
    tmpb = em.sb([128, D], BF16, "tmpb")
    xuT = em.sb([128, 8 * 128], BF16, "xuT")
    lg = em.sb([128, ES], F32, "lg")
    cb = em.sb([128, ES], F32, "cb")
    oT = em.sb([128, 128], F32, "oT")
    psF = em.ps([128, 128], F32, "pf")
    stats = em.sb([128, 16], F32, "st")
    mu = em.sb([128, 1], F32, "mu")
    rstd = em.sb([128, 1], F32, "rs")
    eps = em.const_col(LN_EPS)
    scol = em.sb([128, 1], F32, "scol")
    ncol = em.sb([128, 1], F32, "ncol")
    acc2 = em.sb([128, 1], F32, "acc2")
    pst = em.ps([128, 256], BF16, "pt")
    psQ = em.ps([128, 512], F32, "pq")
    psL = em.ps([128, ES], F32, "pl")

    nc.vector.memset(onescol[:], 1.0)
    for k in range(8):
        em.dma(muf[:, k * ES:(k + 1) * ES], mu_in[k * 128:(k + 1) * 128, :])
    em.dma(ident[:], ident_in)
    em.dma(identf[:], io["identf"])
    em.flush()
    for i, dst in enumerate([gmb, btb]):
        em.dma(stage[:], lnrow_in[0:1, i * D:(i + 1) * D])
        em.flush()
        _bcast_row(em, dst, stage[0:1, :], onesf, psQ)
    em.dma(stage[0:1, 0:1], scale_in)
    em.flush()
    nc.tensor.matmul(psQ[:, 0:1], onesf[:, 0:128], stage[0:1, 0:1],
                     start=True, stop=True)
    em.bar()
    nc.vector.tensor_copy(scol[:], psQ[:, 0:1])
    em.bar()
    for k in range(8):
        nc.scalar.activation(rb[:, 0:ES], muf[:, k * ES:(k + 1) * ES], AF.Square)
        em.bar()
        nc.tensor.matmul(psL[0:1, 0:ES], onescol[:, 0:1], rb[:, 0:ES],
                         start=(k == 0), stop=(k == 7))
        em.bar()
    nc.scalar.activation(srow[:], psL[0:1, 0:ES], AF.Sqrt)
    em.bar()
    nc.vector.tensor_scalar(srow[:], srow[:], 1e-12, None, ALU.max)
    em.bar()
    nc.vector.reciprocal(srow[:], srow[:])
    em.bar()
    nc.tensor.matmul(psL[:, 0:ES], onesf[:, 0:128], srow[0:1, 0:ES],
                     start=True, stop=True)
    em.bar()
    nc.vector.tensor_copy(rb[:], psL[:, 0:ES])
    em.bar()
    for k in range(8):
        nc.vector.tensor_tensor(mub[:, k * ES:(k + 1) * ES],
                                muf[:, k * ES:(k + 1) * ES], rb[:], ALU.mult)
        em.bar()

    for t in range(TOKC // 128):
        em.dma(xg[:], io["x_tile"](t * 128))
        em.flush()
        em.ln_cols(xg[:], mu, rstd, stats, eps)
        em.bar()
        nc.vector.tensor_scalar(xn[:], xg[:], mu[:, 0:1], rstd[:, 0:1],
                                ALU.subtract, ALU.mult)
        em.bar()
        _ln_affine(em, xn[:], xn[:], gmb, btb, mu, rstd, stats, eps, tmpf)
        em.dma(io["xn_out"]((slice(t * 128, (t + 1) * 128), slice(None))), xn[:])
        nc.scalar.activation(tmpf[:], xn[:], AF.Square)
        em.bar()
        nc.vector.reduce_sum(ncol[:], tmpf[:], axis=AX.X)
        em.bar()
        nc.scalar.activation(ncol[:], ncol[:], AF.Sqrt)
        em.bar()
        nc.vector.tensor_scalar(ncol[:], ncol[:], 1e-12, None, ALU.max)
        em.bar()
        nc.vector.reciprocal(ncol[:], ncol[:])
        em.bar()
        nc.vector.tensor_scalar(tmpf[:], xn[:], ncol[:, 0:1], None, ALU.mult)
        em.bar()
        _transpose_to(em, xuT, tmpf[:], tmpb, ident, pst, 0, 128)
        for kd in range(8):
            nc.tensor.matmul(psL[:, 0:ES], xuT[:, kd * 128:(kd + 1) * 128],
                             mub[:, kd * ES:(kd + 1) * ES],
                             start=(kd == 0), stop=(kd == 7))
        em.bar()
        nc.vector.tensor_scalar(lg[:], psL[:, 0:ES], scol[:, 0:1], None, ALU.mult)
        em.bar()
        for cch in range(2):
            nc.tensor.transpose(psF[:, 0:128], lg[:, cch * 128:(cch + 1) * 128],
                                identf[:])
            em.bar()
            nc.vector.tensor_copy(oT[:], psF[:, 0:128])
            em.bar()
            em.dma(io["lgT_out"]((slice(cch * 128, (cch + 1) * 128),
                                  slice(t * 128, (t + 1) * 128))), oT[:])
        nc.vector.reduce_max(ncol[:], lg[:], axis=AX.X, negate=True)
        em.bar()
        nc.scalar.activation(cb[:], lg[:], AF.Exp, bias=ncol[:, 0:1],
                             accum_out=acc2[:, 0:1])
        em.bar()
        nc.vector.reciprocal(ncol[:], acc2[:])
        em.bar()
        nc.vector.tensor_scalar(cb[:], cb[:], ncol[:, 0:1], None, ALU.mult)
        em.bar()
        for cch in range(2):
            nc.tensor.transpose(psF[:, 0:128], cb[:, cch * 128:(cch + 1) * 128],
                                identf[:])
            em.bar()
            nc.vector.tensor_copy(oT[:], psF[:, 0:128])
            em.bar()
            em.dma(io["cbT_out"]((slice(cch * 128, (cch + 1) * 128),
                                  slice(t * 128, (t + 1) * 128))), oT[:])
        em.flush()
    em.pop_scope()
    if standalone:
        em.close()
    return em


def build_L1F_slots(em=None, io=None):
    NS, SQ = 64, S
    standalone = em is None
    if standalone:
        em = Emit("l1fs").start()
        lg_in = em.din("lgsm", [NS, SQ], F32)
        xn0_in = em.din("xn0", [SQ, D], F32)
        xn1_in = em.din("xn1", [SQ, D], F32)
        io = {"lgs_src": lg_in,
              "xn_tile": lambda b, r: (xn0_in if b == 0 else xn1_in)[r:r + 128, :],
              "w1": em.din("w1", [2, D, 256], F32),
              "w2": em.din("w2", [2, 256, D], F32),
              "brow": em.din("brow", [1, 2 * 256 + 2 * D], F32),
              "ident": em.din("ident", [128, 128], BF16),
              "so_out": em.dout("so", [NS, D], F32)}
    nc = em.nc
    w1_in, w2_in = io["w1"], io["w2"]
    brow_in, ident_in = io["brow"], io["ident"]
    so_out = io["so_out"]
    em.push_scope()

    lgs = em.sb([64, SQ], F32, "lgs")
    E = em.sb([64, SQ], BF16, "E")
    etb = em.sb([128, 64], BF16, "etb")
    xnb = em.sb([128, D], BF16, "xnb")
    w1b = em.sb([128, 8 * 512], BF16, "w1b")
    w2b = em.sb([128, 2 * 2 * D], BF16, "w2b")
    brow = em.sb([1, 2 * 256 + 2 * D], F32, "brow")
    browb = em.sb([1, 2 * 256 + 2 * D], BF16, "browb")
    onesb = em.sb([1, 128], BF16, "onesb")
    ident = em.sb([128, 128], BF16, "id")
    slot_in = em.sb([64, D], F32, "si")
    siTb = em.sb([128, 8 * 64], BF16, "siT")
    h16 = em.sb([16, 256], F32, "h16")
    h16b = em.sb([16, 256], BF16, "h16b")
    hTb = em.sb([128, 2 * 16], BF16, "hTb")
    o16 = em.sb([16, D], F32, "o16")
    negm = em.sb([64, 1], F32, "nm")
    rsum = em.sb([64, 1], F32, "rsum")
    pst = em.ps([128, 256], BF16, "pt")
    psS = em.ps([32, D], F32, "psS")
    psH = em.ps([16, 512], F32, "psH")
    psO2 = em.ps([16, D], F32, "psO2")

    nc.vector.memset(onesb[:], 1.0)
    lgs_dst = (lgs[:].rearrange(io["lgs_rearr"], **io.get("lgs_kw", {}))
               if "lgs_rearr" in io else lgs[:])
    em.dma(lgs_dst, io["lgs_src"])
    em.dma(brow[:], brow_in)
    em.dma(browb[:], brow_in, nc.gpsimd)
    em.dma(ident[:], ident_in)
    for e in range(2):
        em.dma(w1b[:, :].rearrange("p (dc c) -> p dc c", dc=8)[:, :, e * 256:(e + 1) * 256],
               w1_in[e].rearrange("(dc p) c -> p dc c", p=128), nc.gpsimd)
        em.dma(w2b[:, :].rearrange("p (hc c) -> p hc c", hc=2)
               [:, :, e * D:(e + 1) * D],
               w2_in[e].rearrange("(hc p) c -> p hc c", p=128), nc.gpsimd)
    em.flush(nc.gpsimd)
    em.flush()
    nc.vector.reduce_max(negm[:], lgs[:], axis=AX.X, negate=True)
    em.bar()
    nc.scalar.activation(E[:], lgs[:], AF.Exp, bias=negm[:, 0:1],
                         accum_out=rsum[:, 0:1])
    em.bar()
    nc.vector.reciprocal(rsum[:], rsum[:])
    em.bar()
    for b in range(2):
        for tt in range(SQ // 128):
            em.dma(xnb[:], io["xn_tile"](b, tt * 128), nc.gpsimd)
            em.flush(nc.gpsimd)
            nc.tensor.transpose(pst[:, 0:32],
                                E[b * 32:(b + 1) * 32, tt * 128:(tt + 1) * 128],
                                ident[b * 32:(b + 1) * 32, b * 32:(b + 1) * 32])
            em.bar()
            nc.vector.tensor_copy(etb[:, 0:32], pst[:, 0:32])
            em.bar()
            for nn in range(2):
                nc.tensor.matmul(psS[:, nn * 512:(nn + 1) * 512], etb[:, 0:32],
                                 xnb[:, nn * 512:(nn + 1) * 512],
                                 start=(tt == 0), stop=(tt == SQ // 128 - 1),
                                 skip_group_check=True)
            em.bar()
        nc.vector.tensor_scalar(slot_in[b * 32:(b + 1) * 32, :], psS[0:32, 0:D],
                                rsum[b * 32:(b + 1) * 32, 0:1], None, ALU.mult)
        em.bar()
    sib = em.sb([64, D], BF16, "sib")
    nc.vector.tensor_copy(sib[:], slot_in[:])
    em.bar()
    for kd in range(8):
        nc.tensor.transpose(pst[:, 0:64], sib[:, kd * 128:(kd + 1) * 128],
                            ident[0:64, 0:64])
        em.bar()
        nc.vector.tensor_copy(siTb[:, kd * 64:(kd + 1) * 64], pst[:, 0:64])
        em.bar()
    for g in range(4):
        b, eL = g // 2, g % 2
        for kd in range(8):
            nc.tensor.matmul(psH[:, 0:256],
                             siTb[:, kd * 64 + g * 16:kd * 64 + (g + 1) * 16],
                             w1b[:, kd * 512 + eL * 256:kd * 512 + (eL + 1) * 256],
                             start=(kd == 0), stop=False)
        nc.tensor.matmul(psH[:, 0:256], onesb[:, 0:16],
                         browb[0:1, eL * 256:(eL + 1) * 256],
                         start=False, stop=True)
        em.bar()
        nc.scalar.activation(h16[:], psH[0:16, 0:256], AF.Gelu)
        em.bar()
        nc.vector.tensor_copy(h16b[:], h16[:])
        em.bar()
        for hc in range(2):
            nc.tensor.transpose(pst[:, 0:16], h16b[:, hc * 128:(hc + 1) * 128],
                                ident[0:16, 0:16])
            em.bar()
            nc.vector.tensor_copy(hTb[:, hc * 16:(hc + 1) * 16], pst[:, 0:16])
            em.bar()
        for hc in range(2):
            nc.tensor.matmul(psO2[:, 0:512],
                             hTb[:, hc * 16:(hc + 1) * 16],
                             w2b[:, hc * 2 * D + eL * D:hc * 2 * D + eL * D + 512],
                             start=(hc == 0), stop=False)
            nc.tensor.matmul(psO2[:, 512:D],
                             hTb[:, hc * 16:(hc + 1) * 16],
                             w2b[:, hc * 2 * D + eL * D + 512:hc * 2 * D + (eL + 1) * D],
                             start=(hc == 0), stop=False)
        for nn in range(2):
            nc.tensor.matmul(psO2[:, nn * 512:(nn + 1) * 512], onesb[:, 0:16],
                             browb[0:1, 512 + eL * D + nn * 512:
                                   512 + eL * D + (nn + 1) * 512],
                             start=False, stop=True)
        em.bar()
        nc.vector.tensor_copy(o16[:], psO2[0:16, 0:D])
        em.bar()
        em.dma(so_out[g * 16:(g + 1) * 16, :], o16[:])
        em.flush()
    em.pop_scope()
    if standalone:
        em.close()
    return em


def build_L1F_out(em=None, io=None):
    ES = 256
    standalone = em is None
    if standalone:
        em = Emit("l1fo").start()
        cT_in = em.din("combT", [ES, TOKC], F32)
        so_in = em.din("sob", [ES, D], F32)
        x_in = em.din("x", [TOKC, D], F32)
        x_out = em.dout("xo", [TOKC, D], F32)
        io = {"cT_src": lambda c: cT_in[c * 128:(c + 1) * 128, :],
              "so_src": lambda c: so_in[c * 128:(c + 1) * 128, :],
              "x_tile": lambda r: x_in[r:r + 128, :],
              "xo_tile": lambda r: x_out[r:r + 128, :]}
    nc = em.nc
    em.push_scope()

    cTb = em.sb([128, 2 * TOKC], BF16, "cTb")
    sob = em.sb([128, 2 * D], BF16, "sob")
    xg = em.sb([128, D], F32, "xg")
    tmpf = em.sb([128, D], F32, "tmpf")
    psO = em.ps([128, D], F32, "po")

    for c in range(2):
        em.dma(cTb[:, c * TOKC:(c + 1) * TOKC], io["cT_src"](c), nc.gpsimd)
        em.dma(sob[:, c * D:(c + 1) * D], io["so_src"](c), nc.gpsimd)
    em.flush(nc.gpsimd)
    em.flush()
    for t in range(TOKC // 128):
        em.dma(xg[:], io["x_tile"](t * 128))
        for c in range(2):
            for nn in range(2):
                nc.tensor.matmul(psO[:, nn * 512:(nn + 1) * 512],
                                 cTb[:, c * TOKC + t * 128:c * TOKC + (t + 1) * 128],
                                 sob[:, c * D + nn * 512:c * D + (nn + 1) * 512],
                                 start=(c == 0), stop=(c == 1))
        em.flush()
        nc.vector.scalar_tensor_tensor(tmpf[:], psO[:, 0:D], 0.5, xg[:],
                                       ALU.mult, ALU.add)
        em.bar()
        em.dma(io["xo_tile"](t * 128), tmpf[:])
        em.flush()
    em.pop_scope()
    if standalone:
        em.close()
    return em


def run_L1F(x, ln_gamma, ln_beta, muw, scale, w1, b1, w2, b2):
    em1 = _cached("build_L1F_pre", build_L1F_pre)
    xf = x.reshape(T, D)
    ident = np.eye(128, dtype=np.float32).astype(BFNP)
    identf = np.eye(128, dtype=np.float32)
    lnrow = np.concatenate([ln_gamma, ln_beta]).astype(np.float32)[None, :]
    muw2 = np.ascontiguousarray(muw.reshape(D, 256))
    common = {"lnrow": lnrow, "muw": muw2, "identf": identf,
              "scl": np.asarray(scale, np.float32).reshape(1, 1), "ident": ident}
    in_maps = [{"x": xf[c * TOKC:(c + 1) * TOKC], **common} for c in range(NCORES)]
    res = _run(em1.nc, in_maps)
    xn = np.concatenate([res.results[c]["xn"] for c in range(NCORES)], 0)
    lgT = np.concatenate([res.results[c]["lgT"] for c in range(NCORES)], 1)
    cbT = np.concatenate([res.results[c]["cbT"] for c in range(NCORES)], 1)

    em2 = _cached("build_L1F_slots", build_L1F_slots)
    xnr = xn.reshape(B, S, D)
    in_maps = []
    for c in range(NCORES):
        eids = [2 * c, 2 * c + 1]
        rows = np.concatenate([
            np.ascontiguousarray(lgT[e * 16:(e + 1) * 16, bb * S:(bb + 1) * S])
            for bb in range(B) for e in eids], 0)
        b1s = np.concatenate([b1[e] for e in eids])
        b2s = np.concatenate([b2[e] for e in eids])
        brow = np.concatenate([b1s, b2s]).astype(np.float32)[None, :]
        in_maps.append({"lgsm": rows, "xn0": xnr[0], "xn1": xnr[1],
                        "w1": w1[eids], "w2": w2[eids], "brow": brow,
                        "ident": ident})
    res = _run(em2.nc, in_maps)
    so = np.zeros((B, 256, D), np.float32)
    for c in range(NCORES):
        r = res.results[c]["so"]
        for bb in range(B):
            for i, e in enumerate([2 * c, 2 * c + 1]):
                so[bb, e * 16:(e + 1) * 16] = r[bb * 32 + i * 16: bb * 32 + (i + 1) * 16]

    em3 = _cached("build_L1F_out", build_L1F_out)
    in_maps = []
    for c in range(NCORES):
        bb = (c * TOKC) // S
        in_maps.append({"combT": cbT[:, c * TOKC:(c + 1) * TOKC],
                        "sob": so[bb], "x": xf[c * TOKC:(c + 1) * TOKC]})
    res = _run(em3.nc, in_maps)
    out = np.concatenate([res.results[c]["xo"] for c in range(NCORES)], 0)
    return out.reshape(B, S, D)



def build_L1A_ln(em=None, io=None):
    standalone = em is None
    if standalone:
        em = Emit("l1ln").start()
        x_in = em.din("x", [TOKC, D], F32)
        xnT_o = em.dout("xnT", [D, TOKC], F32)
        io = {"x_tile": lambda r: x_in[r:r + 128, :],
              "identf": em.din("identf", [128, 128], F32),
              "xnT_out": lambda kd, r: xnT_o[kd * 128:(kd + 1) * 128, r:r + 128]}
    nc = em.nc
    em.push_scope()
    xg = em.sb([128, D], F32, "xg")
    xn = em.sb([128, D], F32, "xn")
    identf = em.sb([128, 128], F32, "idf")
    oT = em.sb([128, 128], F32, "oT")
    stats = em.sb([128, 16], F32, "st")
    mu = em.sb([128, 1], F32, "mu")
    rstd = em.sb([128, 1], F32, "rs")
    eps = em.const_col(LN_EPS)
    psF = em.ps([128, 128], F32, "pf")
    em.dma(identf[:], io["identf"])
    em.flush()
    for t in range(TOKC // 128):
        em.dma(xg[:], io["x_tile"](t * 128))
        em.flush()
        em.ln_cols(xg[:], mu, rstd, stats, eps)
        em.bar()
        nc.vector.tensor_scalar(xn[:], xg[:], mu[:, 0:1], rstd[:, 0:1],
                                ALU.subtract, ALU.mult)
        em.bar()
        for kd in range(8):
            nc.tensor.transpose(psF[:, 0:128], xn[:, kd * 128:(kd + 1) * 128],
                                identf[:])
            em.bar()
            nc.vector.tensor_copy(oT[:], psF[:, 0:128])
            em.bar()
            em.dma(io["xnT_out"](kd, t * 128), oT[:])
            em.flush()
    em.pop_scope()
    if standalone:
        em.close()
    return em


def build_L1A_conv(em=None, io=None):
    standalone = em is None
    if standalone:
        em = Emit("l1cv").start()
        xnf_in = em.din("xnf", [128, B * S], F32)
        xc_o = em.dout("xc", [128, B * S], F32)
        io = {"xn_src": lambda b: xnf_in[:, b * S:(b + 1) * S]
              .rearrange("p (r t) -> p r t", r=1),
              "NRC": 1,
              "cw": em.din("cw", [128, 4], F32),
              "cbr": em.din("cb", [128, 1], F32),
              "xc_out": xc_o}
    nc = em.nc
    SP = S + 3
    em.push_scope()
    xnp = em.sb([128, B * SP], F32, "xnp")
    cw = em.sb([128, 4], F32, "cw")
    cbc = em.sb([128, 1], F32, "cb")
    acc = em.sb([128, S], F32, "acc")
    nc.vector.memset(xnp[:, 0:SP], 0.0)
    nc.vector.memset(xnp[:, SP:2 * SP], 0.0)
    em.bar()
    NRC = io["NRC"]
    for b in range(B):
        em.dma(xnp[:, b * SP + 3:(b + 1) * SP]
               .rearrange("p (r t) -> p r t", r=NRC), io["xn_src"](b))
    em.dma(cw[:], io["cw"])
    em.dma(cbc[:], io["cbr"])
    em.flush()
    for b in range(B):
        base = b * SP
        nc.vector.tensor_scalar(acc[:], xnp[:, base:base + S], cw[:, 0:1],
                                None, ALU.mult)
        em.bar()
        for k in range(1, 4):
            nc.vector.scalar_tensor_tensor(acc[:], xnp[:, base + k:base + k + S],
                                           cw[:, k:k + 1], acc[:],
                                           ALU.mult, ALU.add)
            em.bar()
        nc.vector.tensor_scalar(acc[:], acc[:], cbc[:, 0:1], None, ALU.add)
        em.bar()
        nc.scalar.activation(acc[:], acc[:], AF.Silu)
        em.bar()
        em.dma(io["xc_out"][:, b * S:(b + 1) * S], acc[:])
        em.flush()
    em.pop_scope()
    if standalone:
        em.close()
    return em


def build_L1A_proj(em=None, io=None):
    NP33 = 33
    standalone = em is None
    if standalone:
        em = Emit("l1pj").start()
        xcT_in = em.din("xcT", [D, TOKC], F32)
        pjT_o = em.dout("pjT", [NP33, TOKC], F32)
        io = {"xc_src": lambda kd: xcT_in[kd * 128:(kd + 1) * 128, :],
              "xpw": em.din("xpw", [D, NP33], F32),
              "identf": em.din("identf", [128, 128], F32),
              "pjT_out": lambda r: pjT_o[:, r:r + 128]}
    nc = em.nc
    em.push_scope()
    xcb = em.sb([128, 8 * TOKC], BF16, "xcb")
    wb = em.sb([128, 8 * NP33], BF16, "wb")
    ob = em.sb([128, NP33], F32, "ob")
    identf = em.sb([128, 128], F32, "idf")
    oT = em.sb([33, 128], F32, "oT")
    psP = em.ps([128, NP33], F32, "pp")
    psF = em.ps([33, 128], F32, "pf")
    for k in range(8):
        em.dma(xcb[:, k * TOKC:(k + 1) * TOKC], io["xc_src"](k), nc.gpsimd)
        em.dma(wb[:, k * NP33:(k + 1) * NP33], io["xpw"][k * 128:(k + 1) * 128, :],
               nc.gpsimd)
    em.dma(identf[:], io["identf"])
    em.flush(nc.gpsimd)
    em.flush()
    for t in range(TOKC // 128):
        for kd in range(8):
            nc.tensor.matmul(psP[:, 0:NP33],
                             xcb[:, kd * TOKC + t * 128:kd * TOKC + (t + 1) * 128],
                             wb[:, kd * NP33:(kd + 1) * NP33],
                             start=(kd == 0), stop=(kd == 7))
        em.bar()
        nc.vector.tensor_copy(ob[:], psP[:, 0:NP33])
        em.bar()
        nc.tensor.transpose(psF[:, 0:128], ob[:], identf[:])
        em.bar()
        nc.vector.tensor_copy(oT[:], psF[:, 0:128])
        em.bar()
        em.dma(io["pjT_out"](t * 128), oT[:])
        em.flush()
    em.pop_scope()
    if standalone:
        em.close()
    return em


def build_L1A_scan(em=None, io=None):
    NCH = S // 512
    DST = 16
    standalone = em is None
    if standalone:
        em = Emit("l1sc").start()
        xc_in = em.din("xcf", [128, B * S], F32)
        xn_in = em.din("xnf", [128, B * S], F32)
        prow_in0 = em.din("prow", [1, B * S], F32)
        ball_in = em.din("ball", [B * DST, S], F32)
        call_in = em.din("call", [B * DST, S], F32)
        io = {"xc": xc_in,
              "xn_tile_fm": lambda b, ch: xn_in[:, b * S + ch * 512:
                                                b * S + (ch + 1) * 512],
              "prow_src": lambda b: prow_in0[0:1, b * S:(b + 1) * S]
              .rearrange("o (r t) -> o r t", r=1),
              "brow_src": lambda b, n: ball_in[b * DST + n:b * DST + n + 1, :]
              .rearrange("o (r t) -> o r t", r=1),
              "crow_src": lambda b, n: call_in[b * DST + n:b * DST + n + 1, :]
              .rearrange("o (r t) -> o r t", r=1),
              "NRR": 1,
              "dtwr": em.din("dtwr", [1, 128], F32),
              "dtbc": em.din("dtbc", [128, 1], F32),
              "alog": em.din("alog", [128, DST], F32),
              "dpc": em.din("dpc", [128, 1], F32),
              "scanT_out": em.dout("xo", [128, B * S], F32)}
    nc = em.nc
    xc_in = io["xc"]
    dtw_in, dtb_in = io["dtwr"], io["dtbc"]
    alog_in, dp_in = io["alog"], io["dpc"]
    xo_out = io["scanT_out"]
    NRR = io["NRR"]
    em.push_scope()

    dx = em.sb([128, S], F32, "dx")
    delta = em.sb([128, S], F32, "delta")
    yb = em.sb([128, S], F32, "yb")
    rowA = em.sb([1, S], F32, "rowA")
    rowB = em.sb([1, S], F32, "rowB")
    dtw = em.sb([1, 128], F32, "dtw")
    dtb = em.sb([128, 1], F32, "dtb")
    acols = em.sb([128, DST], F32, "ac")
    dpc = em.sb([128, 1], F32, "dp")
    onec = em.const_col(1.0)
    onesf = em.ones_row(128, F32)
    abar = em.sb([128, 512], F32, "abar")
    usb = em.sb([128, 512], F32, "usb")
    hb = [em.sb([128, 512], F32, "hb") for _ in range(2)]
    t1 = em.sb([128, 512], F32, "t1")
    t2 = em.sb([128, 512], F32, "t2")
    xchunk = em.sb([128, 512], F32, "xck")
    psB = em.ps([128, 512], F32, "pb")
    psC = em.ps([128, 512], F32, "pc")
    psD = em.ps([128, 512], F32, "pd")
    sv = em.sem("sc_sv")
    tv = em.sem("sc_tv")
    vt = em.sem("sc_vt")
    vs = em.sem("sc_vs")
    cnt = {"i": 0}

    em.dma(dtw[:], dtw_in)
    em.dma(dtb[:], dtb_in)
    em.dma(acols[:], alog_in)
    em.dma(dpc[:], dp_in)
    em.flush()
    nc.scalar.activation(acols[:], acols[:], AF.Exp)
    em.bar()
    nc.vector.tensor_scalar(acols[:], acols[:], -1.0, None, ALU.mult)
    em.bar()

    for b in range(B):
        em.dma(rowA[:].rearrange("o (r t) -> o r t", r=NRR), io["prow_src"](b))
        em.dma(dx[:], xc_in[:, b * S:(b + 1) * S])
        em.flush()
        for ch in range(NCH):
            sl = slice(ch * 512, (ch + 1) * 512)
            nc.tensor.matmul(psD[:, 0:512], dtw[0:1, :], rowA[0:1, sl],
                             start=True, stop=True)
            em.bar()
            nc.vector.tensor_scalar(t1[:], psD[:, 0:512], dtb[:, 0:1], 0.0,
                                    ALU.add, ALU.max)
            em.bar()
            nc.scalar.activation(t2[:], psD[:, 0:512], AF.Abs, bias=dtb[:, 0:1])
            em.bar()
            nc.scalar.activation(t2[:], t2[:], AF.Exp, scale=-1.0)
            em.bar()
            nc.scalar.activation(t2[:], t2[:], AF.Ln, bias=onec[:, 0:1])
            em.bar()
            nc.vector.tensor_tensor(delta[:, sl], t1[:], t2[:], ALU.add)
            em.bar()
        nc.vector.tensor_tensor(dx[:], delta[:], dx[:], ALU.mult)
        em.bar()
        for n in range(DST):
            em.dma(rowA[:].rearrange("o (r t) -> o r t", r=NRR),
                   io["brow_src"](b, n))
            em.dma(rowB[:].rearrange("o (r t) -> o r t", r=NRR),
                   io["crow_src"](b, n))
            em.flush()
            for ch in range(NCH):
                sl = slice(ch * 512, (ch + 1) * 512)
                i = cnt["i"]
                cnt["i"] += 1
                if i > 0:
                    nc.scalar.wait_ge(vs, i)
                nc.scalar.activation(abar[:], delta[:, sl], AF.Exp,
                                     scale=acols[:, n:n + 1]).then_inc(sv, 1)
                if i > 0:
                    nc.tensor.wait_ge(vt, 2 * i - 1)
                nc.tensor.matmul(psB[:, 0:512], onesf[:, 0:128],
                                 rowA[0:1, sl], start=True,
                                 stop=True).then_inc(tv, 1)
                if i > 0:
                    nc.tensor.wait_ge(vt, 2 * i)
                nc.tensor.matmul(psC[:, 0:512], onesf[:, 0:128],
                                 rowB[0:1, sl], start=True,
                                 stop=True).then_inc(tv, 1)
                nc.vector.wait_ge(tv, 2 * i + 1)
                nc.vector.tensor_tensor(usb[:], dx[:, sl], psB[:, 0:512],
                                        ALU.mult).then_inc(vt, 1)
                nc.vector.wait_ge(sv, i + 1)
                nc.vector.tensor_tensor_scan(
                    hb[ch % 2][:], abar[:], usb[:],
                    0.0 if ch == 0 else hb[(ch + 1) % 2][:, 511:512],
                    ALU.mult, ALU.add).then_inc(vs, 1)
                nc.vector.wait_ge(tv, 2 * i + 2)
                nc.vector.tensor_tensor(t1[:], hb[ch % 2][:], psC[:, 0:512],
                                        ALU.mult).then_inc(vt, 1)
                if n == 0:
                    nc.vector.tensor_copy(yb[:, sl], t1[:])
                else:
                    nc.vector.tensor_tensor(yb[:, sl], yb[:, sl], t1[:], ALU.add)
            em.bar()
        for ch in range(NCH):
            sl = slice(b * S + ch * 512, b * S + (ch + 1) * 512)
            sll = slice(ch * 512, (ch + 1) * 512)
            em.dma(xchunk[:], io["xn_tile_fm"](b, ch))
            em.flush()
            nc.vector.scalar_tensor_tensor(t1[:], xchunk[:], dpc[:, 0:1],
                                           yb[:, sll], ALU.mult, ALU.add)
            em.bar()
            em.dma(xo_out[:, sl], t1[:])
            em.flush()
    em.pop_scope()
    if standalone:
        em.close()
    return em


def run_L1A(x, conv_w, conv_b, xproj_w, dt_w, dt_b, a_log, Dp):
    xf = x.reshape(T, D)
    identf = np.eye(128, dtype=np.float32)
    em = _cached("build_L1A_ln", build_L1A_ln)
    in_maps = [{"x": xf[c * TOKC:(c + 1) * TOKC], "identf": identf}
               for c in range(NCORES)]
    res = _run(em.nc, in_maps)
    xnT = np.concatenate([res.results[c]["xnT"] for c in range(NCORES)], 1)
    em = _cached("build_L1A_conv", build_L1A_conv)
    cw = conv_w.reshape(D, 4)
    in_maps = [{"xnf": xnT[c * 128:(c + 1) * 128],
                "cw": cw[c * 128:(c + 1) * 128],
                "cb": conv_b[c * 128:(c + 1) * 128, None]}
               for c in range(NCORES)]
    res = _run(em.nc, in_maps)
    xc_fm = np.concatenate([res.results[c]["xc"] for c in range(NCORES)], 0)
    em = _cached("build_L1A_proj", build_L1A_proj)
    in_maps = [{"xcT": xc_fm[:, c * TOKC:(c + 1) * TOKC],
                "xpw": np.ascontiguousarray(xproj_w.T), "identf": identf}
               for c in range(NCORES)]
    res = _run(em.nc, in_maps)
    pjT = np.concatenate([res.results[c]["pjT"] for c in range(NCORES)], 1)
    em = _cached("build_L1A_scan", build_L1A_scan)
    ball = np.ascontiguousarray(pjT[0:16].reshape(16, B, S).transpose(1, 0, 2)
                                .reshape(B * 16, S))
    call = np.ascontiguousarray(pjT[16:32].reshape(16, B, S).transpose(1, 0, 2)
                                .reshape(B * 16, S))
    prow = np.ascontiguousarray(pjT[32:33])
    in_maps = []
    for c in range(NCORES):
        dsl = slice(c * 128, (c + 1) * 128)
        in_maps.append({
            "xcf": xc_fm[dsl], "xnf": xnT[dsl],
            "prow": prow, "ball": ball, "call": call,
            "dtwr": np.ascontiguousarray(dt_w[dsl, 0])[None, :],
            "dtbc": dt_b[dsl, None], "alog": a_log[dsl],
            "dpc": Dp[dsl, None]})
    res = _run(em.nc, in_maps)
    scanT = np.concatenate([res.results[c]["xo"] for c in range(NCORES)], 0)
    em = _cached("addt32", lambda: build_ADDT(src_dtype=F32, name="addt32"))
    in_maps = [{"x": xf[c * TOKC:(c + 1) * TOKC],
                "aT": np.ascontiguousarray(scanT[:, c * TOKC:(c + 1) * TOKC]),
                "identf": identf} for c in range(NCORES)]
    res = _run(em.nc, in_maps)
    out = np.concatenate([res.results[c]["xo"] for c in range(NCORES)], 0)
    return out.reshape(B, S, D)




_BUILD_CACHE = {}


def _cached(name, fn):
    if name not in _BUILD_CACHE:
        _BUILD_CACHE[name] = fn()
    return _BUILD_CACHE[name]


def kernel(**inp):
    return kernel_fused(99, **inp)


def kernel_unfused(**inp):
    x = np.asarray(inp["L0_attn_x"], np.float32)

    def L0F(x):
        return run_L0F(x, inp["L0_ffn_g1"], inp["L0_ffn_be1"], inp["L0_ffn_mix_w"],
                       inp["L0_ffn_mix_b"], inp["L0_ffn_g2"], inp["L0_ffn_be2"],
                       inp["L0_ffn_router_w"], inp["L0_ffn_w1"], inp["L0_ffn_b1"],
                       inp["L0_ffn_w2"], inp["L0_ffn_b2"])

    def L1F(x):
        return run_L1F(x, inp["L1_ffn_ln_gamma"], inp["L1_ffn_ln_beta"],
                       inp["L1_ffn_mu"], inp["L1_ffn_scale"], inp["L1_ffn_w1"],
                       inp["L1_ffn_b1"], inp["L1_ffn_w2"], inp["L1_ffn_b2"])

    def L2F(x):
        return run_L2F(x, inp["L2_ffn_shared_w1"], inp["L2_ffn_shared_w2"],
                       inp["L2_ffn_experts_w1"], inp["L2_ffn_experts_w2"],
                       inp["L2_ffn_gate_w"])

    def L3F(x):
        return run_L3F(x, inp["L3_ffn_w_gate"], inp["L3_ffn_w_up"],
                       inp["L3_ffn_w_down"])

    x = L0F(x)
    x = run_L0A(x, inp["L0_attn_gamma"], inp["L0_attn_beta"], inp["L0_attn_w_qkv"])
    x = L0F(x)
    x = L1F(x)
    x = run_L1A(x, inp["L1_attn_conv_w"], inp["L1_attn_conv_b"],
                inp["L1_attn_xproj_w"], inp["L1_attn_dt_w"], inp["L1_attn_dt_b"],
                inp["L1_attn_a_log"], inp["L1_attn_Dp"])
    x = L1F(x)
    x = L2F(x)
    x = run_L2A(x, inp["L2_attn_in_proj_weight"], inp["L2_attn_in_proj_bias"],
                inp["L2_attn_out_proj_weight"], inp["L2_attn_out_proj_bias"])
    x = L2F(x)
    x = L3F(x)
    x = run_L3A(x, inp["L3_attn_w_qkv"], inp["L3_attn_w_out"])
    x = L3F(x)
    return x



def build_QKV(em=None, io=None, affine=False, name="qkv"):
    standalone = em is None
    if standalone:
        em = Emit(name).start()
        x_in = em.din("x", [TOKC, D], F32)
        io = {"x_tile": lambda r: x_in[r:r + 128, :],
              "wqkv": em.din("wqkv", [D, 3072], F32),
              "ident": em.din("ident", [128, 128], BF16),
              "qkT_out": em.dout("qkT", [2048, TOKC], BF16),
              "v_out": em.dout("v", [TOKC, 1024], BF16)}
        if affine:
            io["gbrow"] = em.din("gbrow", [1, 2 * D], F32)
    nc = em.nc
    F3 = 3072
    em.push_scope()
    w = em.sb([128, 8 * F3], BF16, "w")
    ident = em.sb([128, 128], BF16, "id")
    xg = em.sb([128, D], F32, "xg")
    xn = em.sb([128, D], F32, "xn")
    tmpb = em.sb([128, D], BF16, "tmpb")
    xnT = em.sb([128, 8 * 512], BF16, "xnT")
    ob = em.sb([128, 512], BF16, "ob")
    stats = em.sb([128, 16], F32, "st")
    mu = em.sb([128, 1], F32, "mu")
    rstd = em.sb([128, 1], F32, "rs")
    eps = em.const_col(LN_EPS)
    pst = em.ps([128, 256], BF16, "pt")
    psQ = em.ps([128, 512], F32, "pq")
    if affine:
        onesf = em.ones_row(128, F32)
        stage = em.sb([1, D], F32, "stage")
        gmb = em.sb([128, D], F32, "gmb")
        btb = em.sb([128, D], F32, "btb")

    for k in range(8):
        em.dma(w[:, k * F3:(k + 1) * F3], io["wqkv"][k * 128:(k + 1) * 128, :],
               nc.gpsimd)
    em.dma(ident[:], io["ident"])
    em.flush(nc.gpsimd)
    em.flush()
    if affine:
        for i, dst in enumerate([gmb, btb]):
            em.dma(stage[:], io["gbrow"][0:1, i * D:(i + 1) * D])
            em.flush()
            _bcast_row(em, dst, stage[0:1, :], onesf, psQ)

    for g in range(TOKC // 512):
        for t in range(4):
            row = g * 512 + t * 128
            em.dma(xg[:], io["x_tile"](row))
            em.flush()
            em.ln_cols(xg[:], mu, rstd, stats, eps)
            em.bar()
            nc.vector.tensor_scalar(xn[:], xg[:], mu[:, 0:1], rstd[:, 0:1],
                                    ALU.subtract, ALU.mult)
            em.bar()
            if affine:
                _ln_affine(em, xn[:], xn[:], gmb, btb, mu, rstd, stats, eps, xg)
            _transpose_to(em, xnT, xn[:], tmpb, ident, pst, t * 128, 512)
        for fc in range(16):
            for kd in range(8):
                nc.tensor.matmul(psQ[:, 0:512],
                                 w[:, kd * F3 + fc * 128:kd * F3 + (fc + 1) * 128],
                                 xnT[:, kd * 512:(kd + 1) * 512],
                                 start=(kd == 0), stop=(kd == 7))
            em.bar()
            nc.vector.tensor_copy(ob[:], psQ[:, 0:512])
            em.bar()
            em.dma(io["qkT_out"][fc * 128:(fc + 1) * 128, g * 512:(g + 1) * 512],
                   ob[:])
            em.flush()
        for t in range(4):
            row = g * 512 + t * 128
            for nn in range(2):
                for kd in range(8):
                    nc.tensor.matmul(
                        psQ[:, 0:512],
                        xnT[:, kd * 512 + t * 128:kd * 512 + (t + 1) * 128],
                        w[:, kd * F3 + 2048 + nn * 512:kd * F3 + 2048 + (nn + 1) * 512],
                        start=(kd == 0), stop=(kd == 7))
                em.bar()
                nc.vector.tensor_copy(ob[:], psQ[:, 0:512])
                em.bar()
                em.dma(io["v_out"][row:row + 128, nn * 512:(nn + 1) * 512], ob[:])
                em.flush()
    em.pop_scope()
    if standalone:
        em.close()
    return em


def build_ATTN_CAUSAL(em=None, io=None):
    standalone = em is None
    if standalone:
        em = Emit("attnc").start()
        q_in = em.din("qf", [4, 64, S], BF16)
        k_in = em.din("kf", [4, 64, S], BF16)
        v_in = em.din("v4", [S, 256], BF16)
        io = {"q_src": lambda p: q_in[p].rearrange("d (r t) -> d r t", r=4),
              "k_src": lambda p: k_in[p].rearrange("d (r t) -> d r t", r=4),
              "v4_src": v_in.rearrange("(kb p) c -> p kb c", p=128),
              "mask": em.din("mask", [128, 128], F32),
              "ident": em.din("ident", [128, 128], BF16),
              "hT_out": em.dout("hT", [256, S], BF16)}
    nc = em.nc
    NP = 4
    SQ = S
    NQC = SQ // 128
    em.push_scope()
    qsb = em.sb([64, SQ], BF16, "q")
    ksb = em.sb([64, SQ], BF16, "k")
    vsb4 = em.sb([128, 64 * 256], BF16, "v4")
    mask = em.sb([128, 128], F32, "mask")
    ident = em.sb([128, 128], BF16, "id")
    S_sb = em.sb([128, SQ], F32, "S")
    P_sb = em.sb([128, SQ], BF16, "P")
    ptb = em.sb([128, 2 * 512], BF16, "ptb")
    osb = em.sb([128, 64], BF16, "o")
    oT = em.sb([64, 128], BF16, "oT")
    negm = em.sb([128, 1], F32, "nm")
    lcol = em.sb([128, 1], F32, "l")
    psS = [em.ps([128, 512], F32, "ps") for _ in range(2)]
    pstg = [em.ps([128, 512], BF16, "pg") for _ in range(2)]
    psO = em.ps([128, 64], F32, "pO")
    psT2 = em.ps([64, 128], BF16, "pT2")

    em.dma(mask[:], io["mask"])
    em.dma(ident[:], io["ident"])
    em.dma(vsb4[:].rearrange("p (kb c) -> p kb c", kb=64), io["v4_src"], nc.sync)
    em.flush()

    mm_sem = em.sem("at_mm")
    cp_sem = em.sem("at_cp")
    tr_sem = em.sem("at_tr")
    c2_sem = em.sem("at_c2")
    pv_sem = em.sem("at_pv")
    v1_sem = em.sem("at_v1")
    s2_sem = em.sem("at_s2")
    t3_sem = em.sem("at_t3")
    n_mm = [0]; n_cp = [0]; n_tr = [0]; n_c2 = [0]; n_pv = [0]
    n_v1 = [0]; n_s2 = [0]; n_t3 = [0]

    for p in range(NP):
        em.dma(qsb[:].rearrange("d (r t) -> d r t", r=io.get("NR", 4)),
               io["q_src"](p), nc.sync)
        em.dma(ksb[:].rearrange("d (r t) -> d r t", r=io.get("NR", 4)),
               io["k_src"](p), nc.sync)
        em.flush()
        for qc in range(NQC):
            L = (qc + 1) * 128
            nchunks = (L + 511) // 512
            cp0, mm0 = n_cp[0], n_mm[0]
            for i in range(nchunks):
                w = min(512, L - i * 512)
                bank = psS[i % 2]
                if i >= 2:
                    nc.tensor.wait_ge(cp_sem, cp0 + i - 1)
                nc.tensor.matmul(bank[:, 0:w], qsb[:, qc * 128:(qc + 1) * 128],
                                 ksb[:, i * 512:i * 512 + w],
                                 start=True, stop=True).then_inc(mm_sem, 1)
                n_mm[0] += 1
                nc.scalar.wait_ge(mm_sem, mm0 + i + 1)
                nc.scalar.activation(S_sb[:, i * 512:i * 512 + w], bank[:, 0:w],
                                     AF.Copy, scale=0.125).then_inc(cp_sem, 1)
                n_cp[0] += 1
            nc.vector.wait_ge(cp_sem, n_cp[0])
            nc.vector.tensor_tensor(S_sb[:, L - 128:L], S_sb[:, L - 128:L],
                                    mask[:], ALU.add)
            nc.vector.reduce_max(negm[:], S_sb[:, 0:L], axis=AX.X,
                                 negate=True).then_inc(v1_sem, 1)
            n_v1[0] += 1
            nc.scalar.wait_ge(v1_sem, n_v1[0])
            nc.scalar.activation(P_sb[:, 0:L], S_sb[:, 0:L], AF.Exp,
                                 bias=negm[:, 0:1],
                                 accum_out=lcol[:, 0:1]).then_inc(s2_sem, 1)
            n_s2[0] += 1
            nc.vector.wait_ge(s2_sem, n_s2[0])
            nc.vector.reciprocal(lcol[:], lcol[:])
            nblk = L // 128
            ngrp = (nblk + 3) // 4
            tr0, c20, pv0 = n_tr[0], n_c2[0], n_pv[0]
            nc.tensor.wait_ge(s2_sem, n_s2[0])
            for g in range(ngrp):
                blo = g * 4
                bhi = min(blo + 4, nblk)
                bank = pstg[g % 2]
                if g >= 2:
                    nc.tensor.wait_ge(c2_sem, c20 + g - 1)
                for b in range(blo, bhi):
                    inst = nc.tensor.transpose(
                        bank[:, (b - blo) * 128:(b - blo + 1) * 128],
                        P_sb[:, b * 128:(b + 1) * 128], ident[:])
                    if b == bhi - 1:
                        inst.then_inc(tr_sem, 1)
                        n_tr[0] += 1
                nc.vector.wait_ge(tr_sem, tr0 + g + 1)
                if g >= 2:
                    nc.vector.wait_ge(pv_sem, pv0 + g - 1)
                nc.vector.tensor_copy(
                    ptb[:, (g % 2) * 512:(g % 2) * 512 + (bhi - blo) * 128],
                    bank[:, 0:(bhi - blo) * 128]).then_inc(c2_sem, 1)
                n_c2[0] += 1
                nc.tensor.wait_ge(c2_sem, c20 + g + 1)
                for b in range(blo, bhi):
                    inst = nc.tensor.matmul(
                        psO[:, 0:64],
                        ptb[:, (g % 2) * 512 + (b - blo) * 128:
                            (g % 2) * 512 + (b - blo + 1) * 128],
                        vsb4[:, b * 256 + p * 64:b * 256 + (p + 1) * 64],
                        start=(b == 0), stop=(b == nblk - 1),
                        skip_group_check=True)
                    if b == bhi - 1:
                        inst.then_inc(pv_sem, 1)
                        n_pv[0] += 1
            nc.vector.wait_ge(pv_sem, n_pv[0])
            nc.vector.tensor_scalar(osb[:], psO[:, 0:64], lcol[:, 0:1], None,
                                    ALU.mult).then_inc(v1_sem, 1)
            n_v1[0] += 1
            nc.tensor.wait_ge(v1_sem, n_v1[0])
            nc.tensor.transpose(psT2[:, 0:128], osb[:], ident[:]).then_inc(t3_sem, 1)
            n_t3[0] += 1
            nc.vector.wait_ge(t3_sem, n_t3[0])
            nc.vector.tensor_copy(oT[:], psT2[:, 0:128])
            em.bar()
            em.dma(io["hT_out"][p * 64:(p + 1) * 64, qc * 128:(qc + 1) * 128],
                   oT[:])
            em.flush()
    em.pop_scope()
    if standalone:
        em.close()
    return em


def build_ADDT(em=None, io=None, src_dtype=BF16, scale=1.0, name="addt"):
    standalone = em is None
    if standalone:
        em = Emit(name).start()
        x_in = em.din("x", [TOKC, D], F32)
        a_in = em.din("aT", [D, TOKC], src_dtype)
        x_out = em.dout("xo", [TOKC, D], F32)
        io = {"x_tile": lambda r: x_in[r:r + 128, :],
              "xo_tile": lambda r: x_out[r:r + 128, :],
              "aT_tile": lambda r, kd: a_in[kd * 128:(kd + 1) * 128, r:r + 128],
              "identf": em.din("identf", [128, 128], src_dtype)}
    nc = em.nc
    em.push_scope()
    ident = em.sb([128, 128], src_dtype, "id")
    xg = em.sb([128, D], F32, "xg")
    asb = em.sb([128, 128], src_dtype, "a")
    tmpf = em.sb([128, 128], F32, "tf")
    psT = em.ps([128, 128], src_dtype, "pT")
    em.dma(ident[:], io["identf"])
    em.flush()
    for t in range(TOKC // 128):
        row = t * 128
        em.dma(xg[:], io["x_tile"](row))
        em.flush()
        for kd in range(8):
            em.dma(asb[:], io["aT_tile"](row, kd))
            em.flush()
            nc.tensor.transpose(psT[:, 0:128], asb[:], ident[:])
            em.bar()
            nc.vector.tensor_copy(tmpf[:], psT[:, 0:128])
            em.bar()
            if scale == 1.0:
                nc.vector.tensor_tensor(xg[:, kd * 128:(kd + 1) * 128],
                                        xg[:, kd * 128:(kd + 1) * 128],
                                        tmpf[:], ALU.add)
            else:
                nc.vector.scalar_tensor_tensor(xg[:, kd * 128:(kd + 1) * 128],
                                               tmpf[:], scale,
                                               xg[:, kd * 128:(kd + 1) * 128],
                                               ALU.mult, ALU.add)
            em.bar()
        em.dma(io["xo_tile"](row), xg[:])
        em.flush()
    em.pop_scope()
    if standalone:
        em.close()
    return em


def run_L0A(x, gamma, beta, w_qkv):
    xf = x.reshape(T, D)
    ident = np.eye(128, dtype=np.float32).astype(BFNP)
    identf = np.eye(128, dtype=np.float32)
    gbrow = np.concatenate([gamma, beta]).astype(np.float32)[None, :]
    wT = np.ascontiguousarray(w_qkv.T)
    em1 = _cached("qkv_aff", lambda: build_QKV(affine=True, name="qkvaff"))
    in_maps = [{"x": xf[c * TOKC:(c + 1) * TOKC], "wqkv": wT, "ident": ident,
                "gbrow": gbrow} for c in range(NCORES)]
    res = _run(em1.nc, in_maps)
    qkT = np.concatenate([res.results[c]["qkT"] for c in range(NCORES)], 1)
    v = np.concatenate([res.results[c]["v"] for c in range(NCORES)], 0)
    qkT = qkT.reshape(2048, B, S)
    v = v.reshape(B, S, 1024)

    em2 = _cached("attnc", build_ATTN_CAUSAL)
    mask = np.triu(np.full((128, 128), -1e30, np.float32), 1)
    in_maps = []
    for c in range(NCORES):
        bb, h0 = c // 4, (c % 4) * 4
        qf = np.stack([np.ascontiguousarray(qkT[(h0 + i) * 64:(h0 + i + 1) * 64, bb])
                       for i in range(4)])
        kf = np.stack([np.ascontiguousarray(
            qkT[1024 + (h0 + i) * 64:1024 + (h0 + i + 1) * 64, bb])
            for i in range(4)])
        v4 = np.ascontiguousarray(v[bb, :, h0 * 64:(h0 + 4) * 64])
        in_maps.append({"qf": qf, "kf": kf, "v4": v4, "mask": mask, "ident": ident})
    res = _run(em2.nc, in_maps)
    headsT = np.zeros((B, 1024, S), dtype=BFNP)
    for c in range(NCORES):
        bb, h0 = c // 4, (c % 4) * 4
        headsT[bb, h0 * 64:(h0 + 4) * 64, :] = res.results[c]["hT"]
    headsT = headsT.reshape(B * 1024, S)

    em3 = _cached("addt16", lambda: build_ADDT(src_dtype=BF16))
    in_maps = []
    for c in range(NCORES):
        bb = (c * TOKC) // S
        scol = slice((c * TOKC) % S, (c * TOKC) % S + TOKC)
        aT = np.ascontiguousarray(headsT[bb * 1024:(bb + 1) * 1024, scol])
        in_maps.append({"x": xf[c * TOKC:(c + 1) * TOKC], "aT": aT,
                        "identf": ident})
    res = _run(em3.nc, in_maps)
    out = np.concatenate([res.results[c]["xo"] for c in range(NCORES)], 0)
    return out.reshape(B, S, D)


def build_ATTN_WINDOW(em=None, io=None):
    standalone = em is None
    if standalone:
        em = Emit("attnw").start()
        q_in = em.din("qf", [4, 64, S], BF16)
        k_in = em.din("kf", [4, 64, S], BF16)
        v_in = em.din("v4", [S, 256], BF16)
        io = {"q_src": lambda p: q_in[p].rearrange("d (r t) -> d r t", r=4),
              "k_src": lambda p: k_in[p].rearrange("d (r t) -> d r t", r=4),
              "v4_src": v_in.rearrange("(kb p) c -> p kb c", p=128),
              "mdiag": em.din("mdiag", [128, 128], F32),
              "mband": em.din("mband", [128, 128], F32),
              "mc1": em.din("mc1", [128, 128], F32),
              "ident": em.din("ident", [128, 128], BF16),
              "hT_out": em.dout("hT", [256, S], BF16)}
    nc = em.nc
    NP = 4
    SQ = S
    NQC = SQ // 128
    em.push_scope()
    qsb = em.sb([64, SQ], BF16, "q")
    ksb = em.sb([64, SQ], BF16, "k")
    vsb4 = em.sb([128, 64 * 256], BF16, "v4")
    mdiag = em.sb([128, 128], F32, "md")
    mband = em.sb([128, 128], F32, "mb")
    mc1 = em.sb([128, 128], F32, "mc")
    ident = em.sb([128, 128], BF16, "id")
    S_sb = em.sb([128, 320], F32, "S")
    P_sb = em.sb([128, 320], BF16, "P")
    ptb = em.sb([128, 448], BF16, "ptb")
    osb = em.sb([128, 64], BF16, "o")
    oT = em.sb([64, 128], BF16, "oT")
    negm = em.sb([128, 1], F32, "nm")
    lcol = em.sb([128, 1], F32, "l")
    psS = em.ps([128, 512], F32, "ps")
    pstg = em.ps([128, 512], BF16, "pg")
    psO = em.ps([128, 64], F32, "pO")
    psT2 = em.ps([64, 128], BF16, "pT2")
    wt_sem = em.sem("aw_t")
    wv_sem = em.sem("aw_v")
    ws_sem = em.sem("aw_s")
    nt_ = [0]; nv_ = [0]; ns_ = [0]

    em.dma(mdiag[:], io["mdiag"])
    em.dma(mband[:], io["mband"])
    em.dma(mc1[:], io["mc1"])
    em.dma(ident[:], io["ident"])
    em.dma(vsb4[:].rearrange("p (kb c) -> p kb c", kb=64), io["v4_src"], nc.sync)
    em.flush()

    for p in range(NP):
        em.dma(qsb[:].rearrange("d (r t) -> d r t", r=io.get("NR", 4)),
               io["q_src"](p), nc.sync)
        em.dma(ksb[:].rearrange("d (r t) -> d r t", r=io.get("NR", 4)),
               io["k_src"](p), nc.sync)
        em.flush()
        for qc in range(NQC):
            if qc == 0:
                parts = [(0, 128, 0, mdiag)]
            elif qc == 1:
                parts = [(0, 128, 0, mc1), (128, 128, 1, mdiag)]
            else:
                parts = [(0, 64, -1, None),
                         ((qc - 1) * 128, 128, qc - 1, mband),
                         (qc * 128, 128, qc, mdiag)]
            W = sum(pp[1] for pp in parts)
            off = 0
            if nv_[0] > 0:
                nc.tensor.wait_ge(wv_sem, nv_[0])
            for (koff, kw, vb, msk) in parts:
                inst = nc.tensor.matmul(psS[:, off:off + kw],
                                        qsb[:, qc * 128:(qc + 1) * 128],
                                        ksb[:, koff:koff + kw],
                                        start=(off == 0), stop=(off + kw == W))
                off += kw
            inst.then_inc(wt_sem, 1)
            nt_[0] += 1
            nc.scalar.wait_ge(wt_sem, nt_[0])
            nc.scalar.activation(S_sb[:, 0:W], psS[:, 0:W], AF.Copy,
                                 scale=0.125).then_inc(ws_sem, 1)
            ns_[0] += 1
            nc.vector.wait_ge(ws_sem, ns_[0])
            off = 0
            for (koff, kw, vb, msk) in parts:
                if msk is not None:
                    nc.vector.tensor_tensor(S_sb[:, off:off + kw],
                                            S_sb[:, off:off + kw], msk[:], ALU.add)
                off += kw
            nc.vector.reduce_max(negm[:], S_sb[:, 0:W], axis=AX.X,
                                 negate=True).then_inc(wv_sem, 1)
            nv_[0] += 1
            nc.scalar.wait_ge(wv_sem, nv_[0])
            nc.scalar.activation(P_sb[:, 0:W], S_sb[:, 0:W], AF.Exp,
                                 bias=negm[:, 0:1],
                                 accum_out=lcol[:, 0:1]).then_inc(ws_sem, 1)
            ns_[0] += 1
            nc.vector.wait_ge(ws_sem, ns_[0])
            nc.vector.reciprocal(lcol[:], lcol[:])
            off = 0
            nc.tensor.wait_ge(ws_sem, ns_[0])
            for (koff, kw, vb, msk) in parts:
                if kw == 128:
                    inst = nc.tensor.transpose(pstg[:, off:off + kw],
                                               P_sb[:, off:off + kw], ident[:])
                else:
                    inst = nc.tensor.transpose(pstg[0:kw, 384:512],
                                               P_sb[:, off:off + kw], ident[:])
                off += kw
            inst.then_inc(wt_sem, 1)
            nt_[0] += 1
            nc.vector.wait_ge(wt_sem, nt_[0])
            off = 0
            for (koff, kw, vb, msk) in parts:
                if kw == 128:
                    inst = nc.vector.tensor_copy(ptb[:, off:off + 128],
                                                 pstg[:, off:off + 128])
                else:
                    inst = nc.vector.tensor_copy(ptb[0:kw, 320:448],
                                                 pstg[0:kw, 384:512])
                off += kw
            inst.then_inc(wv_sem, 1)
            nv_[0] += 1
            nc.tensor.wait_ge(wv_sem, nv_[0])
            off = 0
            first = True
            for (koff, kw, vb, msk) in parts:
                if kw == 128:
                    inst = nc.tensor.matmul(
                        psO[:, 0:64], ptb[:, off:off + 128],
                        vsb4[:, vb * 256 + p * 64:vb * 256 + (p + 1) * 64],
                        start=first, stop=(off + kw == W),
                        skip_group_check=True)
                else:
                    inst = nc.tensor.matmul(
                        psO[:, 0:64], ptb[0:64, 320:448],
                        vsb4[0:64, p * 64:p * 64 + 64],
                        start=first, stop=(off + kw == W),
                        skip_group_check=True)
                first = False
                off += kw
            inst.then_inc(wt_sem, 1)
            nt_[0] += 1
            nc.vector.wait_ge(wt_sem, nt_[0])
            nc.vector.tensor_scalar(osb[:], psO[:, 0:64], lcol[:, 0:1], None,
                                    ALU.mult).then_inc(wv_sem, 1)
            nv_[0] += 1
            nc.tensor.wait_ge(wv_sem, nv_[0])
            nc.tensor.transpose(psT2[:, 0:128], osb[:], ident[:]).then_inc(wt_sem, 1)
            nt_[0] += 1
            nc.vector.wait_ge(wt_sem, nt_[0])
            nc.vector.wait_ge(em.dsem, em.dcount)
            nc.vector.tensor_copy(oT[:], psT2[:, 0:128]).then_inc(wv_sem, 1)
            nv_[0] += 1
            nc.sync.wait_ge(wv_sem, nv_[0])
            em.dma(io["hT_out"][p * 64:(p + 1) * 64, qc * 128:(qc + 1) * 128],
                   oT[:])
    em.pop_scope()
    if standalone:
        em.close()
    return em



from concourse.bass import IndirectOffsetOnAxis

NGIDX = 384


def _fused_add_gather_support(em):
    em.gidx_in = em.din("gidx", [NGIDX, 128], I32)
    em.gather_specs = []


def _gather(em, dst_ap, src_ap2d, idx_fn, width, dtype=F32):
    slot = len(em.gather_specs)
    assert slot < NGIDX
    em.gather_specs.append(idx_fn)
    nc = em.nc
    em.push_scope()
    idx_sb = em.sb([128, 1], I32, "gidx")
    stage = em.sb([128, width], dtype, "gst")
    em.dma(idx_sb[:], em.gidx_in[slot:slot + 1, :].rearrange("o p -> p o"),
           nc.gpsimd)
    em.flush(nc.gpsimd)
    nc.gpsimd.indirect_dma_start(
        out=stage[:], out_offset=None, in_=src_ap2d,
        in_offset=IndirectOffsetOnAxis(ap=idx_sb[:], axis=0),
    ).then_inc(em.dsem, 16)
    em.dcount += 16
    em.flush(nc.gpsimd)
    em.dma(dst_ap, stage[:], nc.gpsimd)
    em.flush(nc.gpsimd)
    em.pop_scope()


def build_FUSED(upto=99):
    em = Emit("fused").start()
    nc = em.nc
    F3 = 3072
    x_in = em.din("x", [TOKC, D], F32)
    x_out = em.dout("xo", [TOKC, D], F32)
    ident_in = em.din("ident", [128, 128], BF16)
    identf_in = em.din("identf", [128, 128], F32)
    mask_in = em.din("mask", [128, 128], F32)
    mdiag_in = em.din("mdiag", [128, 128], F32)
    mband_in = em.din("mband", [128, 128], F32)
    mc1_in = em.din("mc1", [128, 128], F32)
    W = {
        "l0f_mixw": em.din("l0f_mixw", [D, D], F32),
        "l0f_router": em.din("l0f_router", [D, 8], F32),
        "l0f_w1": em.din("l0f_w1", [8, D, 4096], F32),
        "l0f_w2": em.din("l0f_w2", [8, 4096, D], F32),
        "l0f_rows": em.din("l0f_rows", [1, 5 * D], F32),
        "l0f_b1c": em.din("l0f_b1c", [128, 8 * 32], F32),
        "l0f_b2r": em.din("l0f_b2r", [1, 8 * D], F32),
        "l0a_wqkv": em.din("l0a_wqkv", [D, F3], F32),
        "l0a_gbrow": em.din("l0a_gbrow", [1, 2 * D], F32),
        "l1f_lnrow": em.din("l1f_lnrow", [1, 2 * D], F32),
        "l1f_muw": em.din("l1f_muw", [D, 256], F32),
        "l1f_scl": em.din("l1f_scl", [1, 1], F32),
        "l1fs_w1": em.din("l1fs_w1", [2, D, 256], F32),
        "l1fs_w2": em.din("l1fs_w2", [2, 256, D], F32),
        "l1fs_brow": em.din("l1fs_brow", [1, 2 * 256 + 2 * D], F32),
        "l1a_cw": em.din("l1a_cw", [128, 4], F32),
        "l1a_cb": em.din("l1a_cb", [128, 1], F32),
        "l1a_xpw": em.din("l1a_xpw", [D, 33], F32),
        "l1a_dtwr": em.din("l1a_dtwr", [1, 128], F32),
        "l1a_dtbc": em.din("l1a_dtbc", [128, 1], F32),
        "l1a_alog": em.din("l1a_alog", [128, 16], F32),
        "l1a_dpc": em.din("l1a_dpc", [128, 1], F32),
        "l2f_gatew": em.din("l2f_gatew", [D, 8], F32),
        "l2f_w1": em.din("l2f_w1", [9, D, 4096], F32),
        "l2f_w2": em.din("l2f_w2", [9, 4096, D], F32),
        "l2a_win": em.din("l2a_win", [D, F3], F32),
        "l2a_binr": em.din("l2a_binr", [1, F3], F32),
        "l2a_wout": em.din("l2a_wout", [D, D], F32),
        "l2a_boutr": em.din("l2a_boutr", [1, D], F32),
        "l3a_wqkv": em.din("l3a_wqkv", [D, F3], F32),
        "l3a_wout": em.din("l3a_wout", [D, D], F32),
        "l3f_wg": em.din("l3f_wg", [D, 3072], I32),
        "l3f_wu": em.din("l3f_wu", [D, 3072], I32),
        "l3f_wd": em.din("l3f_wd", [3072, D], I32),
    }
    _fused_add_gather_support(em)

    xs = [em.dram(f"xs{i}", [TOKC, D]) for i in range(12)]
    qkT_loc = em.dram("qkT_loc", [2048, TOKC], BF16)
    v_loc = em.dram("v_loc", [TOKC, 1024], BF16)
    qkT_full = em.dram("qkT_full", [8 * 2048, TOKC], BF16, shared=True)
    v_full = em.dram("v_full", [T, 1024], BF16, shared=True)
    hT_loc = em.dram("hT_loc", [256, S], BF16)
    hT_full = em.dram("hT_full", [2048, S], BF16, shared=True)
    qf_loc = em.dram("qf_loc", [1024, 2048], BF16)
    kf_loc = em.dram("kf_loc", [1024, 2048], BF16)
    v4_loc = em.dram("v4_loc", [S, 256], BF16)
    aT_loc = em.dram("aT_loc", [1024, TOKC], BF16)
    xn_loc = em.dram("xn_loc", [TOKC, D], F32)
    xn_full = em.dram("xn_full", [T, D], F32, shared=True)
    lgT_loc = em.dram("lgT_loc", [256, TOKC], F32)
    lgT_full = em.dram("lgT_full", [2048, TOKC], F32, shared=True)
    cbT_loc = em.dram("cbT_loc", [256, TOKC], F32)
    lgs_loc = em.dram("lgs_loc", [256, 2048], F32)
    so_loc = em.dram("so_loc", [64, D], F32)
    so_full = em.dram("so_full", [512, D], F32, shared=True)
    sob_loc = em.dram("sob_loc", [256, D], F32)
    xnT1_loc = em.dram("xnT1_loc", [D, TOKC], F32)
    xnT1_full = em.dram("xnT1_full", [8 * D, TOKC], F32, shared=True)
    xnf_loc = em.dram("xnf_loc", [1024, 2048], F32)
    xc_loc = em.dram("xc_loc", [128, B * S], F32)
    xc_full = em.dram("xc_full", [1024, B * S], F32, shared=True)
    xcT_loc = em.dram("xcT_loc", [1024, 2048], F32)
    projT_loc = em.dram("projT_loc", [33, TOKC], F32)
    projT_full = em.dram("projT_full", [264, TOKC], F32, shared=True)
    scanT_loc = em.dram("scanT_loc", [128, B * S], F32)
    scan_full = em.dram("scan_full", [1024, B * S], F32, shared=True)
    scanA_loc = em.dram("scanA_loc", [1024, 2048], F32)
    xg_full = em.dram("xg_full", [T, D], F32, shared=True)
    x2a_loc = em.dram("x2a_loc", [2048, D], F32)
    xp_loc = em.dram("xp_loc", [2048, D], F32)
    xp_full = em.dram("xp_full", [T, D], F32, shared=True)

    def xtile(buf):
        return lambda r: buf[r:r + 128, :]

    state = {"cur": None, "done": False, "ph": -1}

    def phase(label):
        state["ph"] += 1
        active = state["ph"] <= upto and not state["done"]
        if active and state["ph"] > 0:
            em.dma_wait()
            em.bar()
            em.rotate_dsem()
        return active

    cur = x_in
    nxt = iter(xs)

    def step_ffn(build_fn, wio, cur):
        buf = next(nxt)
        io = {"x_tile": xtile(cur), "xo_tile": xtile(buf), **wio}
        build_fn(em=em, io=io)
        em.bar()
        return buf

    def do_attention(wqkv_key, affine, window, cur):
        io = {"x_tile": xtile(cur), "wqkv": W[wqkv_key], "ident": ident_in,
              "qkT_out": qkT_loc, "v_out": v_loc}
        if affine:
            io["gbrow"] = W["l0a_gbrow"]
        build_QKV(em=em, io=io, affine=affine)
        em.allgather(qkT_loc, qkT_full)
        em.allgather(v_loc, v_full)
        for g in range(8):
            def qidx(cid, g=g, koff=0):
                out = np.empty(128, np.int32)
                bb, h0 = cid // 4, (cid % 4) * 4
                for j in range(128):
                    i = g * 128 + j
                    p, rem = i // 256, i % 256
                    r, dd = rem // 64, rem % 64
                    out[j] = (bb * 4 + r) * 2048 + koff + (h0 + p) * 64 + dd
                return out
            _gather(em, qf_loc[g * 128:(g + 1) * 128, :], qkT_full, qidx, 2048, BF16)
            def kidx(cid, g=g):
                return qidx(cid, g, 1024)
            _gather(em, kf_loc[g * 128:(g + 1) * 128, :], qkT_full, kidx, 2048, BF16)
        vview = v_full.rearrange("t (hq c) -> (t hq) c", hq=4)
        for kb in range(64):
            def vidx(cid, kb=kb):
                bb, h0 = cid // 4, (cid % 4) * 4
                t = bb * 8192 + kb * 128 + np.arange(128)
                return (t * 4 + h0 // 4).astype(np.int32)
            _gather(em, v4_loc[kb * 128:(kb + 1) * 128, :], vview, vidx, 256, BF16)
        aio = {"q_src": lambda p: qf_loc.rearrange("(pp r dd) t -> pp dd r t",
                                                   pp=4, r=4)[p],
               "k_src": lambda p: kf_loc.rearrange("(pp r dd) t -> pp dd r t",
                                                   pp=4, r=4)[p],
               "v4_src": v4_loc.rearrange("(kb p) c -> p kb c", p=128),
               "ident": ident_in, "hT_out": hT_loc, "NR": 4}
        if window:
            aio.update({"mdiag": mdiag_in, "mband": mband_in, "mc1": mc1_in})
            build_ATTN_WINDOW(em=em, io=aio)
        else:
            aio["mask"] = mask_in
            build_ATTN_CAUSAL(em=em, io=aio)
        em.allgather(hT_loc, hT_full)
        hview = hT_full.rearrange("f (sb t) -> (f sb) t", sb=4)
        for kd in range(8):
            def hidx(cid, kd=kd):
                bb, sb = cid // 4, cid % 4
                f = bb * 1024 + kd * 128 + np.arange(128)
                return (f * 4 + sb).astype(np.int32)
            _gather(em, aT_loc[kd * 128:(kd + 1) * 128, :], hview, hidx, 2048, BF16)
        em.bar()

    if phase("l0f1"):
        cur = step_ffn(build_L0F, {"mixw": W["l0f_mixw"], "router": W["l0f_router"],
                                   "w1": W["l0f_w1"], "w2": W["l0f_w2"],
                                   "rows": W["l0f_rows"], "b1c": W["l0f_b1c"],
                                   "b2r": W["l0f_b2r"], "ident": ident_in}, cur)
    if phase("l0a"):
        do_attention("l0a_wqkv", True, False, cur)
        buf = next(nxt)
        build_ADDT(em=em, io={"x_tile": xtile(cur), "xo_tile": xtile(buf),
                              "aT_tile": lambda r, kd: aT_loc[kd * 128:(kd + 1) * 128,
                                                              r:r + 128],
                              "identf": ident_in}, src_dtype=BF16)
        em.bar()
        cur = buf
    if phase("l0f2"):
        cur = step_ffn(build_L0F, {"mixw": W["l0f_mixw"], "router": W["l0f_router"],
                                   "w1": W["l0f_w1"], "w2": W["l0f_w2"],
                                   "rows": W["l0f_rows"], "b1c": W["l0f_b1c"],
                                   "b2r": W["l0f_b2r"], "ident": ident_in}, cur)

    def do_l1f(cur):
        build_L1F_pre(em=em, io={
            "x_tile": xtile(cur), "lnrow": W["l1f_lnrow"], "muw": W["l1f_muw"],
            "scl": W["l1f_scl"], "ident": ident_in, "identf": identf_in,
            "xn_out": lambda sl: xn_loc[sl], "lgT_out": lambda sl: lgT_loc[sl],
            "cbT_out": lambda sl: cbT_loc[sl]})
        em.allgather(xn_loc, xn_full)
        em.allgather(lgT_loc, lgT_full)
        for g in range(2):
            def lidx(cid, g=g):
                out = np.empty(128, np.int32)
                for j in range(128):
                    i = g * 128 + j
                    b2, rem = i // 128, i % 128
                    eL, rem2 = rem // 64, rem % 64
                    s, rr = rem2 // 4, rem2 % 4
                    out[j] = (b2 * 4 + rr) * 256 + (2 * cid + eL) * 16 + s
                return out
            _gather(em, lgs_loc[g * 128:(g + 1) * 128, :], lgT_full, lidx, 2048, F32)
        build_L1F_slots(em=em, io={
            "lgs_src": lgs_loc.rearrange("(bes rr) t -> bes rr t", rr=4),
            "lgs_rearr": "p (r t) -> p r t", "lgs_kw": {"r": 4},
            "xn_tile": lambda b, r: xn_full[b * S + r:b * S + r + 128, :],
            "w1": W["l1fs_w1"], "w2": W["l1fs_w2"], "brow": W["l1fs_brow"],
            "ident": ident_in, "so_out": so_loc})
        em.allgather(so_loc, so_full)
        for g in range(2):
            def sidx(cid, g=g):
                bb = cid // 4
                out = np.empty(128, np.int32)
                for j in range(128):
                    es = g * 128 + j
                    e, s = es // 16, es % 16
                    out[j] = (e // 2) * 64 + bb * 32 + (e % 2) * 16 + s
                return out
            _gather(em, sob_loc[g * 128:(g + 1) * 128, :], so_full, sidx, 1024, F32)
        buf = next(nxt)
        build_L1F_out(em=em, io={
            "cT_src": lambda c: cbT_loc[c * 128:(c + 1) * 128, :],
            "so_src": lambda c: sob_loc[c * 128:(c + 1) * 128, :],
            "x_tile": xtile(cur), "xo_tile": xtile(buf)})
        em.bar()
        return buf

    if phase("l1f1"):
        cur = do_l1f(cur)
    if phase("l1a"):
        build_L1A_ln(em=em, io={
            "x_tile": xtile(cur), "identf": identf_in,
            "xnT_out": lambda kd, r: xnT1_loc[kd * 128:(kd + 1) * 128, r:r + 128]})
        em.allgather(xnT1_loc, xnT1_full)
        for rr in range(8):
            def xnidx(cid, rr=rr):
                return (rr * 1024 + cid * 128 + np.arange(128)).astype(np.int32)
            _gather(em, xnf_loc[rr * 128:(rr + 1) * 128, :], xnT1_full, xnidx, 2048, F32)
        xnf_view = xnf_loc.rearrange("(rr dd) t -> dd rr t", dd=128)
        build_L1A_conv(em=em, io={
            "xn_src": lambda b: xnf_view[:, b * 4:(b + 1) * 4, :], "NRC": 4,
            "cw": W["l1a_cw"], "cbr": W["l1a_cb"], "xc_out": xc_loc})
        em.allgather(xc_loc, xc_full)
        xcview = xc_full.rearrange("dd (cb t) -> (dd cb) t", cb=8)
        for kd in range(8):
            def xcidx(cid, kd=kd):
                return ((kd * 128 + np.arange(128)) * 8 + cid).astype(np.int32)
            _gather(em, xcT_loc[kd * 128:(kd + 1) * 128, :], xcview, xcidx, 2048, F32)
        build_L1A_proj(em=em, io={
            "xc_src": lambda kd: xcT_loc[kd * 128:(kd + 1) * 128, :],
            "xpw": W["l1a_xpw"], "identf": identf_in,
            "pjT_out": lambda r: projT_loc[:, r:r + 128]})
        em.allgather(projT_loc, projT_full)
        pview = projT_full.rearrange("(r n) t -> n r t", n=33)
        xnfv = xnf_loc.rearrange("(rr dd) t -> dd rr t", dd=128)
        build_L1A_scan(em=em, io={
            "xc": xc_loc,
            "xn_tile_fm": lambda b, ch: xnfv[:, (b * S + ch * 512) // 2048,
                                             (b * S + ch * 512) % 2048:
                                             (b * S + ch * 512) % 2048 + 512],
            "prow_src": lambda b: pview[32:33, b * 4:(b + 1) * 4, :],
            "brow_src": lambda b, n: pview[n:n + 1, b * 4:(b + 1) * 4, :],
            "crow_src": lambda b, n: pview[16 + n:17 + n, b * 4:(b + 1) * 4, :],
            "NRR": 4, "dtwr": W["l1a_dtwr"], "dtbc": W["l1a_dtbc"],
            "alog": W["l1a_alog"], "dpc": W["l1a_dpc"], "scanT_out": scanT_loc})
        em.allgather(scanT_loc, scan_full)
        sview = scan_full.rearrange("d (cb t) -> (d cb) t", cb=8)
        for kd in range(8):
            def scidx(cid, kd=kd):
                return ((kd * 128 + np.arange(128)) * 8 + cid).astype(np.int32)
            _gather(em, scanA_loc[kd * 128:(kd + 1) * 128, :], sview, scidx, 2048, F32)
        buf = next(nxt)
        build_ADDT(em=em, io={"x_tile": xtile(cur), "xo_tile": xtile(buf),
                              "aT_tile": lambda r, kd: scanA_loc[kd * 128:(kd + 1) * 128,
                                                                 r:r + 128],
                              "identf": identf_in}, src_dtype=F32)
        em.bar()
        cur = buf
    if phase("l1f2"):
        cur = do_l1f(cur)

    if phase("l2f1"):
        cur = step_ffn(build_L2F, {"gatew": W["l2f_gatew"], "w1": W["l2f_w1"],
                                   "w2": W["l2f_w2"], "ident": ident_in}, cur)
    if phase("l2a"):
        em.allgather(cur, xg_full)
        for g in range(16):
            def x2idx(cid, g=g):
                i = g * 128 + np.arange(128)
                b2 = i // 1024
                return (b2 * 8192 + cid * 1024 + (i % 1024)).astype(np.int32)
            _gather(em, x2a_loc[g * 128:(g + 1) * 128, :], xg_full, x2idx, 1024, F32)
        build_L2A(em=em, io={
            "x_tile": lambda b, r: x2a_loc[b * 1024 + r:b * 1024 + r + 128, :],
            "xo_tile": lambda b, r: xp_loc[b * 1024 + r:b * 1024 + r + 128, :],
            "win": W["l2a_win"], "binr": W["l2a_binr"], "wout": W["l2a_wout"],
            "boutr": W["l2a_boutr"], "ident": ident_in})
        em.allgather(xp_loc, xp_full)
        buf = next(nxt)
        for g in range(16):
            def xpidx(cid, g=g):
                i = g * 128 + np.arange(128)
                t = cid * TOKC + i
                b2, s = t // S, t % S
                return ((s // 1024) * 2048 + b2 * 1024 + (s % 1024)).astype(np.int32)
            _gather(em, buf[g * 128:(g + 1) * 128, :], xp_full, xpidx, 1024, F32)
        em.bar()
        cur = buf
    if phase("l2f2"):
        cur = step_ffn(build_L2F, {"gatew": W["l2f_gatew"], "w1": W["l2f_w1"],
                                   "w2": W["l2f_w2"], "ident": ident_in}, cur)

    if phase("l3f1"):
        cur = step_ffn(build_L3F, {"wg": W["l3f_wg"], "wu": W["l3f_wu"],
                                   "wd": W["l3f_wd"], "ident": ident_in}, cur)
    if phase("l3a"):
        do_attention("l3a_wqkv", False, True, cur)
        buf = next(nxt)
        build_PROJRES(1.0, em=em, io={
            "a_src": lambda kd: aT_loc[kd * 128:(kd + 1) * 128, :],
            "w": W["l3a_wout"], "x_tile": xtile(cur), "xo_tile": xtile(buf)})
        em.bar()
        cur = buf
    if phase("l3f2"):
        io = {"x_tile": xtile(cur),
              "xo_tile": lambda r: x_out[r:r + 128, :],
              "wg": W["l3f_wg"], "wu": W["l3f_wu"], "wd": W["l3f_wd"],
              "ident": ident_in}
        build_L3F(em=em, io=io)
        em.bar()
        state["done"] = True

    if not state["done"]:
        em.dma(x_out, cur)
        em.flush()
    em.close()
    return em


def _fused_host_inputs(inp, em):
    x = np.asarray(inp["L0_attn_x"], np.float32).reshape(T, D)
    ident = np.eye(128, dtype=np.float32).astype(BFNP)
    identf = np.eye(128, dtype=np.float32)
    mask = np.triu(np.full((128, 128), -1e30, np.float32), 1)
    i2 = np.arange(128)[:, None]
    j2 = np.arange(128)[None, :]
    mdiag = np.where(j2 <= i2, 0.0, -1e30).astype(np.float32)
    mband = np.where(j2 >= i2, 0.0, -1e30).astype(np.float32)
    mc1 = np.where((j2 >= i2) | (j2 < 64), 0.0, -1e30).astype(np.float32)
    H = 16
    perm = np.concatenate([
        np.concatenate([np.arange(h * 192 + c * 64, h * 192 + c * 64 + 64)
                        for h in range(H)]) for c in range(3)])
    common = {
        "ident": ident, "identf": identf, "mask": mask, "mdiag": mdiag,
        "mband": mband, "mc1": mc1,
        "l0f_mixw": inp["L0_ffn_mix_w"],
        "l0f_router": inp["L0_ffn_router_w"],
        "l0f_w1": inp["L0_ffn_w1"], "l0f_w2": inp["L0_ffn_w2"],
        "l0f_rows": np.concatenate([inp["L0_ffn_g1"], inp["L0_ffn_be1"],
                                    inp["L0_ffn_g2"], inp["L0_ffn_be2"],
                                    inp["L0_ffn_mix_b"]]).astype(np.float32)[None, :],
        "l0f_b1c": np.ascontiguousarray(
            np.asarray(inp["L0_ffn_b1"]).reshape(8, 32, 128)
            .transpose(2, 0, 1).reshape(128, 256)),
        "l0f_b2r": np.ascontiguousarray(np.asarray(inp["L0_ffn_b2"]).reshape(1, -1)),
        "l0a_wqkv": np.ascontiguousarray(np.asarray(inp["L0_attn_w_qkv"]).T),
        "l0a_gbrow": np.concatenate([inp["L0_attn_gamma"],
                                     inp["L0_attn_beta"]]).astype(np.float32)[None, :],
        "l1f_lnrow": np.concatenate([inp["L1_ffn_ln_gamma"],
                                     inp["L1_ffn_ln_beta"]]).astype(np.float32)[None, :],
        "l1f_muw": np.ascontiguousarray(np.asarray(inp["L1_ffn_mu"]).reshape(D, 256)),
        "l1f_scl": np.asarray(inp["L1_ffn_scale"], np.float32).reshape(1, 1),
        "l1a_xpw": np.ascontiguousarray(np.asarray(inp["L1_attn_xproj_w"]).T),
        "l2f_gatew": np.ascontiguousarray(np.asarray(inp["L2_ffn_gate_w"]).T),
        "l2f_w1": np.ascontiguousarray(np.concatenate(
            [np.asarray(inp["L2_ffn_experts_w1"]).transpose(0, 2, 1),
             np.asarray(inp["L2_ffn_shared_w1"]).T[None]], 0)),
        "l2f_w2": np.ascontiguousarray(np.concatenate(
            [np.asarray(inp["L2_ffn_experts_w2"]).transpose(0, 2, 1),
             np.asarray(inp["L2_ffn_shared_w2"]).T[None]], 0)),
        "l2a_win": np.ascontiguousarray(
            np.asarray(inp["L2_attn_in_proj_weight"])[perm, :].T),
        "l2a_binr": np.ascontiguousarray(
            np.asarray(inp["L2_attn_in_proj_bias"])[perm])[None, :],
        "l2a_wout": np.ascontiguousarray(np.asarray(inp["L2_attn_out_proj_weight"]).T),
        "l2a_boutr": np.ascontiguousarray(
            np.asarray(inp["L2_attn_out_proj_bias"]))[None, :],
        "l3a_wqkv": np.asarray(inp["L3_attn_w_qkv"]),
        "l3a_wout": np.asarray(inp["L3_attn_w_out"]),
        "l3f_wg": np.ascontiguousarray(np.asarray(inp["L3_ffn_w_gate"]).T),
        "l3f_wu": np.ascontiguousarray(np.asarray(inp["L3_ffn_w_up"]).T),
        "l3f_wd": np.ascontiguousarray(np.asarray(inp["L3_ffn_w_down"]).T),
    }
    cw = np.asarray(inp["L1_attn_conv_w"]).reshape(D, 4)
    w1s = np.asarray(inp["L1_ffn_w1"])
    w2s = np.asarray(inp["L1_ffn_w2"])
    b1s = np.asarray(inp["L1_ffn_b1"])
    b2s = np.asarray(inp["L1_ffn_b2"])
    dt_w = np.asarray(inp["L1_attn_dt_w"])
    dt_b = np.asarray(inp["L1_attn_dt_b"])
    a_log = np.asarray(inp["L1_attn_a_log"])
    Dp = np.asarray(inp["L1_attn_Dp"])
    in_maps = []
    for c in range(NCORES):
        dsl = slice(c * 128, (c + 1) * 128)
        eids = [2 * c, 2 * c + 1]
        gidx = np.zeros((NGIDX, 128), np.int32)
        for i, fn in enumerate(em.gather_specs):
            gidx[i] = fn(c)
        in_maps.append({
            "x": x[c * TOKC:(c + 1) * TOKC], "gidx": gidx,
            "l1fs_w1": w1s[eids], "l1fs_w2": w2s[eids],
            "l1fs_brow": np.concatenate(
                [b1s[eids].reshape(-1), b2s[eids].reshape(-1)]
            ).astype(np.float32)[None, :],
            "l1a_cw": cw[dsl], "l1a_cb": np.asarray(inp["L1_attn_conv_b"])[dsl, None],
            "l1a_dtwr": np.ascontiguousarray(dt_w[dsl, 0])[None, :],
            "l1a_dtbc": dt_b[dsl, None], "l1a_alog": a_log[dsl],
            "l1a_dpc": Dp[dsl, None],
            **common})
    return in_maps


def kernel_fused(upto=99, **inp):
    key = f"fused_{upto}"
    em = _cached(key, lambda: build_FUSED(upto))
    in_maps = _fused_host_inputs(inp, em)
    res = _run(em.nc, in_maps)
    out = np.concatenate([res.results[c]["xo"] for c in range(NCORES)], 0)
    return out.reshape(B, S, D)


# version 04
# speedup vs baseline: 1.1013x; 1.1013x over previous
import contextlib
import numpy as np
import ml_dtypes
import concourse.bass as bass
import concourse.mybir as mybir
from concourse.bass_utils import run_bass_kernel_spmd

F32 = mybir.dt.float32
BF16 = mybir.dt.bfloat16
I32 = mybir.dt.int32
AF = mybir.ActivationFunctionType
ALU = mybir.AluOpType
AX = mybir.AxisListType
BFNP = ml_dtypes.bfloat16

B, S, D = 2, 8192, 1024
T = B * S
NCORES = 8
TOKC = T // NCORES
LN_EPS = 1e-5

_launch_count = [0]


def _run(nc, in_maps):
    import time
    _launch_count[0] += 1
    try:
        return run_bass_kernel_spmd(nc, in_maps, core_ids=list(range(NCORES)))
    except Exception as exc:
        if "UNRECOVERABLE" not in str(exc) and "PassThrough" not in str(exc):
            raise
        time.sleep(5)
        _launch_count[0] += 1
        return run_bass_kernel_spmd(nc, in_maps, core_ids=list(range(NCORES)))


class Emit:

    def __init__(self, name):
        self.nc = bass.Bass("TRN2", target_bir_lowering=False)
        self.name = name
        self.stack = contextlib.ExitStack()
        self.scopes = [self.stack]
        self.n = 0
        self.dsem = None
        self.dcount = 0
        self.sems = {}
        self.ccsem = None
        self.cccount = 0
        self.counters = {}

    def _nm(self, pfx):
        self.n += 1
        return f"{pfx}_{self.n}"

    def start(self):
        self.dsem = self.stack.enter_context(self.nc.semaphore("dsem"))
        return self

    def rotate_dsem(self):
        if not hasattr(self, "_dsem_pool"):
            self._dsem_pool = [self.dsem] + [
                self.stack.enter_context(self.nc.semaphore(f"dsemr{i}"))
                for i in range(1, 8)]
            self._dsem_vals = [0] * 8
            self._dsem_i = 0
        self._dsem_vals[self._dsem_i] = self.dcount
        self._dsem_i = (self._dsem_i + 1) % 8
        self.dsem = self._dsem_pool[self._dsem_i]
        self.dcount = self._dsem_vals[self._dsem_i]

    def push_scope(self):
        st = contextlib.ExitStack()
        self.scopes.append(st)

    def pop_scope(self):
        st = self.scopes.pop()
        st.close()

    @contextlib.contextmanager
    def scope(self):
        st = contextlib.ExitStack()
        self.scopes.append(st)
        try:
            yield
        finally:
            self.scopes.pop()
            st.close()

    def allgather(self, src_ap, dst_ap):
        if self.ccsem is None:
            self.ccsem = self.stack.enter_context(self.nc.semaphore("ccsem"))
        self.bar()
        self.nc.gpsimd.collective_compute(
            "AllGather", mybir.AluOpType.bypass,
            replica_groups=[list(range(NCORES))],
            ins=[src_ap], outs=[dst_ap],
        ).then_inc(self.ccsem, 1)
        self.cccount += 1
        self.nc.gpsimd.wait_ge(self.ccsem, self.cccount)
        self.bar()

    def sem(self, key):
        if key not in self.sems:
            self.sems[key] = self.stack.enter_context(self.nc.semaphore(key))
        return self.sems[key]

    def close(self):
        self.stack.close()

    def sb(self, shape, dtype=F32, pfx="t"):
        return self.scopes[-1].enter_context(
            self.nc.sbuf_tensor(self._nm(pfx), shape, dtype))

    def ps(self, shape, dtype=F32, pfx="p"):
        import math
        esz = {F32: 4, BF16: 2, I32: 4}[dtype]
        per_part = 1
        for d in shape[1:]:
            per_part *= d
        per_part *= esz
        pad_bytes = math.ceil(per_part / 2048) * 2048
        if pad_bytes != per_part:
            assert len(shape) == 2, "pad only for 2-d psum shapes"
            shape = [shape[0], pad_bytes // esz]
        return self.scopes[-1].enter_context(
            self.nc.psum_tensor(self._nm(pfx), shape, dtype))

    def dram(self, name, shape, dtype=F32, shared=False):
        kw = {"addr_space": "Shared"} if shared else {}
        return self.nc.dram_tensor(name, shape, dtype, kind="Internal", **kw).ap()

    def din(self, name, shape, dtype=F32):
        return self.nc.dram_tensor(name, shape, dtype, kind="ExternalInput").ap()

    def dout(self, name, shape, dtype=F32):
        return self.nc.dram_tensor(name, shape, dtype, kind="ExternalOutput").ap()

    def dma(self, out, in_, eng=None):
        eng = eng or self.nc.sync
        eng.dma_start(out=out, in_=in_).then_inc(self.dsem, 16)
        self.dcount += 16

    def dma_wait(self, eng=None):
        (eng or self.nc.sync).wait_ge(self.dsem, self.dcount)

    def flush(self, eng=None):
        self.dma_wait(eng)
        self.nc.all_engine_barrier()

    def bar(self):
        self.nc.all_engine_barrier()

    def const_col(self, value):
        c = self.sb([128, 1], F32, "c")
        self.nc.vector.memset(c[:], value)
        return c

    def ones_row(self, n=128, dtype=F32):
        r = self.sb([1, n], dtype, "ones")
        self.nc.vector.memset(r[:], 1.0)
        return r

    def ln_cols(self, x_ap, mu_col, rstd_col, stats, eps_col, width=D):
        nc = self.nc
        nchunk = (width + 511) // 512
        for i in range(nchunk):
            lo, hi = i * 512, min((i + 1) * 512, width)
            nc.vector.bn_stats(stats[:, 6 * i:6 * i + 6], x_ap[:, lo:hi])
        self.bar()
        nc.vector.bn_aggr(stats[:, 6 * nchunk:6 * nchunk + 2], stats[:, 0:6 * nchunk])
        self.bar()
        nc.vector.tensor_copy(mu_col[:], stats[:, 6 * nchunk:6 * nchunk + 1])
        nc.scalar.activation(rstd_col[:], stats[:, 6 * nchunk + 1:6 * nchunk + 2],
                            AF.Sqrt, bias=eps_col[:, 0:1])
        self.bar()
        nc.vector.reciprocal(rstd_col[:], rstd_col[:])



def build_L3F(em=None, io=None):
    standalone = em is None
    if standalone:
        em = Emit("l3f").start()
        x_in = em.din("x", [TOKC, D], F32)
        x_out = em.dout("xo", [TOKC, D], F32)
        io = {"x_tile": lambda r: x_in[r:r + 128, :],
              "xo_tile": lambda r: x_out[r:r + 128, :],
              "wg": em.din("wg", [D, 3072], I32),
              "wu": em.din("wu", [D, 3072], I32),
              "wd": em.din("wd", [3072, D], I32),
              "ident": em.din("ident", [128, 128], BF16)}
    nc = em.nc
    HID = 3072
    NH = HID // 128
    wg_in, wu_in, wd_in, ident_in = io["wg"], io["wu"], io["wd"], io["ident"]
    em.push_scope()

    GRP = 512
    NGRP = TOKC // GRP

    wg = em.sb([128, 8 * HID], BF16, "wg")
    wu = em.sb([128, 8 * HID], BF16, "wu")
    wd = em.sb([128, NH * D], BF16, "wd")
    ident = em.sb([128, 128], BF16, "id")
    xg = em.sb([128, D], F32, "xg")
    xn = em.sb([128, D], F32, "xn")
    xnb = em.sb([128, D], BF16, "xnb")
    xnT = em.sb([128, 8 * GRP], BF16, "xnT")
    hbuf = em.sb([128, NH * GRP], BF16, "h")
    tmp = em.sb([128, 2 * GRP], F32, "tmp")
    stats = em.sb([128, 16], F32, "st")
    mu = em.sb([128, 1], F32, "mu")
    rstd = em.sb([128, 1], F32, "rs")
    eps = em.const_col(LN_EPS)
    xo = em.sb([128, D], F32, "xo")

    pst = em.ps([128, 256], BF16, "pt")
    psA = [em.ps([128, 512], F32, "pa") for _ in range(2)]
    psB = [em.ps([128, 512], F32, "pb") for _ in range(2)]
    psO = em.ps([128, D], F32, "po")
    l3mm = em.sem("l3_mm")
    l3sl = em.sem("l3_sl")
    l3ev = em.sem("l3_ev")
    for key in ["l3_mm", "l3_sl", "l3_ev"]:
        em.counters.setdefault(key, 0)
    C3 = em.counters

    for k in range(8):
        em.dma(wg[:, k * HID:(k + 1) * HID], wg_in[k * 128:(k + 1) * 128, :], nc.gpsimd)
        em.dma(wu[:, k * HID:(k + 1) * HID], wu_in[k * 128:(k + 1) * 128, :], nc.gpsimd)
    for k in range(NH):
        em.dma(wd[:, k * D:(k + 1) * D], wd_in[k * 128:(k + 1) * 128, :], nc.gpsimd)
    em.dma(ident[:], ident_in)
    em.flush(nc.gpsimd)
    em.flush()

    s = 1.0 / 127.0
    for g in range(NGRP):
        for t in range(4):
            em.dma(xg[:], io["x_tile"]((g * 4 + t) * 128))
            em.flush()
            xt = xg[:]
            em.ln_cols(xt, mu, rstd, stats, eps)
            em.bar()
            nc.vector.tensor_scalar(xn[:], xt, mu[:, 0:1], rstd[:, 0:1],
                                    ALU.subtract, ALU.mult)
            em.bar()
            _transpose_to(em, xnT, xn[:], xnb, ident, pst, t * 128, GRP)
        em.bar()
        for hcH in range(NH):
            i = C3["l3_mm"]
            if i >= 2:
                nc.tensor.wait_ge(l3ev, i - 1)
            for kd in range(8):
                nc.tensor.matmul(psA[i % 2][:, 0:GRP],
                                 wg[:, kd * HID + hcH * 128:
                                    kd * HID + (hcH + 1) * 128],
                                 xnT[:, kd * GRP:(kd + 1) * GRP],
                                 start=(kd == 0), stop=(kd == 7))
            for kd in range(8):
                inst = nc.tensor.matmul(psB[i % 2][:, 0:GRP],
                                        wu[:, kd * HID + hcH * 128:
                                           kd * HID + (hcH + 1) * 128],
                                        xnT[:, kd * GRP:(kd + 1) * GRP],
                                        start=(kd == 0), stop=(kd == 7))
            inst.then_inc(l3mm, 1)
            C3["l3_mm"] += 1
            nc.scalar.wait_ge(l3mm, C3["l3_mm"])
            nc.scalar.activation(tmp[:, (i % 2) * GRP:(i % 2 + 1) * GRP],
                                 psA[i % 2][:, 0:GRP], AF.Silu,
                                 scale=s).then_inc(l3sl, 1)
            C3["l3_sl"] += 1
            nc.vector.wait_ge(l3sl, C3["l3_sl"])
            nc.vector.scalar_tensor_tensor(hbuf[:, hcH * GRP:(hcH + 1) * GRP],
                                           psB[i % 2][:, 0:GRP], s,
                                           tmp[:, (i % 2) * GRP:(i % 2 + 1) * GRP],
                                           ALU.mult, ALU.mult).then_inc(l3ev, 1)
            C3["l3_ev"] += 1
        em.bar()
        for t in range(4):
            em.dma(xg[:], io["x_tile"]((g * 4 + t) * 128))
            for kf in range(NH):
                for nn in range(2):
                    nc.tensor.matmul(
                        psO[:, nn * 512:(nn + 1) * 512],
                        hbuf[:, kf * GRP + t * 128:kf * GRP + (t + 1) * 128],
                        wd[:, kf * D + nn * 512:kf * D + (nn + 1) * 512],
                        start=(kf == 0), stop=(kf == NH - 1))
            em.flush()
            nc.vector.scalar_tensor_tensor(xo[:], psO[:, 0:D],
                                           0.5 * s, xg[:],
                                           ALU.mult, ALU.add)
            em.bar()
            em.dma(io["xo_tile"]((g * 4 + t) * 128), xo[:])
            em.flush()
    em.pop_scope()
    if standalone:
        em.close()
    return em


def run_L3F(x, wg, wu, wd):
    em = _cached("build_L3F", build_L3F)
    xf = x.reshape(T, D)
    wgT = np.ascontiguousarray(wg.T)
    wuT = np.ascontiguousarray(wu.T)
    wdT = np.ascontiguousarray(wd.T)
    ident = np.eye(128, dtype=np.float32).astype(BFNP)
    in_maps = [{"x": xf[c * TOKC:(c + 1) * TOKC], "wg": wgT, "wu": wuT,
                "wd": wdT, "ident": ident} for c in range(NCORES)]
    res = _run(em.nc, in_maps)
    out = np.concatenate([res.results[c]["xo"] for c in range(NCORES)], 0)
    return out.reshape(B, S, D)





def _ln_affine(em, out_ap, x_ap, gb, bb, mu, rstd, stats, eps, tmpf):
    nc = em.nc
    em.ln_cols(x_ap, mu, rstd, stats, eps)
    em.bar()
    nc.vector.tensor_scalar(tmpf[:], x_ap, mu[:, 0:1], rstd[:, 0:1],
                            ALU.subtract, ALU.mult)
    em.bar()
    nc.vector.tensor_tensor(tmpf[:], tmpf[:], gb[:], ALU.mult)
    em.bar()
    nc.vector.tensor_tensor(out_ap, tmpf[:], bb[:], ALU.add)
    em.bar()


def _bcast_row(em, dst_sb, row_sb, ones_row_f32, psum, width=D):
    nc = em.nc
    for nn in range(0, width, 512):
        w = min(512, width - nn)
        nc.tensor.matmul(psum[:, 0:w], ones_row_f32[:, 0:128], row_sb[0:1, nn:nn + w],
                         start=True, stop=True)
        em.bar()
        nc.vector.tensor_copy(dst_sb[:, nn:nn + w], psum[:, 0:w])
        em.bar()


def _transpose_to(em, dstT, src_f32, tmp_bf, ident, pst, tok_off, grp_w):
    nc = em.nc
    tr = em.sem("ttr")
    cp = em.sem("tcp")
    em.counters.setdefault("ttr", 0)
    em.counters.setdefault("tcp", 0)
    nc.vector.tensor_copy(tmp_bf[:], src_f32)
    em.bar()
    t0, c0 = em.counters["ttr"], em.counters["tcp"]
    for k in range(8):
        if k >= 1:
            nc.tensor.wait_ge(cp, c0 + k)
        nc.tensor.transpose(pst[:, (k % 2) * 128:(k % 2 + 1) * 128],
                            tmp_bf[:, k * 128:(k + 1) * 128],
                            ident[:]).then_inc(tr, 1)
        em.counters["ttr"] += 1
        nc.vector.wait_ge(tr, t0 + k + 1)
        nc.vector.tensor_copy(dstT[:, k * grp_w + tok_off:k * grp_w + tok_off + 128],
                              pst[:, (k % 2) * 128:(k % 2 + 1) * 128]).then_inc(cp, 1)
        em.counters["tcp"] += 1
    em.bar()


def build_L0F(em=None, io=None):
    standalone = em is None
    if standalone:
        em = Emit("l0f").start()
        x_in = em.din("x", [TOKC, D], F32)
        x_out = em.dout("xo", [TOKC, D], F32)
        io = {"x_tile": lambda r: x_in[r:r + 128, :],
              "xo_tile": lambda r: x_out[r:r + 128, :],
              "mixw": em.din("mixw", [D, D], F32),
              "router": em.din("router", [D, 8], F32),
              "w1": em.din("w1", [8, D, 4096], F32),
              "w2": em.din("w2", [8, 4096, D], F32),
              "rows": em.din("rows", [1, 5 * D], F32),
              "b1c": em.din("b1c", [128, 8 * 32], F32),
              "b2r": em.din("b2r", [1, 8 * D], F32),
              "ident": em.din("ident", [128, 128], BF16)}
    nc = em.nc
    NE, HID = 8, 4096
    mixw_in, router_in = io["mixw"], io["router"]
    w1_in, w2_in = io["w1"], io["w2"]
    rows_in, b1_in, b2_in, ident_in = io["rows"], io["b1c"], io["b2r"], io["ident"]
    em.push_scope()

    GRP = 512
    NGRP = TOKC // GRP
    NT = GRP // 128

    mixw = em.sb([128, 8 * D], BF16, "mixw")
    router = em.sb([128, 8 * NE], BF16, "rt")
    stage = em.sb([1, D], F32, "stage")
    mixb_row = em.sb([1, D], BF16, "mixbr")
    combT = em.sb([8, 4 * 128], BF16, "combT")
    b1c = em.sb([128, NE * 32], F32, "b1c")
    b2sb = em.sb([8, D], BF16, "b2sb")
    ident = em.sb([128, 128], BF16, "id")
    onesf = em.ones_row(128, F32)
    onesb = em.sb([1, 128], BF16, "onesb")
    g1b = em.sb([128, D], F32, "g1b")
    be1b = em.sb([128, D], F32, "be1b")
    g2b = em.sb([128, D], F32, "g2b")
    be2b = em.sb([128, D], F32, "be2b")
    xg = em.sb([128, D], F32, "xg")
    n0 = em.sb([128, D], F32, "n0")
    ln1 = em.sb([128, D], F32, "ln1")
    ln2 = em.sb([128, D], F32, "ln2")
    tmpf = em.sb([128, D], F32, "tmpf")
    tmpb = em.sb([128, D], BF16, "tmpb")
    lnT = em.sb([128, 8 * 128], BF16, "lnT")
    ln2T = em.sb([128, 8 * GRP], BF16, "ln2T")
    hbuf = em.sb([128, 32 * GRP], BF16, "h")
    outacc = em.sb([128, NT * D], F32, "oacc")
    comb = em.sb([128, NT * NE], F32, "comb")
    w1b = [em.sb([128, 8 * 512], BF16, "w1b") for _ in range(2)]
    w2full = em.sb([128, 32 * D], BF16, "w2f")
    stats = em.sb([128, 16], F32, "st")
    mu = em.sb([128, 1], F32, "mu")
    rstd = em.sb([128, 1], F32, "rs")
    eps = em.const_col(LN_EPS)
    colA = em.sb([128, 8], F32, "cA")
    colB = em.sb([128, 8], F32, "cB")
    colC = em.sb([128, 8], F32, "cC")
    ones8 = em.sb([128, 8], F32, "o8")
    ncol = em.sb([128, 1], F32, "ncol")
    scol = em.sb([128, 1], F32, "scol")

    pst = em.ps([128, 256], BF16, "pt")
    psR = em.ps([128, 512], F32, "pr")
    psA = [em.ps([128, 512], F32, "pa") for _ in range(2)]
    psO = [em.ps([128, D], F32, "po") for _ in range(2)]
    hmm_s = em.sem("moe_hmm")
    hev_s = em.sem("moe_hev")
    ymm_s = em.sem("moe_ymm")
    yev_s = em.sem("moe_yev")
    wld_s = em.sem("moe_wld")
    for key in ["moe_hmm", "moe_hev", "moe_ymm", "moe_yev", "moe_wld"]:
        em.counters.setdefault(key, 0)
    C = em.counters

    nc.vector.memset(ones8[:], 1.0)
    nc.vector.memset(onesb[:], 1.0)
    for k in range(8):
        em.dma(mixw[:, k * D:(k + 1) * D], mixw_in[k * 128:(k + 1) * 128, :], nc.gpsimd)
        em.dma(router[:, k * NE:(k + 1) * NE],
               router_in[k * 128:(k + 1) * 128, :], nc.gpsimd)
    em.dma(b1c[:], b1_in)
    em.dma(b2sb[:], b2_in.rearrange("o (e d) -> (o e) d", e=NE), nc.gpsimd)
    em.dma(mixb_row[:], rows_in[0:1, 4 * D:5 * D], nc.gpsimd)
    em.dma(ident[:], ident_in)
    em.flush(nc.gpsimd)
    em.flush()
    for i, dst in enumerate([g1b, be1b, g2b, be2b]):
        em.dma(stage[:], rows_in[0:1, i * D:(i + 1) * D])
        em.flush()
        _bcast_row(em, dst, stage[0:1, :], onesf, psR)

    def w1_chunk_ap(e, fc):
        return (w1_in[e].rearrange("(dc p) f -> p dc f", p=128)
                [:, :, fc * 512:(fc + 1) * 512])

    def w1load(buf, e, fc):
        em.dma(buf[:].rearrange("p (dc f) -> p dc f", dc=8), w1_chunk_ap(e, fc),
               nc.gpsimd)

    def w2load(e):
        for kc in range(8):
            em.dma(w2full[:, kc * 4 * D:(kc + 1) * 4 * D]
                   .rearrange("p (fc d) -> p fc d", fc=4),
                   w2_in[e][kc * 512:(kc + 1) * 512, :]
                   .rearrange("(fc p) d -> p fc d", p=128),
                   nc.gpsimd)

    for g in range(NGRP):
        for t in range(NT):
            row = (g * NT + t) * 128
            em.dma(xg[:], io["x_tile"](row))
            em.flush()
            em.ln_cols(xg[:], mu, rstd, stats, eps)
            em.bar()
            nc.vector.tensor_scalar(n0[:], xg[:], mu[:, 0:1], rstd[:, 0:1],
                                    ALU.subtract, ALU.mult)
            em.bar()
            _ln_affine(em, ln1[:], n0[:], g1b, be1b, mu, rstd, stats, eps, tmpf)
            _transpose_to(em, lnT, ln1[:], tmpb, ident, pst, 0, 128)
            for kd in range(8):
                for nn in range(2):
                    nc.tensor.matmul(psO[0][:, nn * 512:(nn + 1) * 512],
                                     lnT[:, kd * 128:(kd + 1) * 128],
                                     mixw[:, kd * D + nn * 512:kd * D + (nn + 1) * 512],
                                     start=(kd == 0), stop=False)
            for nn in range(2):
                nc.tensor.matmul(psO[0][:, nn * 512:(nn + 1) * 512], onesb[:, 0:128],
                                 mixb_row[0:1, nn * 512:(nn + 1) * 512],
                                 start=False, stop=True)
            em.bar()
            nc.vector.tensor_tensor(tmpf[:], n0[:], ln1[:], ALU.add)
            em.bar()
            nc.vector.tensor_tensor(outacc[:, t * D:(t + 1) * D], tmpf[:],
                                    psO[0][:, 0:D], ALU.add)
            em.bar()
            _ln_affine(em, ln2[:], outacc[:, t * D:(t + 1) * D], g2b, be2b,
                       mu, rstd, stats, eps, tmpf)
            _transpose_to(em, ln2T, ln2[:], tmpb, ident, pst, t * 128, GRP)
            for kd in range(8):
                nc.tensor.matmul(psR[:, 0:NE],
                                 ln2T[:, kd * GRP + t * 128:kd * GRP + (t + 1) * 128],
                                 router[:, kd * NE:(kd + 1) * NE],
                                 start=(kd == 0), stop=(kd == 7))
            em.bar()
            nc.vector.reduce_max(ncol[:], psR[:, 0:NE], axis=AX.X, negate=True)
            em.bar()
            nc.scalar.activation(colA[:], psR[:, 0:NE], AF.Exp, bias=ncol[:, 0:1],
                                 accum_out=scol[:, 0:1])
            em.bar()
            nc.vector.reciprocal(ncol[:], scol[:])
            em.bar()
            nc.vector.tensor_scalar(colA[:], colA[:], ncol[:, 0:1], None, ALU.mult)
            em.bar()
            nc.vector.reduce_max(ncol[:], colA[:], axis=AX.X)
            em.bar()
            nc.vector.tensor_scalar(colB[:], colA[:], ncol[:, 0:1], None, ALU.is_ge)
            em.bar()
            nc.vector.tensor_tensor_scan(colC[:], ones8[:], colB[:], 0.0,
                                         ALU.mult, ALU.add)
            em.bar()
            nc.vector.tensor_scalar(colC[:], colC[:], 1.0, None, ALU.is_le)
            em.bar()
            nc.vector.tensor_tensor(colB[:], colB[:], colC[:], ALU.mult)
            em.bar()
            cslot = comb[:, t * NE:(t + 1) * NE]
            nc.vector.tensor_scalar(cslot, colB[:], ncol[:, 0:1], None, ALU.mult)
            em.bar()
            nc.vector.tensor_tensor(colC[:], colA[:], colB[:], ALU.mult)
            em.bar()
            nc.vector.tensor_tensor(colA[:], colA[:], colC[:], ALU.subtract)
            em.bar()
            nc.vector.reduce_max(ncol[:], colA[:], axis=AX.X)
            em.bar()
            nc.vector.tensor_scalar(colB[:], colA[:], ncol[:, 0:1], None, ALU.is_ge)
            em.bar()
            nc.vector.tensor_tensor_scan(colC[:], ones8[:], colB[:], 0.0,
                                         ALU.mult, ALU.add)
            em.bar()
            nc.vector.tensor_scalar(colC[:], colC[:], 1.0, None, ALU.is_le)
            em.bar()
            nc.vector.tensor_tensor(colB[:], colB[:], colC[:], ALU.mult)
            em.bar()
            nc.vector.scalar_tensor_tensor(cslot, colB[:], ncol[:, 0:1], cslot,
                                           ALU.mult, ALU.add)
            em.bar()
            nc.vector.tensor_copy(tmpb[:, 0:NE], cslot)
            em.bar()
            nc.tensor.transpose(pst[0:NE, 0:128], tmpb[:, 0:NE], ident[:])
            em.bar()
            nc.vector.tensor_copy(combT[:, t * 128:(t + 1) * 128], pst[0:NE, 0:128])
            em.bar()
        em.bar()

        def wmark():
            C["moe_wld"] = em.dcount

        def wwait():
            nc.tensor.wait_ge(em.dsem, C["moe_wld"])

        w1load(w1b[0], 0, 0)
        wmark()
        for e in range(NE):
            if C["moe_ymm"] > 0:
                nc.gpsimd.wait_ge(yev_s, C["moe_ymm"])
            w2load(e)
            w2mark = em.dcount
            for fc in range(8):
                buf = w1b[fc % 2]
                wwait()
                nxt_e, nxt_fc = e, fc + 1
                if nxt_fc == 8:
                    nxt_e, nxt_fc = e + 1, 0
                if C["moe_hmm"] > 0:
                    nc.gpsimd.wait_ge(hmm_s, C["moe_hmm"])
                if nxt_e < NE:
                    w1load(w1b[(fc + 1) % 2], nxt_e, nxt_fc)
                elif g + 1 < NGRP:
                    w1load(w1b[(fc + 1) % 2], 0, 0)
                wmark()
                for fs in range(4):
                    i = C["moe_hmm"]
                    if fc == 0 and fs == 0 and C["moe_ymm"] > 0:
                        nc.scalar.wait_ge(ymm_s, C["moe_ymm"])
                    if i >= 2:
                        nc.tensor.wait_ge(hev_s, i - 1)
                    for kd in range(8):
                        inst = nc.tensor.matmul(
                            psA[i % 2][:, 0:512],
                            buf[:, kd * 512 + fs * 128:kd * 512 + (fs + 1) * 128],
                            ln2T[:, kd * GRP:(kd + 1) * GRP],
                            start=(kd == 0), stop=(kd == 7))
                    inst.then_inc(hmm_s, 1)
                    C["moe_hmm"] += 1
                    fcol = b1c[:, e * 32 + fc * 4 + fs:e * 32 + fc * 4 + fs + 1]
                    nc.scalar.wait_ge(hmm_s, C["moe_hmm"])
                    nc.scalar.activation(
                        hbuf[:, (fc * 4 + fs) * GRP:(fc * 4 + fs + 1) * GRP],
                        psA[i % 2][:, 0:512], AF.Silu,
                        bias=fcol).then_inc(hev_s, 1)
            nc.tensor.wait_ge(hev_s, C["moe_hmm"])
            nc.tensor.wait_ge(em.dsem, w2mark)
            for tc in range(NT):
                i = C["moe_ymm"]
                if i >= 2:
                    nc.tensor.wait_ge(yev_s, i - 1)
                for kf in range(32):
                    for nn in range(2):
                        inst = nc.tensor.matmul(
                            psO[i % 2][:, nn * 512:(nn + 1) * 512],
                            hbuf[:, kf * GRP + tc * 128:kf * GRP + (tc + 1) * 128],
                            w2full[:, kf * D + nn * 512:kf * D + (nn + 1) * 512],
                            start=(kf == 0), stop=(kf == 31))
                inst.then_inc(ymm_s, 1)
                C["moe_ymm"] += 1
                ccol = comb[:, tc * NE + e:tc * NE + e + 1]
                nc.vector.wait_ge(ymm_s, C["moe_ymm"])
                nc.vector.scalar_tensor_tensor(outacc[:, tc * D:(tc + 1) * D],
                                               psO[i % 2][:, 0:D], ccol,
                                               outacc[:, tc * D:(tc + 1) * D],
                                               ALU.mult,
                                               ALU.add).then_inc(yev_s, 1)
        em.bar()
        for tc in range(NT):
            for nn in range(2):
                nc.tensor.matmul(psO[0][:, nn * 512:(nn + 1) * 512],
                                 combT[:, tc * 128:(tc + 1) * 128],
                                 b2sb[:, nn * 512:(nn + 1) * 512],
                                 start=True, stop=True)
            em.bar()
            nc.vector.tensor_tensor(outacc[:, tc * D:(tc + 1) * D],
                                    outacc[:, tc * D:(tc + 1) * D], psO[0][:, 0:D],
                                    ALU.add)
            em.bar()
        for t in range(NT):
            row = (g * NT + t) * 128
            em.dma(xg[:], io["x_tile"](row))
            em.flush()
            nc.vector.scalar_tensor_tensor(tmpf[:], outacc[:, t * D:(t + 1) * D],
                                           0.5, xg[:], ALU.mult, ALU.add)
            em.bar()
            em.dma(io["xo_tile"](row), tmpf[:])
            em.flush()
    em.pop_scope()
    if standalone:
        em.close()
    return em


def run_L0F(x, g1, be1, mix_w, mix_b, g2, be2, router_w, w1, b1, w2, b2):
    em = _cached("build_L0F", build_L0F)
    xf = x.reshape(T, D)
    rows = np.concatenate([g1, be1, g2, be2, mix_b]).astype(np.float32)[None, :]
    b1c = np.ascontiguousarray(b1.reshape(8, 32, 128).transpose(2, 0, 1).reshape(128, 256))
    ident = np.eye(128, dtype=np.float32).astype(BFNP)
    common = {"mixw": mix_w, "router": router_w, "w1": w1, "w2": w2,
              "rows": rows, "b1c": b1c, "b2r": np.ascontiguousarray(b2.reshape(1, -1)), "ident": ident}
    in_maps = [{"x": xf[c * TOKC:(c + 1) * TOKC], **common} for c in range(NCORES)]
    res = _run(em.nc, in_maps)
    out = np.concatenate([res.results[c]["xo"] for c in range(NCORES)], 0)
    return out.reshape(B, S, D)



def build_L2F(em=None, io=None):
    standalone = em is None
    if standalone:
        em = Emit("l2f").start()
        x_in = em.din("x", [TOKC, D], F32)
        x_out = em.dout("xo", [TOKC, D], F32)
        io = {"x_tile": lambda r: x_in[r:r + 128, :],
              "xo_tile": lambda r: x_out[r:r + 128, :],
              "gatew": em.din("gatew", [D, 8], F32),
              "w1": em.din("w1", [9, D, 4096], F32),
              "w2": em.din("w2", [9, 4096, D], F32),
              "ident": em.din("ident", [128, 128], BF16)}
    nc = em.nc
    NE, HID = 9, 4096
    gate_in, w1_in, w2_in, ident_in = io["gatew"], io["w1"], io["w2"], io["ident"]
    em.push_scope()

    GRP = 512
    NGRP = TOKC // GRP
    NT = GRP // 128

    gatew = em.sb([128, 8 * 8], BF16, "gw")
    ident = em.sb([128, 128], BF16, "id")
    xg = em.sb([128, D], F32, "xg")
    n0 = em.sb([128, D], F32, "n0")
    tmpf = em.sb([128, D], F32, "tmpf")
    tmpb = em.sb([128, D], BF16, "tmpb")
    nT = em.sb([128, 8 * GRP], BF16, "nT")
    hbuf = em.sb([128, 32 * GRP], BF16, "h")
    outacc = em.sb([128, NT * D], F32, "oacc")
    comb = em.sb([128, NT * NE], F32, "comb")
    w1b = [em.sb([128, 8 * 512], BF16, "w1b") for _ in range(2)]
    w2full = em.sb([128, 32 * D], BF16, "w2f")
    stats = em.sb([128, 16], F32, "st")
    mu = em.sb([128, 1], F32, "mu")
    rstd = em.sb([128, 1], F32, "rs")
    eps = em.const_col(LN_EPS)
    colA = em.sb([128, 8], F32, "cA")
    colB = em.sb([128, 8], F32, "cB")
    colC = em.sb([128, 8], F32, "cC")
    ones8 = em.sb([128, 8], F32, "o8")
    ncol = em.sb([128, 1], F32, "ncol")
    scol = em.sb([128, 1], F32, "scol")
    g1col = em.sb([128, 1], F32, "g1c")

    pst = em.ps([128, 256], BF16, "pt")
    psR = em.ps([128, 512], F32, "pr")
    psA = [em.ps([128, 512], F32, "pa") for _ in range(2)]
    psO = [em.ps([128, D], F32, "po") for _ in range(2)]
    hmm_s = em.sem("moe_hmm")
    hev_s = em.sem("moe_hev")
    ymm_s = em.sem("moe_ymm")
    yev_s = em.sem("moe_yev")
    for key in ["moe_hmm", "moe_hev", "moe_ymm", "moe_yev"]:
        em.counters.setdefault(key, 0)
    C = em.counters

    nc.vector.memset(ones8[:], 1.0)
    for k in range(8):
        em.dma(gatew[:, k * 8:(k + 1) * 8], gate_in[k * 128:(k + 1) * 128, :],
               nc.gpsimd)
    em.dma(ident[:], ident_in)
    em.flush(nc.gpsimd)
    em.flush()

    def w1load(buf, e, fc):
        em.dma(buf[:].rearrange("p (dc f) -> p dc f", dc=8),
               w1_in[e].rearrange("(dc p) f -> p dc f", p=128)
               [:, :, fc * 512:(fc + 1) * 512], nc.gpsimd)

    def w2load(e):
        for kc in range(8):
            em.dma(w2full[:, kc * 4 * D:(kc + 1) * 4 * D]
                   .rearrange("p (fc d) -> p fc d", fc=4),
                   w2_in[e][kc * 512:(kc + 1) * 512, :]
                   .rearrange("(fc p) d -> p fc d", p=128),
                   nc.gpsimd)

    for g in range(NGRP):
        for t in range(NT):
            row = (g * NT + t) * 128
            em.dma(xg[:], io["x_tile"](row))
            em.flush()
            em.ln_cols(xg[:], mu, rstd, stats, eps)
            em.bar()
            nc.vector.tensor_scalar(n0[:], xg[:], mu[:, 0:1], rstd[:, 0:1],
                                    ALU.subtract, ALU.mult)
            em.bar()
            _transpose_to(em, nT, n0[:], tmpb, ident, pst, t * 128, GRP)
            for kd in range(8):
                nc.tensor.matmul(psR[:, 0:8],
                                 nT[:, kd * GRP + t * 128:kd * GRP + (t + 1) * 128],
                                 gatew[:, kd * 8:(kd + 1) * 8],
                                 start=(kd == 0), stop=(kd == 7))
            em.bar()
            nc.vector.tensor_copy(colA[:], psR[:, 0:8])
            em.bar()
            nc.vector.reduce_max(ncol[:], colA[:], axis=AX.X, negate=True)
            em.bar()
            nc.vector.tensor_scalar(colB[:], colA[:], ncol[:, 0:1], None, ALU.add)
            em.bar()
            nc.vector.tensor_scalar(colB[:], colB[:], 0.0, None, ALU.is_ge)
            em.bar()
            nc.vector.tensor_tensor_scan(colC[:], ones8[:], colB[:], 0.0,
                                         ALU.mult, ALU.add)
            em.bar()
            nc.vector.tensor_scalar(colC[:], colC[:], 1.0, None, ALU.is_le)
            em.bar()
            nc.vector.tensor_tensor(colB[:], colB[:], colC[:], ALU.mult)
            em.bar()
            nc.vector.tensor_scalar(colC[:], colB[:], -1e30, None, ALU.mult)
            em.bar()
            nc.vector.tensor_tensor(colC[:], colA[:], colC[:], ALU.add)
            em.bar()
            nc.vector.reduce_max(scol[:], colC[:], axis=AX.X)
            em.bar()
            nc.scalar.activation(g1col[:], scol[:], AF.Exp, bias=ncol[:, 0:1])
            em.bar()
            nc.vector.tensor_scalar(g1col[:], g1col[:], 1.0, None, ALU.add)
            em.bar()
            nc.vector.reciprocal(g1col[:], g1col[:])
            em.bar()
            nc.vector.tensor_scalar(colC[:], colC[:], scol[:, 0:1], None, ALU.is_ge)
            em.bar()
            nc.vector.tensor_tensor_scan(colA[:], ones8[:], colC[:], 0.0,
                                         ALU.mult, ALU.add)
            em.bar()
            nc.vector.tensor_scalar(colA[:], colA[:], 1.0, None, ALU.is_le)
            em.bar()
            nc.vector.tensor_tensor(colC[:], colC[:], colA[:], ALU.mult)
            em.bar()
            cslot = comb[:, t * NE:t * NE + 8]
            nc.vector.tensor_scalar(cslot, colB[:], g1col[:, 0:1], None, ALU.mult)
            em.bar()
            nc.vector.tensor_scalar(scol[:], g1col[:], -1.0, 1.0, ALU.mult, ALU.add)
            em.bar()
            nc.vector.scalar_tensor_tensor(cslot, colC[:], scol[:, 0:1], cslot,
                                           ALU.mult, ALU.add)
            em.bar()
            nc.vector.memset(comb[:, t * NE + 8:t * NE + 9], 1.0)
            em.bar()
        em.bar()

        def wmark():
            C["moe_wld"] = em.dcount

        def wwait():
            nc.tensor.wait_ge(em.dsem, C["moe_wld"])

        C.setdefault("moe_wld", 0)
        w1load(w1b[0], 0, 0)
        wmark()
        for e in range(NE):
            if C["moe_ymm"] > 0:
                nc.gpsimd.wait_ge(yev_s, C["moe_ymm"])
            w2load(e)
            w2mark = em.dcount
            for fc in range(8):
                buf = w1b[fc % 2]
                wwait()
                nxt_e, nxt_fc = e, fc + 1
                if nxt_fc == 8:
                    nxt_e, nxt_fc = e + 1, 0
                if C["moe_hmm"] > 0:
                    nc.gpsimd.wait_ge(hmm_s, C["moe_hmm"])
                if nxt_e < NE:
                    w1load(w1b[(fc + 1) % 2], nxt_e, nxt_fc)
                elif g + 1 < NGRP:
                    w1load(w1b[(fc + 1) % 2], 0, 0)
                wmark()
                for fs in range(4):
                    i = C["moe_hmm"]
                    if fc == 0 and fs == 0 and C["moe_ymm"] > 0:
                        nc.scalar.wait_ge(ymm_s, C["moe_ymm"])
                    if i >= 2:
                        nc.tensor.wait_ge(hev_s, i - 1)
                    for kd in range(8):
                        inst = nc.tensor.matmul(
                            psA[i % 2][:, 0:512],
                            buf[:, kd * 512 + fs * 128:kd * 512 + (fs + 1) * 128],
                            nT[:, kd * GRP:(kd + 1) * GRP],
                            start=(kd == 0), stop=(kd == 7))
                    inst.then_inc(hmm_s, 1)
                    C["moe_hmm"] += 1
                    nc.scalar.wait_ge(hmm_s, C["moe_hmm"])
                    nc.scalar.activation(
                        hbuf[:, (fc * 4 + fs) * GRP:(fc * 4 + fs + 1) * GRP],
                        psA[i % 2][:, 0:512], AF.Silu).then_inc(hev_s, 1)
            nc.tensor.wait_ge(hev_s, C["moe_hmm"])
            nc.tensor.wait_ge(em.dsem, w2mark)
            for tc in range(NT):
                i = C["moe_ymm"]
                if i >= 2:
                    nc.tensor.wait_ge(yev_s, i - 1)
                for kf in range(32):
                    for nn in range(2):
                        inst = nc.tensor.matmul(
                            psO[i % 2][:, nn * 512:(nn + 1) * 512],
                            hbuf[:, kf * GRP + tc * 128:kf * GRP + (tc + 1) * 128],
                            w2full[:, kf * D + nn * 512:kf * D + (nn + 1) * 512],
                            start=(kf == 0), stop=(kf == 31))
                inst.then_inc(ymm_s, 1)
                C["moe_ymm"] += 1
                ccol = comb[:, tc * NE + e:tc * NE + e + 1]
                nc.vector.wait_ge(ymm_s, C["moe_ymm"])
                if e == 0:
                    nc.vector.tensor_scalar(outacc[:, tc * D:(tc + 1) * D],
                                            psO[i % 2][:, 0:D], ccol, None,
                                            ALU.mult).then_inc(yev_s, 1)
                else:
                    nc.vector.scalar_tensor_tensor(outacc[:, tc * D:(tc + 1) * D],
                                                   psO[i % 2][:, 0:D], ccol,
                                                   outacc[:, tc * D:(tc + 1) * D],
                                                   ALU.mult,
                                                   ALU.add).then_inc(yev_s, 1)
        em.bar()
        for t in range(NT):
            row = (g * NT + t) * 128
            em.dma(xg[:], io["x_tile"](row))
            em.flush()
            nc.vector.scalar_tensor_tensor(tmpf[:], outacc[:, t * D:(t + 1) * D],
                                           0.5, xg[:], ALU.mult, ALU.add)
            em.bar()
            em.dma(io["xo_tile"](row), tmpf[:])
            em.flush()
    em.pop_scope()
    if standalone:
        em.close()
    return em


def run_L2F(x, shared_w1, shared_w2, experts_w1, experts_w2, gate_w):
    em = _cached("build_L2F", build_L2F)
    xf = x.reshape(T, D)
    w1_all = np.concatenate([experts_w1.transpose(0, 2, 1),
                             shared_w1.T[None]], 0)
    w1_all = np.ascontiguousarray(w1_all)
    w2_all = np.concatenate([experts_w2.transpose(0, 2, 1),
                             shared_w2.T[None]], 0)
    w2_all = np.ascontiguousarray(w2_all)
    ident = np.eye(128, dtype=np.float32).astype(BFNP)
    common = {"gatew": np.ascontiguousarray(gate_w.T), "w1": w1_all,
              "w2": w2_all, "ident": ident}
    in_maps = [{"x": xf[c * TOKC:(c + 1) * TOKC], **common} for c in range(NCORES)]
    res = _run(em.nc, in_maps)
    out = np.concatenate([res.results[c]["xo"] for c in range(NCORES)], 0)
    return out.reshape(B, S, D)



def build_L2A(em=None, io=None):
    POSC = TOKC // 2
    F3 = 3072
    standalone = em is None
    if standalone:
        em = Emit("l2a").start()
        x0_in = em.din("x0", [POSC, D], F32)
        x1_in = em.din("x1", [POSC, D], F32)
        xo0 = em.dout("xo0", [POSC, D], F32)
        xo1 = em.dout("xo1", [POSC, D], F32)
        io = {"x_tile": lambda b, r: (x0_in if b == 0 else x1_in)[r:r + 128, :],
              "xo_tile": lambda b, r: (xo0 if b == 0 else xo1)[r:r + 128, :],
              "win": em.din("win", [D, F3], F32),
              "binr": em.din("binr", [1, F3], F32),
              "wout": em.din("wout", [D, D], F32),
              "boutr": em.din("boutr", [1, D], F32),
              "ident": em.din("ident", [128, 128], BF16)}
    nc = em.nc
    win_in, bin_in = io["win"], io["binr"]
    wout_in, bout_in, ident_in = io["wout"], io["boutr"], io["ident"]
    em.push_scope()

    win = em.sb([128, 8 * F3], BF16, "win")
    wout = em.sb([128, 8 * D], BF16, "wout")
    binb = em.sb([128, F3], F32, "binb")
    boutb = em.sb([128, D], F32, "boutb")
    stage = em.sb([1, F3], F32, "stage")
    ident = em.sb([128, 128], BF16, "id")
    onesf = em.ones_row(128, F32)
    xg = [em.sb([128, D], F32, "xg") for _ in range(2)]
    xn = em.sb([128, D], F32, "xn")
    tmpb = em.sb([128, D], BF16, "tmpb")
    xnT = em.sb([128, 8 * 128], BF16, "xnT")
    qkv = [em.sb([128, F3], F32, "qkv") for _ in range(2)]
    tmpf = em.sb([128, D], F32, "tmpf")
    sc = em.sb([128, 4 * 16], F32, "sc")
    pcol = em.sb([128, 4 * 16], F32, "pcol")
    ctx = em.sb([128, D], F32, "ctx")
    stats = em.sb([128, 16], F32, "st")
    mu = em.sb([128, 1], F32, "mu")
    rstd = em.sb([128, 1], F32, "rs")
    eps = em.const_col(LN_EPS)

    pst = em.ps([128, 256], BF16, "pt")
    psQ = em.ps([128, 512], F32, "pq")
    psO = em.ps([128, D], F32, "po")

    for k in range(8):
        em.dma(win[:, k * F3:(k + 1) * F3], win_in[k * 128:(k + 1) * 128, :],
               nc.gpsimd)
        em.dma(wout[:, k * D:(k + 1) * D], wout_in[k * 128:(k + 1) * 128, :],
               nc.gpsimd)
    em.dma(ident[:], ident_in)
    em.dma(stage[:], bin_in)
    em.flush(nc.gpsimd)
    em.flush()
    for nn in range(6):
        nc.tensor.matmul(psQ[:, 0:512], onesf[:, 0:128],
                         stage[0:1, nn * 512:(nn + 1) * 512], start=True, stop=True)
        em.bar()
        nc.vector.tensor_copy(binb[:, nn * 512:(nn + 1) * 512], psQ[:, 0:512])
        em.bar()
    em.dma(stage[0:1, 0:D], bout_in)
    em.flush()
    for nn in range(2):
        nc.tensor.matmul(psQ[:, 0:512], onesf[:, 0:128],
                         stage[0:1, nn * 512:(nn + 1) * 512], start=True, stop=True)
        em.bar()
        nc.vector.tensor_copy(boutb[:, nn * 512:(nn + 1) * 512], psQ[:, 0:512])
        em.bar()

    NTP = POSC // 128
    for t in range(NTP):
        row = t * 128
        for b in range(2):
            em.dma(xg[b][:], io["x_tile"](b, row))
            em.flush()
            em.ln_cols(xg[b][:], mu, rstd, stats, eps)
            em.bar()
            nc.vector.tensor_scalar(xn[:], xg[b][:], mu[:, 0:1], rstd[:, 0:1],
                                    ALU.subtract, ALU.mult)
            em.bar()
            _transpose_to(em, xnT, xn[:], tmpb, ident, pst, 0, 128)
            for nn in range(6):
                for kd in range(8):
                    nc.tensor.matmul(psQ[:, 0:512],
                                     xnT[:, kd * 128:(kd + 1) * 128],
                                     win[:, kd * F3 + nn * 512:kd * F3 + (nn + 1) * 512],
                                     start=(kd == 0), stop=(kd == 7))
                em.bar()
                nc.vector.tensor_tensor(qkv[b][:, nn * 512:(nn + 1) * 512],
                                        psQ[:, 0:512], binb[:, nn * 512:(nn + 1) * 512],
                                        ALU.add)
                em.bar()
        for s in range(2):
            for tt in range(2):
                nc.vector.tensor_tensor(tmpf[:], qkv[s][:, 0:D],
                                        qkv[tt][:, D:2 * D], ALU.mult)
                em.bar()
                nc.vector.reduce_sum(sc[:, (s * 2 + tt) * 16:(s * 2 + tt + 1) * 16],
                                     tmpf[:].rearrange("p (h d) -> p h d", h=16),
                                     axis=AX.X)
                em.bar()
        nc.vector.tensor_scalar(sc[:], sc[:], 0.125, None, ALU.mult)
        em.bar()
        for s in range(2):
            nc.vector.tensor_tensor(pcol[:, (s * 2) * 16:(s * 2 + 1) * 16],
                                    sc[:, (s * 2) * 16:(s * 2 + 1) * 16],
                                    sc[:, (s * 2 + 1) * 16:(s * 2 + 2) * 16],
                                    ALU.subtract)
            em.bar()
            nc.scalar.activation(pcol[:, (s * 2) * 16:(s * 2 + 1) * 16],
                                 pcol[:, (s * 2) * 16:(s * 2 + 1) * 16], AF.Sigmoid)
            em.bar()
            nc.vector.tensor_scalar(pcol[:, (s * 2 + 1) * 16:(s * 2 + 2) * 16],
                                    pcol[:, (s * 2) * 16:(s * 2 + 1) * 16],
                                    -1.0, 1.0, ALU.mult, ALU.add)
            em.bar()
        for s in range(2):
            for h in range(16):
                nc.vector.tensor_scalar(ctx[:, h * 64:(h + 1) * 64],
                                        qkv[0][:, 2 * D + h * 64:2 * D + (h + 1) * 64],
                                        pcol[:, s * 32 + h:s * 32 + h + 1],
                                        None, ALU.mult)
            em.bar()
            for h in range(16):
                nc.vector.scalar_tensor_tensor(
                    ctx[:, h * 64:(h + 1) * 64],
                    qkv[1][:, 2 * D + h * 64:2 * D + (h + 1) * 64],
                    pcol[:, s * 32 + 16 + h:s * 32 + 16 + h + 1],
                    ctx[:, h * 64:(h + 1) * 64], ALU.mult, ALU.add)
            em.bar()
            _transpose_to(em, xnT, ctx[:], tmpb, ident, pst, 0, 128)
            for nn in range(2):
                for kd in range(8):
                    nc.tensor.matmul(psO[:, nn * 512:(nn + 1) * 512],
                                     xnT[:, kd * 128:(kd + 1) * 128],
                                     wout[:, kd * D + nn * 512:kd * D + (nn + 1) * 512],
                                     start=(kd == 0), stop=(kd == 7))
            em.bar()
            nc.vector.tensor_tensor(tmpf[:], psO[:, 0:D], boutb[:], ALU.add)
            em.bar()
            nc.vector.tensor_tensor(tmpf[:], tmpf[:], xg[s][:], ALU.add)
            em.bar()
            em.dma(io["xo_tile"](s, row), tmpf[:])
            em.flush()
    em.pop_scope()
    if standalone:
        em.close()
    return em


def run_L2A(x, in_proj_weight, in_proj_bias, out_proj_weight, out_proj_bias):
    em = _cached("build_L2A", build_L2A)
    H, hd = 16, 64
    perm = np.concatenate([
        np.concatenate([np.arange(h * 192 + c * 64, h * 192 + c * 64 + 64)
                        for h in range(H)])
        for c in range(3)])
    winp = np.ascontiguousarray(in_proj_weight[perm, :].T)
    binp = np.ascontiguousarray(in_proj_bias[perm])[None, :]
    woutp = np.ascontiguousarray(out_proj_weight.T)
    boutp = np.ascontiguousarray(out_proj_bias)[None, :]
    ident = np.eye(128, dtype=np.float32).astype(BFNP)
    POSC = TOKC // 2
    common = {"win": winp, "binr": binp, "wout": woutp, "boutr": boutp,
              "ident": ident}
    in_maps = [{"x0": x[0, c * POSC:(c + 1) * POSC], "x1": x[1, c * POSC:(c + 1) * POSC],
                **common} for c in range(NCORES)]
    res = _run(em.nc, in_maps)
    o0 = np.concatenate([res.results[c]["xo0"] for c in range(NCORES)], 0)
    o1 = np.concatenate([res.results[c]["xo1"] for c in range(NCORES)], 0)
    return np.stack([o0, o1], 0)



def build_L0A_qkv():
    em = Emit("l0aq").start()
    nc = em.nc
    F3 = 3072
    x_in = em.din("x", [TOKC, D], F32)
    w_in = em.din("wqkv", [D, F3], F32)
    gb_in = em.din("gbrow", [1, 2 * D], F32)
    ident_in = em.din("ident", [128, 128], BF16)
    qkv_out = em.dout("qkv", [TOKC, F3], BF16)

    w = em.sb([128, 8 * F3], BF16, "w")
    ident = em.sb([128, 128], BF16, "id")
    onesf = em.ones_row(128, F32)
    stage = em.sb([1, D], F32, "stage")
    gmb = em.sb([128, D], F32, "gmb")
    btb = em.sb([128, D], F32, "btb")
    xg = em.sb([128, D], F32, "xg")
    xn = em.sb([128, D], F32, "xn")
    tmpb = em.sb([128, D], BF16, "tmpb")
    xnT = em.sb([128, 8 * 128], BF16, "xnT")
    ob = em.sb([128, F3], BF16, "ob")
    stats = em.sb([128, 16], F32, "st")
    mu = em.sb([128, 1], F32, "mu")
    rstd = em.sb([128, 1], F32, "rs")
    eps = em.const_col(LN_EPS)
    pst = em.ps([128, 256], BF16, "pt")
    psQ = em.ps([128, 512], F32, "pq")

    for k in range(8):
        em.dma(w[:, k * F3:(k + 1) * F3], w_in[k * 128:(k + 1) * 128, :], nc.gpsimd)
    em.dma(ident[:], ident_in)
    em.flush(nc.gpsimd)
    em.flush()
    for i, dst in enumerate([gmb, btb]):
        em.dma(stage[:], gb_in[0:1, i * D:(i + 1) * D])
        em.flush()
        _bcast_row(em, dst, stage[0:1, :], onesf, psQ)
    for t in range(TOKC // 128):
        em.dma(xg[:], x_in[t * 128:(t + 1) * 128, :])
        em.flush()
        em.ln_cols(xg[:], mu, rstd, stats, eps)
        em.bar()
        nc.vector.tensor_scalar(xn[:], xg[:], mu[:, 0:1], rstd[:, 0:1],
                                ALU.subtract, ALU.mult)
        em.bar()
        _ln_affine(em, xn[:], xn[:], gmb, btb, mu, rstd, stats, eps, xg)
        _transpose_to(em, xnT, xn[:], tmpb, ident, pst, 0, 128)
        for nn in range(6):
            for kd in range(8):
                nc.tensor.matmul(psQ[:, 0:512],
                                 xnT[:, kd * 128:(kd + 1) * 128],
                                 w[:, kd * F3 + nn * 512:kd * F3 + (nn + 1) * 512],
                                 start=(kd == 0), stop=(kd == 7))
            em.bar()
            nc.vector.tensor_copy(ob[:, nn * 512:(nn + 1) * 512], psQ[:, 0:512])
            em.bar()
        em.dma(qkv_out[t * 128:(t + 1) * 128, :], ob[:])
        em.flush()
    em.close()
    return em


def build_L0A_attn():
    em = Emit("l0aa").start()
    nc = em.nc
    NP = 4
    SQ = S
    NQC = SQ // 128
    q_in = em.din("qf", [NP, 64, SQ], BF16)
    k_in = em.din("kf", [NP, 64, SQ], BF16)
    v_in = em.din("vt", [NP, SQ, 64], BF16)
    x_in = em.din("xs", [NP, SQ, 64], F32)
    mask_in = em.din("mask", [128, 128], F32)
    ident_in = em.din("ident", [128, 128], BF16)
    xo = em.dout("xo", [NP, SQ, 64], F32)

    qsb = em.sb([64, SQ], BF16, "q")
    ksb = em.sb([64, SQ], BF16, "k")
    vsb = em.sb([128, 64 * 64], BF16, "v")
    xsb = em.sb([128, 64 * 64], F32, "xs")
    mask = em.sb([128, 128], F32, "mask")
    ident = em.sb([128, 128], BF16, "id")
    S_sb = em.sb([128, SQ], F32, "S")
    P_sb = em.sb([128, SQ], BF16, "P")
    ptb = em.sb([128, 2 * 512], BF16, "ptb")
    osb = em.sb([128, 64], F32, "o")
    negm = em.sb([128, 1], F32, "nm")
    lcol = em.sb([128, 1], F32, "l")
    psS = [em.ps([128, 512], F32, "ps") for _ in range(2)]
    pstg = [em.ps([128, 512], BF16, "pg") for _ in range(2)]
    psO = em.ps([128, 64], F32, "pO")

    em.dma(mask[:], mask_in)
    em.dma(ident[:], ident_in)
    em.flush()

    mm_sem = em.sem("mm")
    cp_sem = em.sem("cp")
    tr_sem = em.sem("tr")
    c2_sem = em.sem("c2")
    pv_sem = em.sem("pv")
    n_mm = [0]; n_cp = [0]; n_tr = [0]; n_c2 = [0]; n_pv = [0]

    for p in range(NP):
        em.dma(qsb[:], q_in[p], nc.sync)
        em.dma(ksb[:], k_in[p], nc.sync)
        em.dma(vsb[:].rearrange("p (kb d) -> p kb d", kb=64),
               v_in[p].rearrange("(kb p) d -> p kb d", p=128), nc.sync)
        em.dma(xsb[:].rearrange("p (kb d) -> p kb d", kb=64),
               x_in[p].rearrange("(kb p) d -> p kb d", p=128), nc.sync)
        em.flush()
        for qc in range(NQC):
            L = (qc + 1) * 128
            nchunks = (L + 511) // 512
            cp0, mm0 = n_cp[0], n_mm[0]
            for i in range(nchunks):
                w = min(512, L - i * 512)
                bank = psS[i % 2]
                if i >= 2:
                    nc.tensor.wait_ge(cp_sem, cp0 + i - 1)
                nc.tensor.matmul(bank[:, 0:w], qsb[:, qc * 128:(qc + 1) * 128],
                                 ksb[:, i * 512:i * 512 + w],
                                 start=True, stop=True).then_inc(mm_sem, 1)
                n_mm[0] += 1
                nc.scalar.wait_ge(mm_sem, mm0 + i + 1)
                nc.scalar.activation(S_sb[:, i * 512:i * 512 + w], bank[:, 0:w],
                                     AF.Copy, scale=0.125).then_inc(cp_sem, 1)
                n_cp[0] += 1
            em.bar()
            nc.vector.tensor_tensor(S_sb[:, L - 128:L], S_sb[:, L - 128:L],
                                    mask[:], ALU.add)
            em.bar()
            nc.vector.reduce_max(negm[:], S_sb[:, 0:L], axis=AX.X, negate=True)
            em.bar()
            nc.scalar.activation(P_sb[:, 0:L], S_sb[:, 0:L], AF.Exp,
                                 bias=negm[:, 0:1], accum_out=lcol[:, 0:1])
            em.bar()
            nc.vector.reciprocal(lcol[:], lcol[:])
            em.bar()
            nblk = L // 128
            ngrp = (nblk + 3) // 4
            tr0, c20, pv0 = n_tr[0], n_c2[0], n_pv[0]
            for g in range(ngrp):
                blo = g * 4
                bhi = min(blo + 4, nblk)
                bank = pstg[g % 2]
                if g >= 2:
                    nc.tensor.wait_ge(c2_sem, c20 + g - 1)
                for b in range(blo, bhi):
                    inst = nc.tensor.transpose(
                        bank[:, (b - blo) * 128:(b - blo + 1) * 128],
                        P_sb[:, b * 128:(b + 1) * 128], ident[:])
                    if b == bhi - 1:
                        inst.then_inc(tr_sem, 1)
                        n_tr[0] += 1
                nc.vector.wait_ge(tr_sem, tr0 + g + 1)
                if g >= 2:
                    nc.vector.wait_ge(pv_sem, pv0 + g - 1)
                nc.vector.tensor_copy(
                    ptb[:, (g % 2) * 512:(g % 2) * 512 + (bhi - blo) * 128],
                    bank[:, 0:(bhi - blo) * 128]).then_inc(c2_sem, 1)
                n_c2[0] += 1
                nc.tensor.wait_ge(c2_sem, c20 + g + 1)
                for b in range(blo, bhi):
                    inst = nc.tensor.matmul(
                        psO[:, 0:64],
                        ptb[:, (g % 2) * 512 + (b - blo) * 128:
                            (g % 2) * 512 + (b - blo + 1) * 128],
                        vsb[:, b * 64:(b + 1) * 64],
                        start=(b == 0), stop=(b == nblk - 1),
                        skip_group_check=True)
                    if b == bhi - 1:
                        inst.then_inc(pv_sem, 1)
                        n_pv[0] += 1
            em.bar()
            nc.vector.tensor_scalar(osb[:], psO[:, 0:64], lcol[:, 0:1], None,
                                    ALU.mult)
            em.bar()
            nc.vector.tensor_tensor(osb[:], osb[:], xsb[:, qc * 64:(qc + 1) * 64],
                                    ALU.add)
            em.bar()
            em.dma(xo[p][qc * 128:(qc + 1) * 128, :], osb[:])
            em.flush()
    em.close()
    return em


def build_L3A_qkv():
    em = Emit("l3aq").start()
    nc = em.nc
    F3 = 3072
    x_in = em.din("x", [TOKC, D], F32)
    w_in = em.din("wqkv", [D, F3], F32)
    ident_in = em.din("ident", [128, 128], BF16)
    qkv_out = em.dout("qkv", [TOKC, F3], BF16)

    w = em.sb([128, 8 * F3], BF16, "w")
    ident = em.sb([128, 128], BF16, "id")
    xg = em.sb([128, D], F32, "xg")
    xn = em.sb([128, D], F32, "xn")
    tmpb = em.sb([128, D], BF16, "tmpb")
    xnT = em.sb([128, 8 * 128], BF16, "xnT")
    ob = em.sb([128, F3], BF16, "ob")
    stats = em.sb([128, 16], F32, "st")
    mu = em.sb([128, 1], F32, "mu")
    rstd = em.sb([128, 1], F32, "rs")
    eps = em.const_col(LN_EPS)
    pst = em.ps([128, 256], BF16, "pt")
    psQ = em.ps([128, 512], F32, "pq")

    for k in range(8):
        em.dma(w[:, k * F3:(k + 1) * F3], w_in[k * 128:(k + 1) * 128, :], nc.gpsimd)
    em.dma(ident[:], ident_in)
    em.flush(nc.gpsimd)
    em.flush()
    for t in range(TOKC // 128):
        em.dma(xg[:], x_in[t * 128:(t + 1) * 128, :])
        em.flush()
        em.ln_cols(xg[:], mu, rstd, stats, eps)
        em.bar()
        nc.vector.tensor_scalar(xn[:], xg[:], mu[:, 0:1], rstd[:, 0:1],
                                ALU.subtract, ALU.mult)
        em.bar()
        _transpose_to(em, xnT, xn[:], tmpb, ident, pst, 0, 128)
        for nn in range(6):
            for kd in range(8):
                nc.tensor.matmul(psQ[:, 0:512],
                                 xnT[:, kd * 128:(kd + 1) * 128],
                                 w[:, kd * F3 + nn * 512:kd * F3 + (nn + 1) * 512],
                                 start=(kd == 0), stop=(kd == 7))
            em.bar()
            nc.vector.tensor_copy(ob[:, nn * 512:(nn + 1) * 512], psQ[:, 0:512])
            em.bar()
        em.dma(qkv_out[t * 128:(t + 1) * 128, :], ob[:])
        em.flush()
    em.close()
    return em


def build_L3A_attn():
    em = Emit("l3aa").start()
    nc = em.nc
    NP = 4
    SQ = S
    NQC = SQ // 128
    q_in = em.din("qf", [NP, 64, SQ], BF16)
    k_in = em.din("kf", [NP, 64, SQ], BF16)
    v_in = em.din("vt", [NP, SQ, 64], BF16)
    mdiag_in = em.din("mdiag", [128, 128], F32)
    mband_in = em.din("mband", [128, 128], F32)
    mc1_in = em.din("mc1", [128, 128], F32)
    ident_in = em.din("ident", [128, 128], BF16)
    ho = em.dout("ho", [NP, SQ, 64], BF16)

    qsb = em.sb([64, SQ], BF16, "q")
    ksb = em.sb([64, SQ], BF16, "k")
    vsb = em.sb([128, 64 * 64], BF16, "v")
    mdiag = em.sb([128, 128], F32, "md")
    mband = em.sb([128, 128], F32, "mb")
    mc1 = em.sb([128, 128], F32, "mc")
    ident = em.sb([128, 128], BF16, "id")
    S_sb = em.sb([128, 320], F32, "S")
    P_sb = em.sb([128, 320], BF16, "P")
    ptb = em.sb([128, 448], BF16, "ptb")
    osb = em.sb([128, 64], BF16, "o")
    negm = em.sb([128, 1], F32, "nm")
    lcol = em.sb([128, 1], F32, "l")
    psS = em.ps([128, 512], F32, "ps")
    pstg = em.ps([128, 512], BF16, "pg")
    psO = em.ps([128, 64], F32, "pO")

    em.dma(mdiag[:], mdiag_in)
    em.dma(mband[:], mband_in)
    em.dma(mc1[:], mc1_in)
    em.dma(ident[:], ident_in)
    em.flush()

    for p in range(NP):
        em.dma(qsb[:], q_in[p], nc.sync)
        em.dma(ksb[:], k_in[p], nc.sync)
        em.dma(vsb[:].rearrange("p (kb d) -> p kb d", kb=64),
               v_in[p].rearrange("(kb p) d -> p kb d", p=128), nc.sync)
        em.flush()
        for qc in range(NQC):
            if qc == 0:
                parts = [(0, 128, 0, mdiag)]
            elif qc == 1:
                parts = [(0, 128, 0, mc1), (128, 128, 1, mdiag)]
            else:
                parts = [(0, 64, -1, None),
                         ((qc - 1) * 128, 128, qc - 1, mband),
                         (qc * 128, 128, qc, mdiag)]
            W = sum(pp[1] for pp in parts)
            off = 0
            for (koff, kw, vb, msk) in parts:
                nc.tensor.matmul(psS[:, off:off + kw],
                                 qsb[:, qc * 128:(qc + 1) * 128],
                                 ksb[:, koff:koff + kw],
                                 start=(off == 0), stop=(off + kw == W))
                off += kw
            em.bar()
            nc.scalar.activation(S_sb[:, 0:W], psS[:, 0:W], AF.Copy, scale=0.125)
            em.bar()
            off = 0
            for (koff, kw, vb, msk) in parts:
                if msk is not None:
                    nc.vector.tensor_tensor(S_sb[:, off:off + kw],
                                            S_sb[:, off:off + kw], msk[:], ALU.add)
                off += kw
            em.bar()
            nc.vector.reduce_max(negm[:], S_sb[:, 0:W], axis=AX.X, negate=True)
            em.bar()
            nc.scalar.activation(P_sb[:, 0:W], S_sb[:, 0:W], AF.Exp,
                                 bias=negm[:, 0:1], accum_out=lcol[:, 0:1])
            em.bar()
            nc.vector.reciprocal(lcol[:], lcol[:])
            em.bar()
            off = 0
            for (koff, kw, vb, msk) in parts:
                nc.tensor.transpose(pstg[:, off:off + kw] if kw == 128
                                    else pstg[0:kw, 0:128],
                                    P_sb[:, off:off + kw], ident[:])
                em.bar()
                if kw == 128:
                    nc.vector.tensor_copy(ptb[:, off:off + 128], pstg[:, off:off + 128])
                else:
                    nc.vector.tensor_copy(ptb[0:kw, 320:448], pstg[0:kw, 0:128])
                em.bar()
                off += kw
            off = 0
            first = True
            for (koff, kw, vb, msk) in parts:
                if kw == 128:
                    nc.tensor.matmul(psO[:, 0:64], ptb[:, off:off + 128],
                                     vsb[:, vb * 64:(vb + 1) * 64],
                                     start=first, stop=(off + kw == W),
                                     skip_group_check=True)
                else:
                    nc.tensor.matmul(psO[:, 0:64], ptb[0:64, 320:448],
                                     vsb[0:64, 0:64],
                                     start=first, stop=(off + kw == W),
                                     skip_group_check=True)
                first = False
                off += kw
            em.bar()
            nc.vector.tensor_scalar(osb[:], psO[:, 0:64], lcol[:, 0:1], None,
                                    ALU.mult)
            em.bar()
            em.dma(ho[p][qc * 128:(qc + 1) * 128, :], osb[:])
            em.flush()
    em.close()
    return em


def build_PROJRES(scale_out=1.0, em=None, io=None):
    standalone = em is None
    if standalone:
        em = Emit("projres").start()
        a_in = em.din("afm", [D, TOKC], BF16)
        x_in = em.din("x", [TOKC, D], F32)
        x_out = em.dout("xo", [TOKC, D], F32)
        io = {"a_src": lambda kd: a_in[kd * 128:(kd + 1) * 128, :],
              "w": em.din("w", [D, D], F32),
              "x_tile": lambda r: x_in[r:r + 128, :],
              "xo_tile": lambda r: x_out[r:r + 128, :]}
    nc = em.nc
    w_in = io["w"]
    em.push_scope()

    w = em.sb([128, 8 * D], BF16, "w")
    afm = em.sb([128, 8 * 2048], BF16, "afm")
    xg = em.sb([128, D], F32, "xg")
    tmpf = em.sb([128, D], F32, "tmpf")
    psO = em.ps([128, D], F32, "po")

    for k in range(8):
        em.dma(w[:, k * D:(k + 1) * D], w_in[k * 128:(k + 1) * 128, :], nc.gpsimd)
        em.dma(afm[:, k * 2048:(k + 1) * 2048], io["a_src"](k), nc.sync)
    em.flush(nc.gpsimd)
    em.flush()
    for t in range(TOKC // 128):
        em.dma(xg[:], io["x_tile"](t * 128))
        for kd in range(8):
            for nn in range(2):
                nc.tensor.matmul(psO[:, nn * 512:(nn + 1) * 512],
                                 afm[:, kd * 2048 + t * 128:kd * 2048 + (t + 1) * 128],
                                 w[:, kd * D + nn * 512:kd * D + (nn + 1) * 512],
                                 start=(kd == 0), stop=(kd == 7))
        em.flush()
        nc.vector.scalar_tensor_tensor(tmpf[:], psO[:, 0:D], scale_out, xg[:],
                                       ALU.mult, ALU.add)
        em.bar()
        em.dma(io["xo_tile"](t * 128), tmpf[:])
        em.flush()
    em.pop_scope()
    if standalone:
        em.close()
    return em


def run_L3A(x, w_qkv, w_out):
    xf = x.reshape(T, D)
    ident = np.eye(128, dtype=np.float32).astype(BFNP)
    em1 = _cached("qkv_plain", lambda: build_QKV(affine=False, name="qkvpl"))
    in_maps = [{"x": xf[c * TOKC:(c + 1) * TOKC], "wqkv": w_qkv, "ident": ident}
               for c in range(NCORES)]
    res = _run(em1.nc, in_maps)
    qkT = np.concatenate([res.results[c]["qkT"] for c in range(NCORES)], 1)
    v = np.concatenate([res.results[c]["v"] for c in range(NCORES)], 0)
    qkT = qkT.reshape(2048, B, S)
    v = v.reshape(B, S, 1024)

    em2 = _cached("attnw", build_ATTN_WINDOW)
    NEG = -1e30
    i = np.arange(128)[:, None]
    j2 = np.arange(128)[None, :]
    mdiag = np.where(j2 <= i, 0.0, NEG).astype(np.float32)
    mband = np.where(j2 >= i, 0.0, NEG).astype(np.float32)
    mc1 = np.where((j2 >= i) | (j2 < 64), 0.0, NEG).astype(np.float32)
    in_maps = []
    for c in range(NCORES):
        bb, h0 = c // 4, (c % 4) * 4
        qf = np.stack([np.ascontiguousarray(qkT[(h0 + k) * 64:(h0 + k + 1) * 64, bb])
                       for k in range(4)])
        kf = np.stack([np.ascontiguousarray(
            qkT[1024 + (h0 + k) * 64:1024 + (h0 + k + 1) * 64, bb])
            for k in range(4)])
        v4 = np.ascontiguousarray(v[bb, :, h0 * 64:(h0 + 4) * 64])
        in_maps.append({"qf": qf, "kf": kf, "v4": v4, "mdiag": mdiag,
                        "mband": mband, "mc1": mc1, "ident": ident})
    res = _run(em2.nc, in_maps)
    headsT = np.zeros((B, 1024, S), dtype=BFNP)
    for c in range(NCORES):
        bb, h0 = c // 4, (c % 4) * 4
        headsT[bb, h0 * 64:(h0 + 4) * 64, :] = res.results[c]["hT"]

    em3 = _cached("projres1", lambda: build_PROJRES(1.0))
    in_maps = []
    for c in range(NCORES):
        bb = (c * TOKC) // S
        scol = slice((c * TOKC) % S, (c * TOKC) % S + TOKC)
        afm = np.ascontiguousarray(headsT[bb][:, scol])
        in_maps.append({"afm": afm, "w": w_out, "x": xf[c * TOKC:(c + 1) * TOKC]})
    res = _run(em3.nc, in_maps)
    out = np.concatenate([res.results[c]["xo"] for c in range(NCORES)], 0)
    return out.reshape(B, S, D)




def build_L1F_pre(em=None, io=None):
    ES = 256
    standalone = em is None
    if standalone:
        em = Emit("l1fp").start()
        x_in = em.din("x", [TOKC, D], F32)
        io = {"x_tile": lambda r: x_in[r:r + 128, :],
              "lnrow": em.din("lnrow", [1, 2 * D], F32),
              "muw": em.din("muw", [D, ES], F32),
              "scl": em.din("scl", [1, 1], F32),
              "ident": em.din("ident", [128, 128], BF16),
              "identf": em.din("identf", [128, 128], F32),
              "xn_out": em.dout("xn", [TOKC, D], F32).__getitem__,
              "lgT_out": em.dout("lgT", [ES, TOKC], F32).__getitem__,
              "cbT_out": em.dout("cbT", [ES, TOKC], F32).__getitem__}
    nc = em.nc
    lnrow_in, mu_in = io["lnrow"], io["muw"]
    scale_in, ident_in = io["scl"], io["ident"]
    em.push_scope()

    muf = em.sb([128, 8 * ES], F32, "muf")
    mub = em.sb([128, 8 * ES], BF16, "mub")
    rb = em.sb([128, ES], F32, "rb")
    stage = em.sb([1, D], F32, "stage")
    srow = em.sb([1, ES], F32, "srow")
    gmb = em.sb([128, D], F32, "gmb")
    btb = em.sb([128, D], F32, "btb")
    ident = em.sb([128, 128], BF16, "id")
    identf = em.sb([128, 128], F32, "idf")
    onesf = em.ones_row(128, F32)
    onescol = em.sb([128, 1], F32, "oc")
    xg = em.sb([128, D], F32, "xg")
    xn = em.sb([128, D], F32, "xn")
    tmpf = em.sb([128, D], F32, "tmpf")
    tmpb = em.sb([128, D], BF16, "tmpb")
    xuT = em.sb([128, 8 * 128], BF16, "xuT")
    lg = em.sb([128, ES], F32, "lg")
    cb = em.sb([128, ES], F32, "cb")
    oT = em.sb([128, 128], F32, "oT")
    psF = em.ps([128, 128], F32, "pf")
    stats = em.sb([128, 16], F32, "st")
    mu = em.sb([128, 1], F32, "mu")
    rstd = em.sb([128, 1], F32, "rs")
    eps = em.const_col(LN_EPS)
    scol = em.sb([128, 1], F32, "scol")
    ncol = em.sb([128, 1], F32, "ncol")
    acc2 = em.sb([128, 1], F32, "acc2")
    pst = em.ps([128, 256], BF16, "pt")
    psQ = em.ps([128, 512], F32, "pq")
    psL = em.ps([128, ES], F32, "pl")

    nc.vector.memset(onescol[:], 1.0)
    for k in range(8):
        em.dma(muf[:, k * ES:(k + 1) * ES], mu_in[k * 128:(k + 1) * 128, :])
    em.dma(ident[:], ident_in)
    em.dma(identf[:], io["identf"])
    em.flush()
    for i, dst in enumerate([gmb, btb]):
        em.dma(stage[:], lnrow_in[0:1, i * D:(i + 1) * D])
        em.flush()
        _bcast_row(em, dst, stage[0:1, :], onesf, psQ)
    em.dma(stage[0:1, 0:1], scale_in)
    em.flush()
    nc.tensor.matmul(psQ[:, 0:1], onesf[:, 0:128], stage[0:1, 0:1],
                     start=True, stop=True)
    em.bar()
    nc.vector.tensor_copy(scol[:], psQ[:, 0:1])
    em.bar()
    for k in range(8):
        nc.scalar.activation(rb[:, 0:ES], muf[:, k * ES:(k + 1) * ES], AF.Square)
        em.bar()
        nc.tensor.matmul(psL[0:1, 0:ES], onescol[:, 0:1], rb[:, 0:ES],
                         start=(k == 0), stop=(k == 7))
        em.bar()
    nc.scalar.activation(srow[:], psL[0:1, 0:ES], AF.Sqrt)
    em.bar()
    nc.vector.tensor_scalar(srow[:], srow[:], 1e-12, None, ALU.max)
    em.bar()
    nc.vector.reciprocal(srow[:], srow[:])
    em.bar()
    nc.tensor.matmul(psL[:, 0:ES], onesf[:, 0:128], srow[0:1, 0:ES],
                     start=True, stop=True)
    em.bar()
    nc.vector.tensor_copy(rb[:], psL[:, 0:ES])
    em.bar()
    for k in range(8):
        nc.vector.tensor_tensor(mub[:, k * ES:(k + 1) * ES],
                                muf[:, k * ES:(k + 1) * ES], rb[:], ALU.mult)
        em.bar()

    for t in range(TOKC // 128):
        em.dma(xg[:], io["x_tile"](t * 128))
        em.flush()
        em.ln_cols(xg[:], mu, rstd, stats, eps)
        em.bar()
        nc.vector.tensor_scalar(xn[:], xg[:], mu[:, 0:1], rstd[:, 0:1],
                                ALU.subtract, ALU.mult)
        em.bar()
        _ln_affine(em, xn[:], xn[:], gmb, btb, mu, rstd, stats, eps, tmpf)
        em.dma(io["xn_out"]((slice(t * 128, (t + 1) * 128), slice(None))), xn[:])
        nc.scalar.activation(tmpf[:], xn[:], AF.Square)
        em.bar()
        nc.vector.reduce_sum(ncol[:], tmpf[:], axis=AX.X)
        em.bar()
        nc.scalar.activation(ncol[:], ncol[:], AF.Sqrt)
        em.bar()
        nc.vector.tensor_scalar(ncol[:], ncol[:], 1e-12, None, ALU.max)
        em.bar()
        nc.vector.reciprocal(ncol[:], ncol[:])
        em.bar()
        nc.vector.tensor_scalar(tmpf[:], xn[:], ncol[:, 0:1], None, ALU.mult)
        em.bar()
        _transpose_to(em, xuT, tmpf[:], tmpb, ident, pst, 0, 128)
        for kd in range(8):
            nc.tensor.matmul(psL[:, 0:ES], xuT[:, kd * 128:(kd + 1) * 128],
                             mub[:, kd * ES:(kd + 1) * ES],
                             start=(kd == 0), stop=(kd == 7))
        em.bar()
        nc.vector.tensor_scalar(lg[:], psL[:, 0:ES], scol[:, 0:1], None, ALU.mult)
        em.bar()
        for cch in range(2):
            nc.tensor.transpose(psF[:, 0:128], lg[:, cch * 128:(cch + 1) * 128],
                                identf[:])
            em.bar()
            nc.vector.tensor_copy(oT[:], psF[:, 0:128])
            em.bar()
            em.dma(io["lgT_out"]((slice(cch * 128, (cch + 1) * 128),
                                  slice(t * 128, (t + 1) * 128))), oT[:])
        nc.vector.reduce_max(ncol[:], lg[:], axis=AX.X, negate=True)
        em.bar()
        nc.scalar.activation(cb[:], lg[:], AF.Exp, bias=ncol[:, 0:1],
                             accum_out=acc2[:, 0:1])
        em.bar()
        nc.vector.reciprocal(ncol[:], acc2[:])
        em.bar()
        nc.vector.tensor_scalar(cb[:], cb[:], ncol[:, 0:1], None, ALU.mult)
        em.bar()
        for cch in range(2):
            nc.tensor.transpose(psF[:, 0:128], cb[:, cch * 128:(cch + 1) * 128],
                                identf[:])
            em.bar()
            nc.vector.tensor_copy(oT[:], psF[:, 0:128])
            em.bar()
            em.dma(io["cbT_out"]((slice(cch * 128, (cch + 1) * 128),
                                  slice(t * 128, (t + 1) * 128))), oT[:])
        em.flush()
    em.pop_scope()
    if standalone:
        em.close()
    return em


def build_L1F_slots(em=None, io=None):
    NS, SQ = 64, S
    standalone = em is None
    if standalone:
        em = Emit("l1fs").start()
        lg_in = em.din("lgsm", [NS, SQ], F32)
        xn0_in = em.din("xn0", [SQ, D], F32)
        xn1_in = em.din("xn1", [SQ, D], F32)
        io = {"lgs_src": lg_in,
              "xn_tile": lambda b, r: (xn0_in if b == 0 else xn1_in)[r:r + 128, :],
              "w1": em.din("w1", [2, D, 256], F32),
              "w2": em.din("w2", [2, 256, D], F32),
              "brow": em.din("brow", [1, 2 * 256 + 2 * D], F32),
              "ident": em.din("ident", [128, 128], BF16),
              "so_out": em.dout("so", [NS, D], F32)}
    nc = em.nc
    w1_in, w2_in = io["w1"], io["w2"]
    brow_in, ident_in = io["brow"], io["ident"]
    so_out = io["so_out"]
    em.push_scope()

    lgs = em.sb([64, SQ], F32, "lgs")
    E = em.sb([64, SQ], BF16, "E")
    etb = em.sb([128, 64], BF16, "etb")
    xnb = em.sb([128, D], BF16, "xnb")
    w1b = em.sb([128, 8 * 512], BF16, "w1b")
    w2b = em.sb([128, 2 * 2 * D], BF16, "w2b")
    brow = em.sb([1, 2 * 256 + 2 * D], F32, "brow")
    browb = em.sb([1, 2 * 256 + 2 * D], BF16, "browb")
    onesb = em.sb([1, 128], BF16, "onesb")
    ident = em.sb([128, 128], BF16, "id")
    slot_in = em.sb([64, D], F32, "si")
    siTb = em.sb([128, 8 * 64], BF16, "siT")
    h16 = em.sb([16, 256], F32, "h16")
    h16b = em.sb([16, 256], BF16, "h16b")
    hTb = em.sb([128, 2 * 16], BF16, "hTb")
    o16 = em.sb([16, D], F32, "o16")
    negm = em.sb([64, 1], F32, "nm")
    rsum = em.sb([64, 1], F32, "rsum")
    pst = em.ps([128, 256], BF16, "pt")
    psS = em.ps([32, D], F32, "psS")
    psH = em.ps([16, 512], F32, "psH")
    psO2 = em.ps([16, D], F32, "psO2")

    nc.vector.memset(onesb[:], 1.0)
    lgs_dst = (lgs[:].rearrange(io["lgs_rearr"], **io.get("lgs_kw", {}))
               if "lgs_rearr" in io else lgs[:])
    em.dma(lgs_dst, io["lgs_src"])
    em.dma(brow[:], brow_in)
    em.dma(browb[:], brow_in, nc.gpsimd)
    em.dma(ident[:], ident_in)
    for e in range(2):
        em.dma(w1b[:, :].rearrange("p (dc c) -> p dc c", dc=8)[:, :, e * 256:(e + 1) * 256],
               w1_in[e].rearrange("(dc p) c -> p dc c", p=128), nc.gpsimd)
        em.dma(w2b[:, :].rearrange("p (hc c) -> p hc c", hc=2)
               [:, :, e * D:(e + 1) * D],
               w2_in[e].rearrange("(hc p) c -> p hc c", p=128), nc.gpsimd)
    em.flush(nc.gpsimd)
    em.flush()
    nc.vector.reduce_max(negm[:], lgs[:], axis=AX.X, negate=True)
    em.bar()
    nc.scalar.activation(E[:], lgs[:], AF.Exp, bias=negm[:, 0:1],
                         accum_out=rsum[:, 0:1])
    em.bar()
    nc.vector.reciprocal(rsum[:], rsum[:])
    em.bar()
    for b in range(2):
        for tt in range(SQ // 128):
            em.dma(xnb[:], io["xn_tile"](b, tt * 128), nc.gpsimd)
            em.flush(nc.gpsimd)
            nc.tensor.transpose(pst[:, 0:32],
                                E[b * 32:(b + 1) * 32, tt * 128:(tt + 1) * 128],
                                ident[b * 32:(b + 1) * 32, b * 32:(b + 1) * 32])
            em.bar()
            nc.vector.tensor_copy(etb[:, 0:32], pst[:, 0:32])
            em.bar()
            for nn in range(2):
                nc.tensor.matmul(psS[:, nn * 512:(nn + 1) * 512], etb[:, 0:32],
                                 xnb[:, nn * 512:(nn + 1) * 512],
                                 start=(tt == 0), stop=(tt == SQ // 128 - 1),
                                 skip_group_check=True)
            em.bar()
        nc.vector.tensor_scalar(slot_in[b * 32:(b + 1) * 32, :], psS[0:32, 0:D],
                                rsum[b * 32:(b + 1) * 32, 0:1], None, ALU.mult)
        em.bar()
    sib = em.sb([64, D], BF16, "sib")
    nc.vector.tensor_copy(sib[:], slot_in[:])
    em.bar()
    for kd in range(8):
        nc.tensor.transpose(pst[:, 0:64], sib[:, kd * 128:(kd + 1) * 128],
                            ident[0:64, 0:64])
        em.bar()
        nc.vector.tensor_copy(siTb[:, kd * 64:(kd + 1) * 64], pst[:, 0:64])
        em.bar()
    for g in range(4):
        b, eL = g // 2, g % 2
        for kd in range(8):
            nc.tensor.matmul(psH[:, 0:256],
                             siTb[:, kd * 64 + g * 16:kd * 64 + (g + 1) * 16],
                             w1b[:, kd * 512 + eL * 256:kd * 512 + (eL + 1) * 256],
                             start=(kd == 0), stop=False)
        nc.tensor.matmul(psH[:, 0:256], onesb[:, 0:16],
                         browb[0:1, eL * 256:(eL + 1) * 256],
                         start=False, stop=True)
        em.bar()
        nc.scalar.activation(h16[:], psH[0:16, 0:256], AF.Gelu)
        em.bar()
        nc.vector.tensor_copy(h16b[:], h16[:])
        em.bar()
        for hc in range(2):
            nc.tensor.transpose(pst[:, 0:16], h16b[:, hc * 128:(hc + 1) * 128],
                                ident[0:16, 0:16])
            em.bar()
            nc.vector.tensor_copy(hTb[:, hc * 16:(hc + 1) * 16], pst[:, 0:16])
            em.bar()
        for hc in range(2):
            nc.tensor.matmul(psO2[:, 0:512],
                             hTb[:, hc * 16:(hc + 1) * 16],
                             w2b[:, hc * 2 * D + eL * D:hc * 2 * D + eL * D + 512],
                             start=(hc == 0), stop=False)
            nc.tensor.matmul(psO2[:, 512:D],
                             hTb[:, hc * 16:(hc + 1) * 16],
                             w2b[:, hc * 2 * D + eL * D + 512:hc * 2 * D + (eL + 1) * D],
                             start=(hc == 0), stop=False)
        for nn in range(2):
            nc.tensor.matmul(psO2[:, nn * 512:(nn + 1) * 512], onesb[:, 0:16],
                             browb[0:1, 512 + eL * D + nn * 512:
                                   512 + eL * D + (nn + 1) * 512],
                             start=False, stop=True)
        em.bar()
        nc.vector.tensor_copy(o16[:], psO2[0:16, 0:D])
        em.bar()
        em.dma(so_out[g * 16:(g + 1) * 16, :], o16[:])
        em.flush()
    em.pop_scope()
    if standalone:
        em.close()
    return em


def build_L1F_out(em=None, io=None):
    ES = 256
    standalone = em is None
    if standalone:
        em = Emit("l1fo").start()
        cT_in = em.din("combT", [ES, TOKC], F32)
        so_in = em.din("sob", [ES, D], F32)
        x_in = em.din("x", [TOKC, D], F32)
        x_out = em.dout("xo", [TOKC, D], F32)
        io = {"cT_src": lambda c: cT_in[c * 128:(c + 1) * 128, :],
              "so_src": lambda c: so_in[c * 128:(c + 1) * 128, :],
              "x_tile": lambda r: x_in[r:r + 128, :],
              "xo_tile": lambda r: x_out[r:r + 128, :]}
    nc = em.nc
    em.push_scope()

    cTb = em.sb([128, 2 * TOKC], BF16, "cTb")
    sob = em.sb([128, 2 * D], BF16, "sob")
    xg = em.sb([128, D], F32, "xg")
    tmpf = em.sb([128, D], F32, "tmpf")
    psO = em.ps([128, D], F32, "po")

    for c in range(2):
        em.dma(cTb[:, c * TOKC:(c + 1) * TOKC], io["cT_src"](c), nc.gpsimd)
        em.dma(sob[:, c * D:(c + 1) * D], io["so_src"](c), nc.gpsimd)
    em.flush(nc.gpsimd)
    em.flush()
    for t in range(TOKC // 128):
        em.dma(xg[:], io["x_tile"](t * 128))
        for c in range(2):
            for nn in range(2):
                nc.tensor.matmul(psO[:, nn * 512:(nn + 1) * 512],
                                 cTb[:, c * TOKC + t * 128:c * TOKC + (t + 1) * 128],
                                 sob[:, c * D + nn * 512:c * D + (nn + 1) * 512],
                                 start=(c == 0), stop=(c == 1))
        em.flush()
        nc.vector.scalar_tensor_tensor(tmpf[:], psO[:, 0:D], 0.5, xg[:],
                                       ALU.mult, ALU.add)
        em.bar()
        em.dma(io["xo_tile"](t * 128), tmpf[:])
        em.flush()
    em.pop_scope()
    if standalone:
        em.close()
    return em


def run_L1F(x, ln_gamma, ln_beta, muw, scale, w1, b1, w2, b2):
    em1 = _cached("build_L1F_pre", build_L1F_pre)
    xf = x.reshape(T, D)
    ident = np.eye(128, dtype=np.float32).astype(BFNP)
    identf = np.eye(128, dtype=np.float32)
    lnrow = np.concatenate([ln_gamma, ln_beta]).astype(np.float32)[None, :]
    muw2 = np.ascontiguousarray(muw.reshape(D, 256))
    common = {"lnrow": lnrow, "muw": muw2, "identf": identf,
              "scl": np.asarray(scale, np.float32).reshape(1, 1), "ident": ident}
    in_maps = [{"x": xf[c * TOKC:(c + 1) * TOKC], **common} for c in range(NCORES)]
    res = _run(em1.nc, in_maps)
    xn = np.concatenate([res.results[c]["xn"] for c in range(NCORES)], 0)
    lgT = np.concatenate([res.results[c]["lgT"] for c in range(NCORES)], 1)
    cbT = np.concatenate([res.results[c]["cbT"] for c in range(NCORES)], 1)

    em2 = _cached("build_L1F_slots", build_L1F_slots)
    xnr = xn.reshape(B, S, D)
    in_maps = []
    for c in range(NCORES):
        eids = [2 * c, 2 * c + 1]
        rows = np.concatenate([
            np.ascontiguousarray(lgT[e * 16:(e + 1) * 16, bb * S:(bb + 1) * S])
            for bb in range(B) for e in eids], 0)
        b1s = np.concatenate([b1[e] for e in eids])
        b2s = np.concatenate([b2[e] for e in eids])
        brow = np.concatenate([b1s, b2s]).astype(np.float32)[None, :]
        in_maps.append({"lgsm": rows, "xn0": xnr[0], "xn1": xnr[1],
                        "w1": w1[eids], "w2": w2[eids], "brow": brow,
                        "ident": ident})
    res = _run(em2.nc, in_maps)
    so = np.zeros((B, 256, D), np.float32)
    for c in range(NCORES):
        r = res.results[c]["so"]
        for bb in range(B):
            for i, e in enumerate([2 * c, 2 * c + 1]):
                so[bb, e * 16:(e + 1) * 16] = r[bb * 32 + i * 16: bb * 32 + (i + 1) * 16]

    em3 = _cached("build_L1F_out", build_L1F_out)
    in_maps = []
    for c in range(NCORES):
        bb = (c * TOKC) // S
        in_maps.append({"combT": cbT[:, c * TOKC:(c + 1) * TOKC],
                        "sob": so[bb], "x": xf[c * TOKC:(c + 1) * TOKC]})
    res = _run(em3.nc, in_maps)
    out = np.concatenate([res.results[c]["xo"] for c in range(NCORES)], 0)
    return out.reshape(B, S, D)



def build_L1A_ln(em=None, io=None):
    standalone = em is None
    if standalone:
        em = Emit("l1ln").start()
        x_in = em.din("x", [TOKC, D], F32)
        xnT_o = em.dout("xnT", [D, TOKC], F32)
        io = {"x_tile": lambda r: x_in[r:r + 128, :],
              "identf": em.din("identf", [128, 128], F32),
              "xnT_out": lambda kd, r: xnT_o[kd * 128:(kd + 1) * 128, r:r + 128]}
    nc = em.nc
    em.push_scope()
    xg = em.sb([128, D], F32, "xg")
    xn = em.sb([128, D], F32, "xn")
    identf = em.sb([128, 128], F32, "idf")
    oT = em.sb([128, 128], F32, "oT")
    stats = em.sb([128, 16], F32, "st")
    mu = em.sb([128, 1], F32, "mu")
    rstd = em.sb([128, 1], F32, "rs")
    eps = em.const_col(LN_EPS)
    psF = em.ps([128, 128], F32, "pf")
    em.dma(identf[:], io["identf"])
    em.flush()
    for t in range(TOKC // 128):
        em.dma(xg[:], io["x_tile"](t * 128))
        em.flush()
        em.ln_cols(xg[:], mu, rstd, stats, eps)
        em.bar()
        nc.vector.tensor_scalar(xn[:], xg[:], mu[:, 0:1], rstd[:, 0:1],
                                ALU.subtract, ALU.mult)
        em.bar()
        for kd in range(8):
            nc.tensor.transpose(psF[:, 0:128], xn[:, kd * 128:(kd + 1) * 128],
                                identf[:])
            em.bar()
            nc.vector.tensor_copy(oT[:], psF[:, 0:128])
            em.bar()
            em.dma(io["xnT_out"](kd, t * 128), oT[:])
            em.flush()
    em.pop_scope()
    if standalone:
        em.close()
    return em


def build_L1A_conv(em=None, io=None):
    standalone = em is None
    if standalone:
        em = Emit("l1cv").start()
        xnf_in = em.din("xnf", [128, B * S], F32)
        xc_o = em.dout("xc", [128, B * S], F32)
        io = {"xn_src": lambda b: xnf_in[:, b * S:(b + 1) * S]
              .rearrange("p (r t) -> p r t", r=1),
              "NRC": 1,
              "cw": em.din("cw", [128, 4], F32),
              "cbr": em.din("cb", [128, 1], F32),
              "xc_out": xc_o}
    nc = em.nc
    SP = S + 3
    em.push_scope()
    xnp = em.sb([128, B * SP], F32, "xnp")
    cw = em.sb([128, 4], F32, "cw")
    cbc = em.sb([128, 1], F32, "cb")
    acc = em.sb([128, S], F32, "acc")
    nc.vector.memset(xnp[:, 0:SP], 0.0)
    nc.vector.memset(xnp[:, SP:2 * SP], 0.0)
    em.bar()
    NRC = io["NRC"]
    for b in range(B):
        em.dma(xnp[:, b * SP + 3:(b + 1) * SP]
               .rearrange("p (r t) -> p r t", r=NRC), io["xn_src"](b))
    em.dma(cw[:], io["cw"])
    em.dma(cbc[:], io["cbr"])
    em.flush()
    for b in range(B):
        base = b * SP
        nc.vector.tensor_scalar(acc[:], xnp[:, base:base + S], cw[:, 0:1],
                                None, ALU.mult)
        em.bar()
        for k in range(1, 4):
            nc.vector.scalar_tensor_tensor(acc[:], xnp[:, base + k:base + k + S],
                                           cw[:, k:k + 1], acc[:],
                                           ALU.mult, ALU.add)
            em.bar()
        nc.vector.tensor_scalar(acc[:], acc[:], cbc[:, 0:1], None, ALU.add)
        em.bar()
        nc.scalar.activation(acc[:], acc[:], AF.Silu)
        em.bar()
        em.dma(io["xc_out"][:, b * S:(b + 1) * S], acc[:])
        em.flush()
    em.pop_scope()
    if standalone:
        em.close()
    return em


def build_L1A_proj(em=None, io=None):
    NP33 = 33
    standalone = em is None
    if standalone:
        em = Emit("l1pj").start()
        xcT_in = em.din("xcT", [D, TOKC], F32)
        pjT_o = em.dout("pjT", [NP33, TOKC], F32)
        io = {"xc_src": lambda kd: xcT_in[kd * 128:(kd + 1) * 128, :],
              "xpw": em.din("xpw", [D, NP33], F32),
              "identf": em.din("identf", [128, 128], F32),
              "pjT_out": lambda r: pjT_o[:, r:r + 128]}
    nc = em.nc
    em.push_scope()
    xcb = em.sb([128, 8 * TOKC], BF16, "xcb")
    wb = em.sb([128, 8 * NP33], BF16, "wb")
    ob = em.sb([128, NP33], F32, "ob")
    identf = em.sb([128, 128], F32, "idf")
    oT = em.sb([33, 128], F32, "oT")
    psP = em.ps([128, NP33], F32, "pp")
    psF = em.ps([33, 128], F32, "pf")
    for k in range(8):
        em.dma(xcb[:, k * TOKC:(k + 1) * TOKC], io["xc_src"](k), nc.gpsimd)
        em.dma(wb[:, k * NP33:(k + 1) * NP33], io["xpw"][k * 128:(k + 1) * 128, :],
               nc.gpsimd)
    em.dma(identf[:], io["identf"])
    em.flush(nc.gpsimd)
    em.flush()
    for t in range(TOKC // 128):
        for kd in range(8):
            nc.tensor.matmul(psP[:, 0:NP33],
                             xcb[:, kd * TOKC + t * 128:kd * TOKC + (t + 1) * 128],
                             wb[:, kd * NP33:(kd + 1) * NP33],
                             start=(kd == 0), stop=(kd == 7))
        em.bar()
        nc.vector.tensor_copy(ob[:], psP[:, 0:NP33])
        em.bar()
        nc.tensor.transpose(psF[:, 0:128], ob[:], identf[:])
        em.bar()
        nc.vector.tensor_copy(oT[:], psF[:, 0:128])
        em.bar()
        em.dma(io["pjT_out"](t * 128), oT[:])
        em.flush()
    em.pop_scope()
    if standalone:
        em.close()
    return em


def build_L1A_scan(em=None, io=None):
    NCH = S // 512
    DST = 16
    standalone = em is None
    if standalone:
        em = Emit("l1sc").start()
        xc_in = em.din("xcf", [128, B * S], F32)
        xn_in = em.din("xnf", [128, B * S], F32)
        prow_in0 = em.din("prow", [1, B * S], F32)
        ball_in = em.din("ball", [B * DST, S], F32)
        call_in = em.din("call", [B * DST, S], F32)
        io = {"xc": xc_in,
              "xn_tile_fm": lambda b, ch: xn_in[:, b * S + ch * 512:
                                                b * S + (ch + 1) * 512],
              "prow_src": lambda b: prow_in0[0:1, b * S:(b + 1) * S]
              .rearrange("o (r t) -> o r t", r=1),
              "brow_src": lambda b, n: ball_in[b * DST + n:b * DST + n + 1, :]
              .rearrange("o (r t) -> o r t", r=1),
              "crow_src": lambda b, n: call_in[b * DST + n:b * DST + n + 1, :]
              .rearrange("o (r t) -> o r t", r=1),
              "NRR": 1,
              "dtwr": em.din("dtwr", [1, 128], F32),
              "dtbc": em.din("dtbc", [128, 1], F32),
              "alog": em.din("alog", [128, DST], F32),
              "dpc": em.din("dpc", [128, 1], F32),
              "scanT_out": em.dout("xo", [128, B * S], F32)}
    nc = em.nc
    xc_in = io["xc"]
    dtw_in, dtb_in = io["dtwr"], io["dtbc"]
    alog_in, dp_in = io["alog"], io["dpc"]
    xo_out = io["scanT_out"]
    NRR = io["NRR"]
    em.push_scope()

    dx = em.sb([128, S], F32, "dx")
    delta = em.sb([128, S], F32, "delta")
    yb = em.sb([128, S], F32, "yb")
    rowA = em.sb([1, S], F32, "rowA")
    rowB = em.sb([1, S], F32, "rowB")
    dtw = em.sb([1, 128], F32, "dtw")
    dtb = em.sb([128, 1], F32, "dtb")
    acols = em.sb([128, DST], F32, "ac")
    dpc = em.sb([128, 1], F32, "dp")
    onec = em.const_col(1.0)
    onesf = em.ones_row(128, F32)
    abar = em.sb([128, 512], F32, "abar")
    usb = em.sb([128, 512], F32, "usb")
    hb = [em.sb([128, 512], F32, "hb") for _ in range(2)]
    t1 = em.sb([128, 512], F32, "t1")
    t2 = em.sb([128, 512], F32, "t2")
    xchunk = em.sb([128, 512], F32, "xck")
    psB = em.ps([128, 512], F32, "pb")
    psC = em.ps([128, 512], F32, "pc")
    psD = em.ps([128, 512], F32, "pd")
    sv = em.sem("sc_sv")
    tv = em.sem("sc_tv")
    vt = em.sem("sc_vt")
    vs = em.sem("sc_vs")
    cnt = {"i": 0}

    em.dma(dtw[:], dtw_in)
    em.dma(dtb[:], dtb_in)
    em.dma(acols[:], alog_in)
    em.dma(dpc[:], dp_in)
    em.flush()
    nc.scalar.activation(acols[:], acols[:], AF.Exp)
    em.bar()
    nc.vector.tensor_scalar(acols[:], acols[:], -1.0, None, ALU.mult)
    em.bar()

    for b in range(B):
        em.dma(rowA[:].rearrange("o (r t) -> o r t", r=NRR), io["prow_src"](b))
        em.dma(dx[:], xc_in[:, b * S:(b + 1) * S])
        em.flush()
        for ch in range(NCH):
            sl = slice(ch * 512, (ch + 1) * 512)
            nc.tensor.matmul(psD[:, 0:512], dtw[0:1, :], rowA[0:1, sl],
                             start=True, stop=True)
            em.bar()
            nc.vector.tensor_scalar(t1[:], psD[:, 0:512], dtb[:, 0:1], 0.0,
                                    ALU.add, ALU.max)
            em.bar()
            nc.scalar.activation(t2[:], psD[:, 0:512], AF.Abs, bias=dtb[:, 0:1])
            em.bar()
            nc.scalar.activation(t2[:], t2[:], AF.Exp, scale=-1.0)
            em.bar()
            nc.scalar.activation(t2[:], t2[:], AF.Ln, bias=onec[:, 0:1])
            em.bar()
            nc.vector.tensor_tensor(delta[:, sl], t1[:], t2[:], ALU.add)
            em.bar()
        nc.vector.tensor_tensor(dx[:], delta[:], dx[:], ALU.mult)
        em.bar()
        for n in range(DST):
            em.dma(rowA[:].rearrange("o (r t) -> o r t", r=NRR),
                   io["brow_src"](b, n))
            em.dma(rowB[:].rearrange("o (r t) -> o r t", r=NRR),
                   io["crow_src"](b, n))
            em.flush()
            for ch in range(NCH):
                sl = slice(ch * 512, (ch + 1) * 512)
                i = cnt["i"]
                cnt["i"] += 1
                if i > 0:
                    nc.scalar.wait_ge(vs, i)
                nc.scalar.activation(abar[:], delta[:, sl], AF.Exp,
                                     scale=acols[:, n:n + 1]).then_inc(sv, 1)
                if i > 0:
                    nc.tensor.wait_ge(vt, 2 * i - 1)
                nc.tensor.matmul(psB[:, 0:512], onesf[:, 0:128],
                                 rowA[0:1, sl], start=True,
                                 stop=True).then_inc(tv, 1)
                if i > 0:
                    nc.tensor.wait_ge(vt, 2 * i)
                nc.tensor.matmul(psC[:, 0:512], onesf[:, 0:128],
                                 rowB[0:1, sl], start=True,
                                 stop=True).then_inc(tv, 1)
                nc.vector.wait_ge(tv, 2 * i + 1)
                nc.vector.tensor_tensor(usb[:], dx[:, sl], psB[:, 0:512],
                                        ALU.mult).then_inc(vt, 1)
                nc.vector.wait_ge(sv, i + 1)
                nc.vector.tensor_tensor_scan(
                    hb[ch % 2][:], abar[:], usb[:],
                    0.0 if ch == 0 else hb[(ch + 1) % 2][:, 511:512],
                    ALU.mult, ALU.add).then_inc(vs, 1)
                nc.vector.wait_ge(tv, 2 * i + 2)
                nc.vector.tensor_tensor(t1[:], hb[ch % 2][:], psC[:, 0:512],
                                        ALU.mult).then_inc(vt, 1)
                if n == 0:
                    nc.vector.tensor_copy(yb[:, sl], t1[:])
                else:
                    nc.vector.tensor_tensor(yb[:, sl], yb[:, sl], t1[:], ALU.add)
            em.bar()
        for ch in range(NCH):
            sl = slice(b * S + ch * 512, b * S + (ch + 1) * 512)
            sll = slice(ch * 512, (ch + 1) * 512)
            em.dma(xchunk[:], io["xn_tile_fm"](b, ch))
            em.flush()
            nc.vector.scalar_tensor_tensor(t1[:], xchunk[:], dpc[:, 0:1],
                                           yb[:, sll], ALU.mult, ALU.add)
            em.bar()
            em.dma(xo_out[:, sl], t1[:])
            em.flush()
    em.pop_scope()
    if standalone:
        em.close()
    return em


def run_L1A(x, conv_w, conv_b, xproj_w, dt_w, dt_b, a_log, Dp):
    xf = x.reshape(T, D)
    identf = np.eye(128, dtype=np.float32)
    em = _cached("build_L1A_ln", build_L1A_ln)
    in_maps = [{"x": xf[c * TOKC:(c + 1) * TOKC], "identf": identf}
               for c in range(NCORES)]
    res = _run(em.nc, in_maps)
    xnT = np.concatenate([res.results[c]["xnT"] for c in range(NCORES)], 1)
    em = _cached("build_L1A_conv", build_L1A_conv)
    cw = conv_w.reshape(D, 4)
    in_maps = [{"xnf": xnT[c * 128:(c + 1) * 128],
                "cw": cw[c * 128:(c + 1) * 128],
                "cb": conv_b[c * 128:(c + 1) * 128, None]}
               for c in range(NCORES)]
    res = _run(em.nc, in_maps)
    xc_fm = np.concatenate([res.results[c]["xc"] for c in range(NCORES)], 0)
    em = _cached("build_L1A_proj", build_L1A_proj)
    in_maps = [{"xcT": xc_fm[:, c * TOKC:(c + 1) * TOKC],
                "xpw": np.ascontiguousarray(xproj_w.T), "identf": identf}
               for c in range(NCORES)]
    res = _run(em.nc, in_maps)
    pjT = np.concatenate([res.results[c]["pjT"] for c in range(NCORES)], 1)
    em = _cached("build_L1A_scan", build_L1A_scan)
    ball = np.ascontiguousarray(pjT[0:16].reshape(16, B, S).transpose(1, 0, 2)
                                .reshape(B * 16, S))
    call = np.ascontiguousarray(pjT[16:32].reshape(16, B, S).transpose(1, 0, 2)
                                .reshape(B * 16, S))
    prow = np.ascontiguousarray(pjT[32:33])
    in_maps = []
    for c in range(NCORES):
        dsl = slice(c * 128, (c + 1) * 128)
        in_maps.append({
            "xcf": xc_fm[dsl], "xnf": xnT[dsl],
            "prow": prow, "ball": ball, "call": call,
            "dtwr": np.ascontiguousarray(dt_w[dsl, 0])[None, :],
            "dtbc": dt_b[dsl, None], "alog": a_log[dsl],
            "dpc": Dp[dsl, None]})
    res = _run(em.nc, in_maps)
    scanT = np.concatenate([res.results[c]["xo"] for c in range(NCORES)], 0)
    em = _cached("addt32", lambda: build_ADDT(src_dtype=F32, name="addt32"))
    in_maps = [{"x": xf[c * TOKC:(c + 1) * TOKC],
                "aT": np.ascontiguousarray(scanT[:, c * TOKC:(c + 1) * TOKC]),
                "identf": identf} for c in range(NCORES)]
    res = _run(em.nc, in_maps)
    out = np.concatenate([res.results[c]["xo"] for c in range(NCORES)], 0)
    return out.reshape(B, S, D)




_BUILD_CACHE = {}


def _cached(name, fn):
    if name not in _BUILD_CACHE:
        _BUILD_CACHE[name] = fn()
    return _BUILD_CACHE[name]


def kernel(**inp):
    return kernel_fused(99, **inp)


def kernel_unfused(**inp):
    x = np.asarray(inp["L0_attn_x"], np.float32)

    def L0F(x):
        return run_L0F(x, inp["L0_ffn_g1"], inp["L0_ffn_be1"], inp["L0_ffn_mix_w"],
                       inp["L0_ffn_mix_b"], inp["L0_ffn_g2"], inp["L0_ffn_be2"],
                       inp["L0_ffn_router_w"], inp["L0_ffn_w1"], inp["L0_ffn_b1"],
                       inp["L0_ffn_w2"], inp["L0_ffn_b2"])

    def L1F(x):
        return run_L1F(x, inp["L1_ffn_ln_gamma"], inp["L1_ffn_ln_beta"],
                       inp["L1_ffn_mu"], inp["L1_ffn_scale"], inp["L1_ffn_w1"],
                       inp["L1_ffn_b1"], inp["L1_ffn_w2"], inp["L1_ffn_b2"])

    def L2F(x):
        return run_L2F(x, inp["L2_ffn_shared_w1"], inp["L2_ffn_shared_w2"],
                       inp["L2_ffn_experts_w1"], inp["L2_ffn_experts_w2"],
                       inp["L2_ffn_gate_w"])

    def L3F(x):
        return run_L3F(x, inp["L3_ffn_w_gate"], inp["L3_ffn_w_up"],
                       inp["L3_ffn_w_down"])

    x = L0F(x)
    x = run_L0A(x, inp["L0_attn_gamma"], inp["L0_attn_beta"], inp["L0_attn_w_qkv"])
    x = L0F(x)
    x = L1F(x)
    x = run_L1A(x, inp["L1_attn_conv_w"], inp["L1_attn_conv_b"],
                inp["L1_attn_xproj_w"], inp["L1_attn_dt_w"], inp["L1_attn_dt_b"],
                inp["L1_attn_a_log"], inp["L1_attn_Dp"])
    x = L1F(x)
    x = L2F(x)
    x = run_L2A(x, inp["L2_attn_in_proj_weight"], inp["L2_attn_in_proj_bias"],
                inp["L2_attn_out_proj_weight"], inp["L2_attn_out_proj_bias"])
    x = L2F(x)
    x = L3F(x)
    x = run_L3A(x, inp["L3_attn_w_qkv"], inp["L3_attn_w_out"])
    x = L3F(x)
    return x



def build_QKV(em=None, io=None, affine=False, name="qkv"):
    standalone = em is None
    if standalone:
        em = Emit(name).start()
        x_in = em.din("x", [TOKC, D], F32)
        io = {"x_tile": lambda r: x_in[r:r + 128, :],
              "wqkv": em.din("wqkv", [D, 3072], F32),
              "ident": em.din("ident", [128, 128], BF16),
              "qkT_out": em.dout("qkT", [2048, TOKC], BF16),
              "v_out": em.dout("v", [TOKC, 1024], BF16)}
        if affine:
            io["gbrow"] = em.din("gbrow", [1, 2 * D], F32)
    nc = em.nc
    F3 = 3072
    em.push_scope()
    w = em.sb([128, 8 * F3], BF16, "w")
    ident = em.sb([128, 128], BF16, "id")
    xg = em.sb([128, D], F32, "xg")
    xn = em.sb([128, D], F32, "xn")
    tmpb = em.sb([128, D], BF16, "tmpb")
    xnT = em.sb([128, 8 * 512], BF16, "xnT")
    ob = em.sb([128, 512], BF16, "ob")
    stats = em.sb([128, 16], F32, "st")
    mu = em.sb([128, 1], F32, "mu")
    rstd = em.sb([128, 1], F32, "rs")
    eps = em.const_col(LN_EPS)
    pst = em.ps([128, 256], BF16, "pt")
    psQ = em.ps([128, 512], F32, "pq")
    if affine:
        onesf = em.ones_row(128, F32)
        stage = em.sb([1, D], F32, "stage")
        gmb = em.sb([128, D], F32, "gmb")
        btb = em.sb([128, D], F32, "btb")

    for k in range(8):
        em.dma(w[:, k * F3:(k + 1) * F3], io["wqkv"][k * 128:(k + 1) * 128, :],
               nc.gpsimd)
    em.dma(ident[:], io["ident"])
    em.flush(nc.gpsimd)
    em.flush()
    if affine:
        for i, dst in enumerate([gmb, btb]):
            em.dma(stage[:], io["gbrow"][0:1, i * D:(i + 1) * D])
            em.flush()
            _bcast_row(em, dst, stage[0:1, :], onesf, psQ)

    for g in range(TOKC // 512):
        for t in range(4):
            row = g * 512 + t * 128
            em.dma(xg[:], io["x_tile"](row))
            em.flush()
            em.ln_cols(xg[:], mu, rstd, stats, eps)
            em.bar()
            nc.vector.tensor_scalar(xn[:], xg[:], mu[:, 0:1], rstd[:, 0:1],
                                    ALU.subtract, ALU.mult)
            em.bar()
            if affine:
                _ln_affine(em, xn[:], xn[:], gmb, btb, mu, rstd, stats, eps, xg)
            _transpose_to(em, xnT, xn[:], tmpb, ident, pst, t * 128, 512)
        for fc in range(16):
            for kd in range(8):
                nc.tensor.matmul(psQ[:, 0:512],
                                 w[:, kd * F3 + fc * 128:kd * F3 + (fc + 1) * 128],
                                 xnT[:, kd * 512:(kd + 1) * 512],
                                 start=(kd == 0), stop=(kd == 7))
            em.bar()
            nc.vector.tensor_copy(ob[:], psQ[:, 0:512])
            em.bar()
            em.dma(io["qkT_out"][fc * 128:(fc + 1) * 128, g * 512:(g + 1) * 512],
                   ob[:])
            em.flush()
        for t in range(4):
            row = g * 512 + t * 128
            for nn in range(2):
                for kd in range(8):
                    nc.tensor.matmul(
                        psQ[:, 0:512],
                        xnT[:, kd * 512 + t * 128:kd * 512 + (t + 1) * 128],
                        w[:, kd * F3 + 2048 + nn * 512:kd * F3 + 2048 + (nn + 1) * 512],
                        start=(kd == 0), stop=(kd == 7))
                em.bar()
                nc.vector.tensor_copy(ob[:], psQ[:, 0:512])
                em.bar()
                em.dma(io["v_out"][row:row + 128, nn * 512:(nn + 1) * 512], ob[:])
                em.flush()
    em.pop_scope()
    if standalone:
        em.close()
    return em


def build_ATTN_CAUSAL(em=None, io=None):
    standalone = em is None
    if standalone:
        em = Emit("attnc").start()
        q_in = em.din("qf", [4, 64, S], BF16)
        k_in = em.din("kf", [4, 64, S], BF16)
        v_in = em.din("v4", [S, 256], BF16)
        io = {"q_src": lambda p: q_in[p].rearrange("d (r t) -> d r t", r=4),
              "k_src": lambda p: k_in[p].rearrange("d (r t) -> d r t", r=4),
              "v4_src": v_in.rearrange("(kb p) c -> p kb c", p=128),
              "mask": em.din("mask", [128, 128], F32),
              "ident": em.din("ident", [128, 128], BF16),
              "hT_out": em.dout("hT", [256, S], BF16)}
    nc = em.nc
    NP = 4
    SQ = S
    NQC = SQ // 128
    em.push_scope()
    qsb = em.sb([64, SQ], BF16, "q")
    ksb = em.sb([64, SQ], BF16, "k")
    vsb4 = em.sb([128, 64 * 256], BF16, "v4")
    mask = em.sb([128, 128], F32, "mask")
    ident = em.sb([128, 128], BF16, "id")
    S_sb = em.sb([128, SQ], F32, "S")
    P_sb = em.sb([128, SQ], BF16, "P")
    ptb = em.sb([128, 2 * 512], BF16, "ptb")
    osb = em.sb([128, 64], BF16, "o")
    oT = em.sb([64, 128], BF16, "oT")
    negm = em.sb([128, 1], F32, "nm")
    lcol = em.sb([128, 1], F32, "l")
    psS = [em.ps([128, 512], F32, "ps") for _ in range(2)]
    pstg = [em.ps([128, 512], BF16, "pg") for _ in range(2)]
    psO = em.ps([128, 64], F32, "pO")
    psT2 = em.ps([64, 128], BF16, "pT2")

    em.dma(mask[:], io["mask"])
    em.dma(ident[:], io["ident"])
    em.dma(vsb4[:].rearrange("p (kb c) -> p kb c", kb=64), io["v4_src"], nc.sync)
    em.flush()

    mm_sem = em.sem("at_mm")
    cp_sem = em.sem("at_cp")
    tr_sem = em.sem("at_tr")
    c2_sem = em.sem("at_c2")
    pv_sem = em.sem("at_pv")
    v1_sem = em.sem("at_v1")
    s2_sem = em.sem("at_s2")
    t3_sem = em.sem("at_t3")
    n_mm = [0]; n_cp = [0]; n_tr = [0]; n_c2 = [0]; n_pv = [0]
    n_v1 = [0]; n_s2 = [0]; n_t3 = [0]

    for p in range(NP):
        em.dma(qsb[:].rearrange("d (r t) -> d r t", r=io.get("NR", 4)),
               io["q_src"](p), nc.sync)
        em.dma(ksb[:].rearrange("d (r t) -> d r t", r=io.get("NR", 4)),
               io["k_src"](p), nc.sync)
        em.flush()
        for qc in range(NQC):
            L = (qc + 1) * 128
            nchunks = (L + 511) // 512
            cp0, mm0 = n_cp[0], n_mm[0]
            for i in range(nchunks):
                w = min(512, L - i * 512)
                bank = psS[i % 2]
                if i >= 2:
                    nc.tensor.wait_ge(cp_sem, cp0 + i - 1)
                nc.tensor.matmul(bank[:, 0:w], qsb[:, qc * 128:(qc + 1) * 128],
                                 ksb[:, i * 512:i * 512 + w],
                                 start=True, stop=True).then_inc(mm_sem, 1)
                n_mm[0] += 1
                nc.scalar.wait_ge(mm_sem, mm0 + i + 1)
                nc.scalar.activation(S_sb[:, i * 512:i * 512 + w], bank[:, 0:w],
                                     AF.Copy, scale=0.125).then_inc(cp_sem, 1)
                n_cp[0] += 1
            nc.vector.wait_ge(cp_sem, n_cp[0])
            nc.vector.tensor_tensor(S_sb[:, L - 128:L], S_sb[:, L - 128:L],
                                    mask[:], ALU.add)
            nc.vector.reduce_max(negm[:], S_sb[:, 0:L], axis=AX.X,
                                 negate=True).then_inc(v1_sem, 1)
            n_v1[0] += 1
            nc.scalar.wait_ge(v1_sem, n_v1[0])
            nc.scalar.activation(P_sb[:, 0:L], S_sb[:, 0:L], AF.Exp,
                                 bias=negm[:, 0:1],
                                 accum_out=lcol[:, 0:1]).then_inc(s2_sem, 1)
            n_s2[0] += 1
            nc.vector.wait_ge(s2_sem, n_s2[0])
            nc.vector.reciprocal(lcol[:], lcol[:])
            nblk = L // 128
            ngrp = (nblk + 3) // 4
            tr0, c20, pv0 = n_tr[0], n_c2[0], n_pv[0]
            nc.tensor.wait_ge(s2_sem, n_s2[0])
            for g in range(ngrp):
                blo = g * 4
                bhi = min(blo + 4, nblk)
                bank = pstg[g % 2]
                if g >= 2:
                    nc.tensor.wait_ge(c2_sem, c20 + g - 1)
                for b in range(blo, bhi):
                    inst = nc.tensor.transpose(
                        bank[:, (b - blo) * 128:(b - blo + 1) * 128],
                        P_sb[:, b * 128:(b + 1) * 128], ident[:])
                    if b == bhi - 1:
                        inst.then_inc(tr_sem, 1)
                        n_tr[0] += 1
                nc.vector.wait_ge(tr_sem, tr0 + g + 1)
                if g >= 2:
                    nc.vector.wait_ge(pv_sem, pv0 + g - 1)
                nc.vector.tensor_copy(
                    ptb[:, (g % 2) * 512:(g % 2) * 512 + (bhi - blo) * 128],
                    bank[:, 0:(bhi - blo) * 128]).then_inc(c2_sem, 1)
                n_c2[0] += 1
                nc.tensor.wait_ge(c2_sem, c20 + g + 1)
                for b in range(blo, bhi):
                    inst = nc.tensor.matmul(
                        psO[:, 0:64],
                        ptb[:, (g % 2) * 512 + (b - blo) * 128:
                            (g % 2) * 512 + (b - blo + 1) * 128],
                        vsb4[:, b * 256 + p * 64:b * 256 + (p + 1) * 64],
                        start=(b == 0), stop=(b == nblk - 1),
                        skip_group_check=True)
                    if b == bhi - 1:
                        inst.then_inc(pv_sem, 1)
                        n_pv[0] += 1
            nc.vector.wait_ge(pv_sem, n_pv[0])
            nc.vector.tensor_scalar(osb[:], psO[:, 0:64], lcol[:, 0:1], None,
                                    ALU.mult).then_inc(v1_sem, 1)
            n_v1[0] += 1
            nc.tensor.wait_ge(v1_sem, n_v1[0])
            nc.tensor.transpose(psT2[:, 0:128], osb[:], ident[:]).then_inc(t3_sem, 1)
            n_t3[0] += 1
            nc.vector.wait_ge(t3_sem, n_t3[0])
            nc.vector.tensor_copy(oT[:], psT2[:, 0:128])
            em.bar()
            em.dma(io["hT_out"][p * 64:(p + 1) * 64, qc * 128:(qc + 1) * 128],
                   oT[:])
            em.flush()
    em.pop_scope()
    if standalone:
        em.close()
    return em


def build_ADDT(em=None, io=None, src_dtype=BF16, scale=1.0, name="addt"):
    standalone = em is None
    if standalone:
        em = Emit(name).start()
        x_in = em.din("x", [TOKC, D], F32)
        a_in = em.din("aT", [D, TOKC], src_dtype)
        x_out = em.dout("xo", [TOKC, D], F32)
        io = {"x_tile": lambda r: x_in[r:r + 128, :],
              "xo_tile": lambda r: x_out[r:r + 128, :],
              "aT_tile": lambda r, kd: a_in[kd * 128:(kd + 1) * 128, r:r + 128],
              "identf": em.din("identf", [128, 128], src_dtype)}
    nc = em.nc
    em.push_scope()
    ident = em.sb([128, 128], src_dtype, "id")
    xg = em.sb([128, D], F32, "xg")
    asb = em.sb([128, 8 * 128], src_dtype, "a")
    psT = [em.ps([128, 128], src_dtype, "pT") for _ in range(2)]
    atr = em.sem("ad_tr")
    acp = em.sem("ad_cp")
    em.counters.setdefault("ad_tr", 0)
    em.counters.setdefault("ad_cp", 0)
    AC = em.counters
    em.dma(ident[:], io["identf"])
    em.flush()
    for t in range(TOKC // 128):
        row = t * 128
        em.dma(xg[:], io["x_tile"](row))
        for kd in range(8):
            em.dma(asb[:, kd * 128:(kd + 1) * 128], io["aT_tile"](row, kd))
        em.flush()
        t0, c0 = AC["ad_tr"], AC["ad_cp"]
        for kd in range(8):
            if kd >= 1:
                nc.tensor.wait_ge(acp, c0 + kd)
            nc.tensor.transpose(psT[kd % 2][:, 0:128],
                                asb[:, kd * 128:(kd + 1) * 128],
                                ident[:]).then_inc(atr, 1)
            AC["ad_tr"] += 1
            nc.vector.wait_ge(atr, t0 + kd + 1)
            if scale == 1.0:
                nc.vector.tensor_tensor(xg[:, kd * 128:(kd + 1) * 128],
                                        xg[:, kd * 128:(kd + 1) * 128],
                                        psT[kd % 2][:, 0:128],
                                        ALU.add).then_inc(acp, 1)
            else:
                nc.vector.scalar_tensor_tensor(xg[:, kd * 128:(kd + 1) * 128],
                                               psT[kd % 2][:, 0:128], scale,
                                               xg[:, kd * 128:(kd + 1) * 128],
                                               ALU.mult,
                                               ALU.add).then_inc(acp, 1)
            AC["ad_cp"] += 1
        em.bar()
        em.dma(io["xo_tile"](row), xg[:])
        em.flush()
    em.pop_scope()
    if standalone:
        em.close()
    return em


def run_L0A(x, gamma, beta, w_qkv):
    xf = x.reshape(T, D)
    ident = np.eye(128, dtype=np.float32).astype(BFNP)
    identf = np.eye(128, dtype=np.float32)
    gbrow = np.concatenate([gamma, beta]).astype(np.float32)[None, :]
    wT = np.ascontiguousarray(w_qkv.T)
    em1 = _cached("qkv_aff", lambda: build_QKV(affine=True, name="qkvaff"))
    in_maps = [{"x": xf[c * TOKC:(c + 1) * TOKC], "wqkv": wT, "ident": ident,
                "gbrow": gbrow} for c in range(NCORES)]
    res = _run(em1.nc, in_maps)
    qkT = np.concatenate([res.results[c]["qkT"] for c in range(NCORES)], 1)
    v = np.concatenate([res.results[c]["v"] for c in range(NCORES)], 0)
    qkT = qkT.reshape(2048, B, S)
    v = v.reshape(B, S, 1024)

    em2 = _cached("attnc", build_ATTN_CAUSAL)
    mask = np.triu(np.full((128, 128), -1e30, np.float32), 1)
    in_maps = []
    for c in range(NCORES):
        bb, h0 = c // 4, (c % 4) * 4
        qf = np.stack([np.ascontiguousarray(qkT[(h0 + i) * 64:(h0 + i + 1) * 64, bb])
                       for i in range(4)])
        kf = np.stack([np.ascontiguousarray(
            qkT[1024 + (h0 + i) * 64:1024 + (h0 + i + 1) * 64, bb])
            for i in range(4)])
        v4 = np.ascontiguousarray(v[bb, :, h0 * 64:(h0 + 4) * 64])
        in_maps.append({"qf": qf, "kf": kf, "v4": v4, "mask": mask, "ident": ident})
    res = _run(em2.nc, in_maps)
    headsT = np.zeros((B, 1024, S), dtype=BFNP)
    for c in range(NCORES):
        bb, h0 = c // 4, (c % 4) * 4
        headsT[bb, h0 * 64:(h0 + 4) * 64, :] = res.results[c]["hT"]
    headsT = headsT.reshape(B * 1024, S)

    em3 = _cached("addt16", lambda: build_ADDT(src_dtype=BF16))
    in_maps = []
    for c in range(NCORES):
        bb = (c * TOKC) // S
        scol = slice((c * TOKC) % S, (c * TOKC) % S + TOKC)
        aT = np.ascontiguousarray(headsT[bb * 1024:(bb + 1) * 1024, scol])
        in_maps.append({"x": xf[c * TOKC:(c + 1) * TOKC], "aT": aT,
                        "identf": ident})
    res = _run(em3.nc, in_maps)
    out = np.concatenate([res.results[c]["xo"] for c in range(NCORES)], 0)
    return out.reshape(B, S, D)


def build_ATTN_WINDOW(em=None, io=None):
    standalone = em is None
    if standalone:
        em = Emit("attnw").start()
        q_in = em.din("qf", [4, 64, S], BF16)
        k_in = em.din("kf", [4, 64, S], BF16)
        v_in = em.din("v4", [S, 256], BF16)
        io = {"q_src": lambda p: q_in[p].rearrange("d (r t) -> d r t", r=4),
              "k_src": lambda p: k_in[p].rearrange("d (r t) -> d r t", r=4),
              "v4_src": v_in.rearrange("(kb p) c -> p kb c", p=128),
              "mdiag": em.din("mdiag", [128, 128], F32),
              "mband": em.din("mband", [128, 128], F32),
              "mc1": em.din("mc1", [128, 128], F32),
              "ident": em.din("ident", [128, 128], BF16),
              "hT_out": em.dout("hT", [256, S], BF16)}
    nc = em.nc
    NP = 4
    SQ = S
    NQC = SQ // 128
    em.push_scope()
    qsb = em.sb([64, SQ], BF16, "q")
    ksb = em.sb([64, SQ], BF16, "k")
    vsb4 = em.sb([128, 64 * 256], BF16, "v4")
    mdiag = em.sb([128, 128], F32, "md")
    mband = em.sb([128, 128], F32, "mb")
    mc1 = em.sb([128, 128], F32, "mc")
    ident = em.sb([128, 128], BF16, "id")
    S_sb = em.sb([128, 320], F32, "S")
    P_sb = em.sb([128, 320], BF16, "P")
    ptb = em.sb([128, 448], BF16, "ptb")
    osb = em.sb([128, 64], BF16, "o")
    oT = em.sb([64, 128], BF16, "oT")
    negm = em.sb([128, 1], F32, "nm")
    lcol = em.sb([128, 1], F32, "l")
    psS = em.ps([128, 512], F32, "ps")
    pstg = em.ps([128, 512], BF16, "pg")
    psO = em.ps([128, 64], F32, "pO")
    psT2 = em.ps([64, 128], BF16, "pT2")
    wt_sem = em.sem("aw_t")
    wv_sem = em.sem("aw_v")
    ws_sem = em.sem("aw_s")
    nt_ = [0]; nv_ = [0]; ns_ = [0]

    em.dma(mdiag[:], io["mdiag"])
    em.dma(mband[:], io["mband"])
    em.dma(mc1[:], io["mc1"])
    em.dma(ident[:], io["ident"])
    em.dma(vsb4[:].rearrange("p (kb c) -> p kb c", kb=64), io["v4_src"], nc.sync)
    em.flush()

    for p in range(NP):
        em.dma(qsb[:].rearrange("d (r t) -> d r t", r=io.get("NR", 4)),
               io["q_src"](p), nc.sync)
        em.dma(ksb[:].rearrange("d (r t) -> d r t", r=io.get("NR", 4)),
               io["k_src"](p), nc.sync)
        em.flush()
        for qc in range(NQC):
            if qc == 0:
                parts = [(0, 128, 0, mdiag)]
            elif qc == 1:
                parts = [(0, 128, 0, mc1), (128, 128, 1, mdiag)]
            else:
                parts = [(0, 64, -1, None),
                         ((qc - 1) * 128, 128, qc - 1, mband),
                         (qc * 128, 128, qc, mdiag)]
            W = sum(pp[1] for pp in parts)
            off = 0
            if nv_[0] > 0:
                nc.tensor.wait_ge(wv_sem, nv_[0])
            for (koff, kw, vb, msk) in parts:
                inst = nc.tensor.matmul(psS[:, off:off + kw],
                                        qsb[:, qc * 128:(qc + 1) * 128],
                                        ksb[:, koff:koff + kw],
                                        start=(off == 0), stop=(off + kw == W))
                off += kw
            inst.then_inc(wt_sem, 1)
            nt_[0] += 1
            nc.scalar.wait_ge(wt_sem, nt_[0])
            nc.scalar.activation(S_sb[:, 0:W], psS[:, 0:W], AF.Copy,
                                 scale=0.125).then_inc(ws_sem, 1)
            ns_[0] += 1
            nc.vector.wait_ge(ws_sem, ns_[0])
            off = 0
            for (koff, kw, vb, msk) in parts:
                if msk is not None:
                    nc.vector.tensor_tensor(S_sb[:, off:off + kw],
                                            S_sb[:, off:off + kw], msk[:], ALU.add)
                off += kw
            nc.vector.reduce_max(negm[:], S_sb[:, 0:W], axis=AX.X,
                                 negate=True).then_inc(wv_sem, 1)
            nv_[0] += 1
            nc.scalar.wait_ge(wv_sem, nv_[0])
            nc.scalar.activation(P_sb[:, 0:W], S_sb[:, 0:W], AF.Exp,
                                 bias=negm[:, 0:1],
                                 accum_out=lcol[:, 0:1]).then_inc(ws_sem, 1)
            ns_[0] += 1
            nc.vector.wait_ge(ws_sem, ns_[0])
            nc.vector.reciprocal(lcol[:], lcol[:])
            off = 0
            nc.tensor.wait_ge(ws_sem, ns_[0])
            for (koff, kw, vb, msk) in parts:
                if kw == 128:
                    inst = nc.tensor.transpose(pstg[:, off:off + kw],
                                               P_sb[:, off:off + kw], ident[:])
                else:
                    inst = nc.tensor.transpose(pstg[0:kw, 384:512],
                                               P_sb[:, off:off + kw], ident[:])
                off += kw
            inst.then_inc(wt_sem, 1)
            nt_[0] += 1
            nc.vector.wait_ge(wt_sem, nt_[0])
            off = 0
            for (koff, kw, vb, msk) in parts:
                if kw == 128:
                    inst = nc.vector.tensor_copy(ptb[:, off:off + 128],
                                                 pstg[:, off:off + 128])
                else:
                    inst = nc.vector.tensor_copy(ptb[0:kw, 320:448],
                                                 pstg[0:kw, 384:512])
                off += kw
            inst.then_inc(wv_sem, 1)
            nv_[0] += 1
            nc.tensor.wait_ge(wv_sem, nv_[0])
            off = 0
            first = True
            for (koff, kw, vb, msk) in parts:
                if kw == 128:
                    inst = nc.tensor.matmul(
                        psO[:, 0:64], ptb[:, off:off + 128],
                        vsb4[:, vb * 256 + p * 64:vb * 256 + (p + 1) * 64],
                        start=first, stop=(off + kw == W),
                        skip_group_check=True)
                else:
                    inst = nc.tensor.matmul(
                        psO[:, 0:64], ptb[0:64, 320:448],
                        vsb4[0:64, p * 64:p * 64 + 64],
                        start=first, stop=(off + kw == W),
                        skip_group_check=True)
                first = False
                off += kw
            inst.then_inc(wt_sem, 1)
            nt_[0] += 1
            nc.vector.wait_ge(wt_sem, nt_[0])
            nc.vector.tensor_scalar(osb[:], psO[:, 0:64], lcol[:, 0:1], None,
                                    ALU.mult).then_inc(wv_sem, 1)
            nv_[0] += 1
            nc.tensor.wait_ge(wv_sem, nv_[0])
            nc.tensor.transpose(psT2[:, 0:128], osb[:], ident[:]).then_inc(wt_sem, 1)
            nt_[0] += 1
            nc.vector.wait_ge(wt_sem, nt_[0])
            nc.vector.wait_ge(em.dsem, em.dcount)
            nc.vector.tensor_copy(oT[:], psT2[:, 0:128]).then_inc(wv_sem, 1)
            nv_[0] += 1
            nc.sync.wait_ge(wv_sem, nv_[0])
            em.dma(io["hT_out"][p * 64:(p + 1) * 64, qc * 128:(qc + 1) * 128],
                   oT[:])
    em.pop_scope()
    if standalone:
        em.close()
    return em



from concourse.bass import IndirectOffsetOnAxis

NGIDX = 384


def _fused_add_gather_support(em):
    em.gidx_in = em.din("gidx", [NGIDX, 128], I32)
    em.gather_specs = []


def _gather(em, dst_ap, src_ap2d, idx_fn, width, dtype=F32):
    slot = len(em.gather_specs)
    assert slot < NGIDX
    em.gather_specs.append(idx_fn)
    nc = em.nc
    em.push_scope()
    idx_sb = em.sb([128, 1], I32, "gidx")
    stage = em.sb([128, width], dtype, "gst")
    em.dma(idx_sb[:], em.gidx_in[slot:slot + 1, :].rearrange("o p -> p o"),
           nc.gpsimd)
    em.flush(nc.gpsimd)
    nc.gpsimd.indirect_dma_start(
        out=stage[:], out_offset=None, in_=src_ap2d,
        in_offset=IndirectOffsetOnAxis(ap=idx_sb[:], axis=0),
    ).then_inc(em.dsem, 16)
    em.dcount += 16
    em.flush(nc.gpsimd)
    em.dma(dst_ap, stage[:], nc.gpsimd)
    em.flush(nc.gpsimd)
    em.pop_scope()


def build_FUSED(upto=99):
    em = Emit("fused").start()
    nc = em.nc
    F3 = 3072
    x_in = em.din("x", [TOKC, D], F32)
    x_out = em.dout("xo", [TOKC, D], F32)
    ident_in = em.din("ident", [128, 128], BF16)
    identf_in = em.din("identf", [128, 128], F32)
    mask_in = em.din("mask", [128, 128], F32)
    mdiag_in = em.din("mdiag", [128, 128], F32)
    mband_in = em.din("mband", [128, 128], F32)
    mc1_in = em.din("mc1", [128, 128], F32)
    W = {
        "l0f_mixw": em.din("l0f_mixw", [D, D], F32),
        "l0f_router": em.din("l0f_router", [D, 8], F32),
        "l0f_w1": em.din("l0f_w1", [8, D, 4096], F32),
        "l0f_w2": em.din("l0f_w2", [8, 4096, D], F32),
        "l0f_rows": em.din("l0f_rows", [1, 5 * D], F32),
        "l0f_b1c": em.din("l0f_b1c", [128, 8 * 32], F32),
        "l0f_b2r": em.din("l0f_b2r", [1, 8 * D], F32),
        "l0a_wqkv": em.din("l0a_wqkv", [D, F3], F32),
        "l0a_gbrow": em.din("l0a_gbrow", [1, 2 * D], F32),
        "l1f_lnrow": em.din("l1f_lnrow", [1, 2 * D], F32),
        "l1f_muw": em.din("l1f_muw", [D, 256], F32),
        "l1f_scl": em.din("l1f_scl", [1, 1], F32),
        "l1fs_w1": em.din("l1fs_w1", [2, D, 256], F32),
        "l1fs_w2": em.din("l1fs_w2", [2, 256, D], F32),
        "l1fs_brow": em.din("l1fs_brow", [1, 2 * 256 + 2 * D], F32),
        "l1a_cw": em.din("l1a_cw", [128, 4], F32),
        "l1a_cb": em.din("l1a_cb", [128, 1], F32),
        "l1a_xpw": em.din("l1a_xpw", [D, 33], F32),
        "l1a_dtwr": em.din("l1a_dtwr", [1, 128], F32),
        "l1a_dtbc": em.din("l1a_dtbc", [128, 1], F32),
        "l1a_alog": em.din("l1a_alog", [128, 16], F32),
        "l1a_dpc": em.din("l1a_dpc", [128, 1], F32),
        "l2f_gatew": em.din("l2f_gatew", [D, 8], F32),
        "l2f_w1": em.din("l2f_w1", [9, D, 4096], F32),
        "l2f_w2": em.din("l2f_w2", [9, 4096, D], F32),
        "l2a_win": em.din("l2a_win", [D, F3], F32),
        "l2a_binr": em.din("l2a_binr", [1, F3], F32),
        "l2a_wout": em.din("l2a_wout", [D, D], F32),
        "l2a_boutr": em.din("l2a_boutr", [1, D], F32),
        "l3a_wqkv": em.din("l3a_wqkv", [D, F3], F32),
        "l3a_wout": em.din("l3a_wout", [D, D], F32),
        "l3f_wg": em.din("l3f_wg", [D, 3072], I32),
        "l3f_wu": em.din("l3f_wu", [D, 3072], I32),
        "l3f_wd": em.din("l3f_wd", [3072, D], I32),
    }
    _fused_add_gather_support(em)

    xs = [em.dram(f"xs{i}", [TOKC, D]) for i in range(12)]
    qkT_loc = em.dram("qkT_loc", [2048, TOKC], BF16)
    v_loc = em.dram("v_loc", [TOKC, 1024], BF16)
    qkT_full = em.dram("qkT_full", [8 * 2048, TOKC], BF16, shared=True)
    v_full = em.dram("v_full", [T, 1024], BF16, shared=True)
    hT_loc = em.dram("hT_loc", [256, S], BF16)
    hT_full = em.dram("hT_full", [2048, S], BF16, shared=True)
    qf_loc = em.dram("qf_loc", [1024, 2048], BF16)
    kf_loc = em.dram("kf_loc", [1024, 2048], BF16)
    v4_loc = em.dram("v4_loc", [S, 256], BF16)
    aT_loc = em.dram("aT_loc", [1024, TOKC], BF16)
    xn_loc = em.dram("xn_loc", [TOKC, D], F32)
    xn_full = em.dram("xn_full", [T, D], F32, shared=True)
    lgT_loc = em.dram("lgT_loc", [256, TOKC], F32)
    lgT_full = em.dram("lgT_full", [2048, TOKC], F32, shared=True)
    cbT_loc = em.dram("cbT_loc", [256, TOKC], F32)
    lgs_loc = em.dram("lgs_loc", [256, 2048], F32)
    so_loc = em.dram("so_loc", [64, D], F32)
    so_full = em.dram("so_full", [512, D], F32, shared=True)
    sob_loc = em.dram("sob_loc", [256, D], F32)
    xnT1_loc = em.dram("xnT1_loc", [D, TOKC], F32)
    xnT1_full = em.dram("xnT1_full", [8 * D, TOKC], F32, shared=True)
    xnf_loc = em.dram("xnf_loc", [1024, 2048], F32)
    xc_loc = em.dram("xc_loc", [128, B * S], F32)
    xc_full = em.dram("xc_full", [1024, B * S], F32, shared=True)
    xcT_loc = em.dram("xcT_loc", [1024, 2048], F32)
    projT_loc = em.dram("projT_loc", [33, TOKC], F32)
    projT_full = em.dram("projT_full", [264, TOKC], F32, shared=True)
    scanT_loc = em.dram("scanT_loc", [128, B * S], F32)
    scan_full = em.dram("scan_full", [1024, B * S], F32, shared=True)
    scanA_loc = em.dram("scanA_loc", [1024, 2048], F32)
    xg_full = em.dram("xg_full", [T, D], F32, shared=True)
    x2a_loc = em.dram("x2a_loc", [2048, D], F32)
    xp_loc = em.dram("xp_loc", [2048, D], F32)
    xp_full = em.dram("xp_full", [T, D], F32, shared=True)

    def xtile(buf):
        return lambda r: buf[r:r + 128, :]

    state = {"cur": None, "done": False, "ph": -1}

    def phase(label):
        state["ph"] += 1
        active = state["ph"] <= upto and not state["done"]
        if active and state["ph"] > 0:
            em.dma_wait()
            em.bar()
            em.rotate_dsem()
        return active

    cur = x_in
    nxt = iter(xs)

    def step_ffn(build_fn, wio, cur):
        buf = next(nxt)
        io = {"x_tile": xtile(cur), "xo_tile": xtile(buf), **wio}
        build_fn(em=em, io=io)
        em.bar()
        return buf

    def do_attention(wqkv_key, affine, window, cur):
        io = {"x_tile": xtile(cur), "wqkv": W[wqkv_key], "ident": ident_in,
              "qkT_out": qkT_loc, "v_out": v_loc}
        if affine:
            io["gbrow"] = W["l0a_gbrow"]
        build_QKV(em=em, io=io, affine=affine)
        em.allgather(qkT_loc, qkT_full)
        em.allgather(v_loc, v_full)
        for g in range(8):
            def qidx(cid, g=g, koff=0):
                out = np.empty(128, np.int32)
                bb, h0 = cid // 4, (cid % 4) * 4
                for j in range(128):
                    i = g * 128 + j
                    p, rem = i // 256, i % 256
                    r, dd = rem // 64, rem % 64
                    out[j] = (bb * 4 + r) * 2048 + koff + (h0 + p) * 64 + dd
                return out
            _gather(em, qf_loc[g * 128:(g + 1) * 128, :], qkT_full, qidx, 2048, BF16)
            def kidx(cid, g=g):
                return qidx(cid, g, 1024)
            _gather(em, kf_loc[g * 128:(g + 1) * 128, :], qkT_full, kidx, 2048, BF16)
        vview = v_full.rearrange("t (hq c) -> (t hq) c", hq=4)
        for kb in range(64):
            def vidx(cid, kb=kb):
                bb, h0 = cid // 4, (cid % 4) * 4
                t = bb * 8192 + kb * 128 + np.arange(128)
                return (t * 4 + h0 // 4).astype(np.int32)
            _gather(em, v4_loc[kb * 128:(kb + 1) * 128, :], vview, vidx, 256, BF16)
        aio = {"q_src": lambda p: qf_loc.rearrange("(pp r dd) t -> pp dd r t",
                                                   pp=4, r=4)[p],
               "k_src": lambda p: kf_loc.rearrange("(pp r dd) t -> pp dd r t",
                                                   pp=4, r=4)[p],
               "v4_src": v4_loc.rearrange("(kb p) c -> p kb c", p=128),
               "ident": ident_in, "hT_out": hT_loc, "NR": 4}
        if window:
            aio.update({"mdiag": mdiag_in, "mband": mband_in, "mc1": mc1_in})
            build_ATTN_WINDOW(em=em, io=aio)
        else:
            aio["mask"] = mask_in
            build_ATTN_CAUSAL(em=em, io=aio)
        em.allgather(hT_loc, hT_full)
        hview = hT_full.rearrange("f (sb t) -> (f sb) t", sb=4)
        for kd in range(8):
            def hidx(cid, kd=kd):
                bb, sb = cid // 4, cid % 4
                f = bb * 1024 + kd * 128 + np.arange(128)
                return (f * 4 + sb).astype(np.int32)
            _gather(em, aT_loc[kd * 128:(kd + 1) * 128, :], hview, hidx, 2048, BF16)
        em.bar()

    if phase("l0f1"):
        cur = step_ffn(build_L0F, {"mixw": W["l0f_mixw"], "router": W["l0f_router"],
                                   "w1": W["l0f_w1"], "w2": W["l0f_w2"],
                                   "rows": W["l0f_rows"], "b1c": W["l0f_b1c"],
                                   "b2r": W["l0f_b2r"], "ident": ident_in}, cur)
    if phase("l0a"):
        do_attention("l0a_wqkv", True, False, cur)
        buf = next(nxt)
        build_ADDT(em=em, io={"x_tile": xtile(cur), "xo_tile": xtile(buf),
                              "aT_tile": lambda r, kd: aT_loc[kd * 128:(kd + 1) * 128,
                                                              r:r + 128],
                              "identf": ident_in}, src_dtype=BF16)
        em.bar()
        cur = buf
    if phase("l0f2"):
        cur = step_ffn(build_L0F, {"mixw": W["l0f_mixw"], "router": W["l0f_router"],
                                   "w1": W["l0f_w1"], "w2": W["l0f_w2"],
                                   "rows": W["l0f_rows"], "b1c": W["l0f_b1c"],
                                   "b2r": W["l0f_b2r"], "ident": ident_in}, cur)

    def do_l1f(cur):
        build_L1F_pre(em=em, io={
            "x_tile": xtile(cur), "lnrow": W["l1f_lnrow"], "muw": W["l1f_muw"],
            "scl": W["l1f_scl"], "ident": ident_in, "identf": identf_in,
            "xn_out": lambda sl: xn_loc[sl], "lgT_out": lambda sl: lgT_loc[sl],
            "cbT_out": lambda sl: cbT_loc[sl]})
        em.allgather(xn_loc, xn_full)
        em.allgather(lgT_loc, lgT_full)
        for g in range(2):
            def lidx(cid, g=g):
                out = np.empty(128, np.int32)
                for j in range(128):
                    i = g * 128 + j
                    b2, rem = i // 128, i % 128
                    eL, rem2 = rem // 64, rem % 64
                    s, rr = rem2 // 4, rem2 % 4
                    out[j] = (b2 * 4 + rr) * 256 + (2 * cid + eL) * 16 + s
                return out
            _gather(em, lgs_loc[g * 128:(g + 1) * 128, :], lgT_full, lidx, 2048, F32)
        build_L1F_slots(em=em, io={
            "lgs_src": lgs_loc.rearrange("(bes rr) t -> bes rr t", rr=4),
            "lgs_rearr": "p (r t) -> p r t", "lgs_kw": {"r": 4},
            "xn_tile": lambda b, r: xn_full[b * S + r:b * S + r + 128, :],
            "w1": W["l1fs_w1"], "w2": W["l1fs_w2"], "brow": W["l1fs_brow"],
            "ident": ident_in, "so_out": so_loc})
        em.allgather(so_loc, so_full)
        for g in range(2):
            def sidx(cid, g=g):
                bb = cid // 4
                out = np.empty(128, np.int32)
                for j in range(128):
                    es = g * 128 + j
                    e, s = es // 16, es % 16
                    out[j] = (e // 2) * 64 + bb * 32 + (e % 2) * 16 + s
                return out
            _gather(em, sob_loc[g * 128:(g + 1) * 128, :], so_full, sidx, 1024, F32)
        buf = next(nxt)
        build_L1F_out(em=em, io={
            "cT_src": lambda c: cbT_loc[c * 128:(c + 1) * 128, :],
            "so_src": lambda c: sob_loc[c * 128:(c + 1) * 128, :],
            "x_tile": xtile(cur), "xo_tile": xtile(buf)})
        em.bar()
        return buf

    if phase("l1f1"):
        cur = do_l1f(cur)
    if phase("l1a"):
        build_L1A_ln(em=em, io={
            "x_tile": xtile(cur), "identf": identf_in,
            "xnT_out": lambda kd, r: xnT1_loc[kd * 128:(kd + 1) * 128, r:r + 128]})
        em.allgather(xnT1_loc, xnT1_full)
        for rr in range(8):
            def xnidx(cid, rr=rr):
                return (rr * 1024 + cid * 128 + np.arange(128)).astype(np.int32)
            _gather(em, xnf_loc[rr * 128:(rr + 1) * 128, :], xnT1_full, xnidx, 2048, F32)
        xnf_view = xnf_loc.rearrange("(rr dd) t -> dd rr t", dd=128)
        build_L1A_conv(em=em, io={
            "xn_src": lambda b: xnf_view[:, b * 4:(b + 1) * 4, :], "NRC": 4,
            "cw": W["l1a_cw"], "cbr": W["l1a_cb"], "xc_out": xc_loc})
        em.allgather(xc_loc, xc_full)
        xcview = xc_full.rearrange("dd (cb t) -> (dd cb) t", cb=8)
        for kd in range(8):
            def xcidx(cid, kd=kd):
                return ((kd * 128 + np.arange(128)) * 8 + cid).astype(np.int32)
            _gather(em, xcT_loc[kd * 128:(kd + 1) * 128, :], xcview, xcidx, 2048, F32)
        build_L1A_proj(em=em, io={
            "xc_src": lambda kd: xcT_loc[kd * 128:(kd + 1) * 128, :],
            "xpw": W["l1a_xpw"], "identf": identf_in,
            "pjT_out": lambda r: projT_loc[:, r:r + 128]})
        em.allgather(projT_loc, projT_full)
        pview = projT_full.rearrange("(r n) t -> n r t", n=33)
        xnfv = xnf_loc.rearrange("(rr dd) t -> dd rr t", dd=128)
        build_L1A_scan(em=em, io={
            "xc": xc_loc,
            "xn_tile_fm": lambda b, ch: xnfv[:, (b * S + ch * 512) // 2048,
                                             (b * S + ch * 512) % 2048:
                                             (b * S + ch * 512) % 2048 + 512],
            "prow_src": lambda b: pview[32:33, b * 4:(b + 1) * 4, :],
            "brow_src": lambda b, n: pview[n:n + 1, b * 4:(b + 1) * 4, :],
            "crow_src": lambda b, n: pview[16 + n:17 + n, b * 4:(b + 1) * 4, :],
            "NRR": 4, "dtwr": W["l1a_dtwr"], "dtbc": W["l1a_dtbc"],
            "alog": W["l1a_alog"], "dpc": W["l1a_dpc"], "scanT_out": scanT_loc})
        em.allgather(scanT_loc, scan_full)
        sview = scan_full.rearrange("d (cb t) -> (d cb) t", cb=8)
        for kd in range(8):
            def scidx(cid, kd=kd):
                return ((kd * 128 + np.arange(128)) * 8 + cid).astype(np.int32)
            _gather(em, scanA_loc[kd * 128:(kd + 1) * 128, :], sview, scidx, 2048, F32)
        buf = next(nxt)
        build_ADDT(em=em, io={"x_tile": xtile(cur), "xo_tile": xtile(buf),
                              "aT_tile": lambda r, kd: scanA_loc[kd * 128:(kd + 1) * 128,
                                                                 r:r + 128],
                              "identf": identf_in}, src_dtype=F32)
        em.bar()
        cur = buf
    if phase("l1f2"):
        cur = do_l1f(cur)

    if phase("l2f1"):
        cur = step_ffn(build_L2F, {"gatew": W["l2f_gatew"], "w1": W["l2f_w1"],
                                   "w2": W["l2f_w2"], "ident": ident_in}, cur)
    if phase("l2a"):
        em.allgather(cur, xg_full)
        for g in range(16):
            def x2idx(cid, g=g):
                i = g * 128 + np.arange(128)
                b2 = i // 1024
                return (b2 * 8192 + cid * 1024 + (i % 1024)).astype(np.int32)
            _gather(em, x2a_loc[g * 128:(g + 1) * 128, :], xg_full, x2idx, 1024, F32)
        build_L2A(em=em, io={
            "x_tile": lambda b, r: x2a_loc[b * 1024 + r:b * 1024 + r + 128, :],
            "xo_tile": lambda b, r: xp_loc[b * 1024 + r:b * 1024 + r + 128, :],
            "win": W["l2a_win"], "binr": W["l2a_binr"], "wout": W["l2a_wout"],
            "boutr": W["l2a_boutr"], "ident": ident_in})
        em.allgather(xp_loc, xp_full)
        buf = next(nxt)
        for g in range(16):
            def xpidx(cid, g=g):
                i = g * 128 + np.arange(128)
                t = cid * TOKC + i
                b2, s = t // S, t % S
                return ((s // 1024) * 2048 + b2 * 1024 + (s % 1024)).astype(np.int32)
            _gather(em, buf[g * 128:(g + 1) * 128, :], xp_full, xpidx, 1024, F32)
        em.bar()
        cur = buf
    if phase("l2f2"):
        cur = step_ffn(build_L2F, {"gatew": W["l2f_gatew"], "w1": W["l2f_w1"],
                                   "w2": W["l2f_w2"], "ident": ident_in}, cur)

    if phase("l3f1"):
        cur = step_ffn(build_L3F, {"wg": W["l3f_wg"], "wu": W["l3f_wu"],
                                   "wd": W["l3f_wd"], "ident": ident_in}, cur)
    if phase("l3a"):
        do_attention("l3a_wqkv", False, True, cur)
        buf = next(nxt)
        build_PROJRES(1.0, em=em, io={
            "a_src": lambda kd: aT_loc[kd * 128:(kd + 1) * 128, :],
            "w": W["l3a_wout"], "x_tile": xtile(cur), "xo_tile": xtile(buf)})
        em.bar()
        cur = buf
    if phase("l3f2"):
        io = {"x_tile": xtile(cur),
              "xo_tile": lambda r: x_out[r:r + 128, :],
              "wg": W["l3f_wg"], "wu": W["l3f_wu"], "wd": W["l3f_wd"],
              "ident": ident_in}
        build_L3F(em=em, io=io)
        em.bar()
        state["done"] = True

    if not state["done"]:
        em.dma(x_out, cur)
        em.flush()
    em.close()
    return em


def _fused_host_inputs(inp, em):
    x = np.asarray(inp["L0_attn_x"], np.float32).reshape(T, D)
    ident = np.eye(128, dtype=np.float32).astype(BFNP)
    identf = np.eye(128, dtype=np.float32)
    mask = np.triu(np.full((128, 128), -1e30, np.float32), 1)
    i2 = np.arange(128)[:, None]
    j2 = np.arange(128)[None, :]
    mdiag = np.where(j2 <= i2, 0.0, -1e30).astype(np.float32)
    mband = np.where(j2 >= i2, 0.0, -1e30).astype(np.float32)
    mc1 = np.where((j2 >= i2) | (j2 < 64), 0.0, -1e30).astype(np.float32)
    H = 16
    perm = np.concatenate([
        np.concatenate([np.arange(h * 192 + c * 64, h * 192 + c * 64 + 64)
                        for h in range(H)]) for c in range(3)])
    common = {
        "ident": ident, "identf": identf, "mask": mask, "mdiag": mdiag,
        "mband": mband, "mc1": mc1,
        "l0f_mixw": inp["L0_ffn_mix_w"],
        "l0f_router": inp["L0_ffn_router_w"],
        "l0f_w1": inp["L0_ffn_w1"], "l0f_w2": inp["L0_ffn_w2"],
        "l0f_rows": np.concatenate([inp["L0_ffn_g1"], inp["L0_ffn_be1"],
                                    inp["L0_ffn_g2"], inp["L0_ffn_be2"],
                                    inp["L0_ffn_mix_b"]]).astype(np.float32)[None, :],
        "l0f_b1c": np.ascontiguousarray(
            np.asarray(inp["L0_ffn_b1"]).reshape(8, 32, 128)
            .transpose(2, 0, 1).reshape(128, 256)),
        "l0f_b2r": np.ascontiguousarray(np.asarray(inp["L0_ffn_b2"]).reshape(1, -1)),
        "l0a_wqkv": np.ascontiguousarray(np.asarray(inp["L0_attn_w_qkv"]).T),
        "l0a_gbrow": np.concatenate([inp["L0_attn_gamma"],
                                     inp["L0_attn_beta"]]).astype(np.float32)[None, :],
        "l1f_lnrow": np.concatenate([inp["L1_ffn_ln_gamma"],
                                     inp["L1_ffn_ln_beta"]]).astype(np.float32)[None, :],
        "l1f_muw": np.ascontiguousarray(np.asarray(inp["L1_ffn_mu"]).reshape(D, 256)),
        "l1f_scl": np.asarray(inp["L1_ffn_scale"], np.float32).reshape(1, 1),
        "l1a_xpw": np.ascontiguousarray(np.asarray(inp["L1_attn_xproj_w"]).T),
        "l2f_gatew": np.ascontiguousarray(np.asarray(inp["L2_ffn_gate_w"]).T),
        "l2f_w1": np.ascontiguousarray(np.concatenate(
            [np.asarray(inp["L2_ffn_experts_w1"]).transpose(0, 2, 1),
             np.asarray(inp["L2_ffn_shared_w1"]).T[None]], 0)),
        "l2f_w2": np.ascontiguousarray(np.concatenate(
            [np.asarray(inp["L2_ffn_experts_w2"]).transpose(0, 2, 1),
             np.asarray(inp["L2_ffn_shared_w2"]).T[None]], 0)),
        "l2a_win": np.ascontiguousarray(
            np.asarray(inp["L2_attn_in_proj_weight"])[perm, :].T),
        "l2a_binr": np.ascontiguousarray(
            np.asarray(inp["L2_attn_in_proj_bias"])[perm])[None, :],
        "l2a_wout": np.ascontiguousarray(np.asarray(inp["L2_attn_out_proj_weight"]).T),
        "l2a_boutr": np.ascontiguousarray(
            np.asarray(inp["L2_attn_out_proj_bias"]))[None, :],
        "l3a_wqkv": np.asarray(inp["L3_attn_w_qkv"]),
        "l3a_wout": np.asarray(inp["L3_attn_w_out"]),
        "l3f_wg": np.ascontiguousarray(np.asarray(inp["L3_ffn_w_gate"]).T),
        "l3f_wu": np.ascontiguousarray(np.asarray(inp["L3_ffn_w_up"]).T),
        "l3f_wd": np.ascontiguousarray(np.asarray(inp["L3_ffn_w_down"]).T),
    }
    cw = np.asarray(inp["L1_attn_conv_w"]).reshape(D, 4)
    w1s = np.asarray(inp["L1_ffn_w1"])
    w2s = np.asarray(inp["L1_ffn_w2"])
    b1s = np.asarray(inp["L1_ffn_b1"])
    b2s = np.asarray(inp["L1_ffn_b2"])
    dt_w = np.asarray(inp["L1_attn_dt_w"])
    dt_b = np.asarray(inp["L1_attn_dt_b"])
    a_log = np.asarray(inp["L1_attn_a_log"])
    Dp = np.asarray(inp["L1_attn_Dp"])
    in_maps = []
    for c in range(NCORES):
        dsl = slice(c * 128, (c + 1) * 128)
        eids = [2 * c, 2 * c + 1]
        gidx = np.zeros((NGIDX, 128), np.int32)
        for i, fn in enumerate(em.gather_specs):
            gidx[i] = fn(c)
        in_maps.append({
            "x": x[c * TOKC:(c + 1) * TOKC], "gidx": gidx,
            "l1fs_w1": w1s[eids], "l1fs_w2": w2s[eids],
            "l1fs_brow": np.concatenate(
                [b1s[eids].reshape(-1), b2s[eids].reshape(-1)]
            ).astype(np.float32)[None, :],
            "l1a_cw": cw[dsl], "l1a_cb": np.asarray(inp["L1_attn_conv_b"])[dsl, None],
            "l1a_dtwr": np.ascontiguousarray(dt_w[dsl, 0])[None, :],
            "l1a_dtbc": dt_b[dsl, None], "l1a_alog": a_log[dsl],
            "l1a_dpc": Dp[dsl, None],
            **common})
    return in_maps


def kernel_fused(upto=99, **inp):
    key = f"fused_{upto}"
    em = _cached(key, lambda: build_FUSED(upto))
    in_maps = _fused_host_inputs(inp, em)
    res = _run(em.nc, in_maps)
    out = np.concatenate([res.results[c]["xo"] for c in range(NCORES)], 0)
    return out.reshape(B, S, D)
